# version 37
# baseline (speedup 1.0000x reference)
import numpy as np
import ml_dtypes
import concourse.bass as bass
import concourse.mybir as mybir
from concourse.bass_utils import run_bass_kernel_spmd
from contextlib import ExitStack

F32 = mybir.dt.float32
BF16 = mybir.dt.bfloat16
ALU = mybir.AluOpType
AF = mybir.ActivationFunctionType
AX = mybir.AxisListType.X

N = 2048
D = 1024
H = 16
NT = N // 128
DC = D // 128
BIG_NEG = -1.0e30
EPS = 1.0e-5

ENGINES = ("sync", "tensor", "vector", "scalar", "gpsimd")

_DBG = ()


class _Sem:
    __slots__ = ("h", "n")

    def __init__(self, h):
        self.h = h
        self.n = 0


class _Prog:
    def __init__(self):
        self.stream = {e: [] for e in ENGINES}

    def op(self, eng, fn, waits=(), inc=None, amt=1):
        ws = tuple(w for w in waits if w is not None)
        if inc is not None:
            inc.n += amt
        self.stream[eng].append((ws, fn, inc, amt, inc.n if inc else 0))
        return (inc, inc.n) if inc is not None else None

    def emit(self, eng_name, eng, self_sem=None):
        waited = {}
        prev_cnt = None
        for ws, fn, inc, amt, post in self.stream[eng_name]:
            if self_sem is not None and prev_cnt is not None:
                k = id(self_sem)
                if waited.get(k, -1) < prev_cnt:
                    eng.wait_ge(self_sem.h, prev_cnt)
                    waited[k] = prev_cnt
            for sem, val in ws:
                k = id(sem)
                if waited.get(k, -1) < val:
                    eng.wait_ge(sem.h, val)
                    waited[k] = val
            inst = fn(eng)
            if inc is not None:
                inst.then_inc(inc.h, amt)
                if inc is self_sem:
                    prev_cnt = post


def _interp_matrix_T():
    s = 32
    pos = np.arange(N, dtype=np.float32) * ((s - 1) / (N - 1))
    i0 = np.floor(pos).astype(np.int32)
    i1 = np.minimum(i0 + 1, s - 1)
    frac = (pos - i0.astype(np.float32)).astype(np.float32)
    M = np.zeros((N, s), dtype=np.float32)
    rows = np.arange(N)
    np.add.at(M, (rows, i0), 1.0 - frac)
    np.add.at(M, (rows, i1), frac)
    M[N - 1, :] = 0.0
    return np.ascontiguousarray(M.T).astype(ml_dtypes.bfloat16)


def build_nc():
    nc = bass.Bass("TRN2", target_bir_lowering=False)

    dx = nc.dram_tensor("x", [N, D], F32, kind="ExternalInput").ap()
    dwq = nc.dram_tensor("wq", [D, D], F32, kind="ExternalInput").ap()
    dwk = nc.dram_tensor("wk", [D, D], F32, kind="ExternalInput").ap()
    dwv = nc.dram_tensor("wv", [D, D], F32, kind="ExternalInput").ap()
    dwo = nc.dram_tensor("wo", [D, D], F32, kind="ExternalInput").ap()
    dfw = nc.dram_tensor("fw", [D, D], F32, kind="ExternalInput").ap()
    dbq = nc.dram_tensor("bq", [D], F32, kind="ExternalInput").ap()
    dbk = nc.dram_tensor("bk", [D], F32, kind="ExternalInput").ap()
    dbv = nc.dram_tensor("bv", [D], F32, kind="ExternalInput").ap()
    dbo = nc.dram_tensor("bo", [1, D], F32, kind="ExternalInput").ap()
    dfb = nc.dram_tensor("fb", [1, D], F32, kind="ExternalInput").ap()
    dga = nc.dram_tensor("gamma", [D], F32, kind="ExternalInput").ap()
    dbe = nc.dram_tensor("beta", [D], F32, kind="ExternalInput").ap()
    dal = nc.dram_tensor("alpha", [1, 1], F32, kind="ExternalInput").ap()
    drel = nc.dram_tensor("relT", [H, 32, 32], F32, kind="ExternalInput").ap()
    dmqt = nc.dram_tensor("d_mqt", [32, N], BF16, kind="ExternalInput").ap()
    dident = nc.dram_tensor("ident_bf", [128, 128], BF16, kind="ExternalInput").ap()
    didf = nc.dram_tensor("ident_f32", [128, 128], F32, kind="ExternalInput").ap()
    dmtm = nc.dram_tensor("d_mask_tm", [128, 128], BF16, kind="ExternalInput").ap()
    dmkb = nc.dram_tensor("d_mask_kb", [128, 512], BF16, kind="ExternalInput").ap()
    dout = nc.dram_tensor("out", [N, D], F32, kind="ExternalOutput").ap()

    ctx = ExitStack()
    e = ctx.enter_context

    xutA = e(nc.sbuf_tensor("xutA", [128, DC * N], BF16))
    xut = [xutA.ap()[:, c * N:(c + 1) * N] for c in range(DC)]
    ot = [e(nc.sbuf_tensor(f"ot{c}", [128, N], BF16)) for c in range(DC)]
    mqt = e(nc.sbuf_tensor("mqt", [32, N], BF16))
    ident = e(nc.sbuf_tensor("ident", [128, 128], BF16))
    identf = e(nc.sbuf_tensor("identf", [128, 128], F32))
    mask_tm = e(nc.sbuf_tensor("mask_tm", [128, 128], BF16))
    mask_kb = e(nc.sbuf_tensor("mask_kb", [128, 512], BF16))
    gcol = e(nc.sbuf_tensor("gcol", [128, DC], F32))
    becol = e(nc.sbuf_tensor("becol", [128, DC], F32))
    becol_bf = e(nc.sbuf_tensor("becol_bf", [128, DC], BF16))
    bqcol = e(nc.sbuf_tensor("bqcol", [128, DC], F32))
    bkcol = e(nc.sbuf_tensor("bkcol", [128, DC], F32))
    bvcol = e(nc.sbuf_tensor("bvcol", [128, DC], F32))
    biasq = e(nc.sbuf_tensor("biasq", [128, DC], F32))
    biask = e(nc.sbuf_tensor("biask", [128, DC], F32))
    biasv = e(nc.sbuf_tensor("biasv", [128, DC], F32))
    relb = e(nc.sbuf_tensor("relb", [32, 512], BF16))
    onescol = e(nc.sbuf_tensor("onescol", [128, 1], F32))
    onesrow_b = e(nc.sbuf_tensor("onesrow_b", [1, 128], BF16))
    onesrow_f = e(nc.sbuf_tensor("onesrow_f", [1, 128], F32))
    st_mu = e(nc.sbuf_tensor("st_mu", [128, NT], F32))
    st_s2 = e(nc.sbuf_tensor("st_s2", [128, NT], F32))
    st_var = e(nc.sbuf_tensor("st_var", [128, NT], F32))
    st_t1 = e(nc.sbuf_tensor("st_t1", [128, NT], F32))
    st_t2 = e(nc.sbuf_tensor("st_t2", [128, NT], F32))
    st_t3 = e(nc.sbuf_tensor("st_t3", [128, NT], F32))
    st_t4 = e(nc.sbuf_tensor("st_t4", [128, NT], F32))
    b_all = e(nc.sbuf_tensor("b_all", [128, NT], F32))
    ab_all = e(nc.sbuf_tensor("ab_all", [128, NT], F32))
    nmub = e(nc.sbuf_tensor("nmub", [128, NT], F32))
    npm = [e(nc.sbuf_tensor(f"npm{i}", [128, 64], F32)) for i in range(2)]
    nmx16 = [e(nc.sbuf_tensor(f"nmx16{i}", [128, NT], F32)) for i in range(2)]
    nm_stage = [e(nc.sbuf_tensor(f"nmstage{i}", [16, 128], BF16)) for i in range(2)]
    usb = [e(nc.sbuf_tensor(f"usb{i}", [65, 512], F32)) for i in range(2)]
    zb = e(nc.sbuf_tensor("zb", [65, 512], F32))
    zrowr = zb.ap()[64:65, :]
    zsh = e(nc.sbuf_tensor("zsh", [64, 512], BF16))
    qaug = [e(nc.sbuf_tensor(f"qaug{i}", [98, N], BF16)) for i in range(4)]
    kaug = [e(nc.sbuf_tensor(f"kaug{i}", [98, N], BF16)) for i in range(4)]
    vtm = [e(nc.sbuf_tensor(f"vtm{i}", [128, NT * 130], BF16)) for i in range(2)]
    pt = [e(nc.sbuf_tensor(f"pt{i}", [128, 1024], BF16)) for i in range(2)]
    sf = [e(nc.sbuf_tensor(f"sf{i}", [128, 1024], F32)) for i in range(2)]
    xr = [e(nc.sbuf_tensor(f"xr{i}", [128, 1024], F32)) for i in range(2)]
    osb = [e(nc.sbuf_tensor(f"osb{i}", [128, 1024], F32)) for i in range(1)]
    stage_hi = e(nc.sbuf_tensor("stage_hi", [128, N], BF16))
    brow_bo = e(nc.sbuf_tensor("brow_bo", [1, D], F32))
    brow_fb = e(nc.sbuf_tensor("brow_fb", [1, D], F32))
    brow_c = e(nc.sbuf_tensor("brow_c", [1, D], BF16))
    al_st = e(nc.sbuf_tensor("al_st", [1, 3], F32))
    bc3 = e(nc.sbuf_tensor("bc3", [128, 3], F32))
    wsum = e(nc.sbuf_tensor("wsum", [128, DC], F32))
    wsum1 = e(nc.sbuf_tensor("wsum1", [128, 1], F32))
    wqT = e(nc.sbuf_tensor("wqT", [128, DC * 1024], BF16))
    wkT = e(nc.sbuf_tensor("wkT", [128, DC * 1024], BF16))
    wvT = e(nc.sbuf_tensor("wvT", [128, DC * 1024], BF16))
    vwo = nc.virtual_sbuf_tensor("vwo", [128, DC * 1024], BF16)
    nc.back_virtual_tensor_with(vwo, wqT.tensor if hasattr(wqT, "tensor") else wqT)
    vft = nc.virtual_sbuf_tensor("vft", [128, DC * 1024], BF16)
    nc.back_virtual_tensor_with(vft, wkT.tensor if hasattr(wkT, "tensor") else wkT)
    woT = [vwo.ap()[:, 1024 * c:1024 * (c + 1)] for c in range(DC)]
    ftT = [vft.ap()[:, 1024 * c:1024 * (c + 1)] for c in range(DC)]
    pt.append(osb[0].ap().bitcast(BF16)[:, 0:1024])

    pA = [e(nc.psum_tensor(f"pA{i}", [128, 512], F32)) for i in range(2)]
    pBw = e(nc.psum_tensor("pBw", [128, 2048], F32))
    pB = [pBw.ap()[:, 1024 * i:1024 * (i + 1)] for i in range(2)]
    pU = [e(nc.psum_tensor(f"pU{i}", [65, 512], F32)) for i in range(2)]

    s_c = _Sem(e(nc.semaphore("s_c")))
    s_x = [_Sem(e(nc.semaphore("s_x0"))), _Sem(e(nc.semaphore("s_x1")))]
    s_sg = _Sem(e(nc.semaphore("s_sg")))
    s_mls = _Sem(e(nc.semaphore("s_mls")))
    s_rb = _Sem(e(nc.semaphore("s_rb")))
    s_nz = _Sem(e(nc.semaphore("s_nz")))
    s_w = [_Sem(e(nc.semaphore("s_w0"))), _Sem(e(nc.semaphore("s_w1")))]
    s_st = [_Sem(e(nc.semaphore("s_st0"))), _Sem(e(nc.semaphore("s_st1")))]
    s_pe = _Sem(e(nc.semaphore("s_pe")))
    s_dv = _Sem(e(nc.semaphore("s_dv")))
    s_ac = _Sem(e(nc.semaphore("s_ac")))
    s_gp = _Sem(e(nc.semaphore("s_gp")))

    P = _Prog()

    def dma(dst, src, waits=(), sem=None):
        return P.op("sync", lambda g, d=dst, s=src: g.dma_start(out=d, in_=s),
                    waits=waits, inc=sem, amt=16)

    def store(dst, src, waits=(), sem=None):
        return P.op("sync", lambda g, d=dst, s=src: g.dma_start(out=d, in_=s),
                    waits=waits, inc=sem, amt=16)

    def vop(fn, waits=(), inc=s_dv):
        return P.op("vector", fn, waits=waits, inc=inc)

    free_tok = {}

    def acquire(key):
        toks = free_tok.get(key, ())
        free_tok[key] = []
        return list(toks)

    def consumed(key, tok):
        free_tok.setdefault(key, []).append(tok)

    dma(mqt[:], dmqt, sem=s_c)
    dma(ident[:], dident, sem=s_c)
    dma(identf[:], didf, sem=s_c)
    dma(mask_tm[:], dmtm, sem=s_c)
    dma(mask_kb[:], dmkb, sem=s_c)
    for h in range(H):
        P.op("gpsimd", lambda g, h=h: g.dma_start(out=relb[:, 32 * h:32 * h + 32],
                                                  in_=drel[h]), inc=s_rb, amt=16)
    t_relb = (s_rb, s_rb.n)
    for c in range(DC):
        dma(gcol[:, c:c + 1], dga[c * 128:(c + 1) * 128].rearrange("(p o) -> p o", o=1), sem=s_c)
        dma(becol[:, c:c + 1], dbe[c * 128:(c + 1) * 128].rearrange("(p o) -> p o", o=1), sem=s_c)
        dma(bqcol[:, c:c + 1], dbq[c * 128:(c + 1) * 128].rearrange("(p o) -> p o", o=1), sem=s_c)
        dma(bkcol[:, c:c + 1], dbk[c * 128:(c + 1) * 128].rearrange("(p o) -> p o", o=1), sem=s_c)
        dma(bvcol[:, c:c + 1], dbv[c * 128:(c + 1) * 128].rearrange("(p o) -> p o", o=1), sem=s_c)
    dma(brow_bo[:], dbo, sem=s_c)
    dma(brow_fb[:], dfb, sem=s_c)
    dma(al_st[:, 2:3], dal, sem=s_c)
    for i in range(4):
        dma(qaug[i][64:96, :], dmqt, sem=s_c)
    t_const = (s_c, s_c.n)
    t_mqt = t_id = t_idf = t_mtm = t_mkb = t_cols = t_rows = t_const
    qa_mq = [t_const] * 4

    P.op("vector", lambda v: v.memset(onescol[:], 1.0))
    P.op("vector", lambda v: v.memset(onesrow_b[:], 1.0))
    P.op("vector", lambda v: v.memset(onesrow_f[:], 1.0))
    ka_ones = [P.op("vector", lambda v, i=i: v.memset(kaug[i][96:98, :], 1.0), inc=s_dv)
               for i in range(4)]
    vt_ones = []
    for i in range(2):
        P.op("vector", lambda v, i=i: v.memset(
            vtm[i].ap().rearrange("p (t c) -> p t c", c=130)[:, :, 64:65], 1.0))
        vt_ones.append(P.op("vector", lambda v, i=i: v.memset(
            vtm[i].ap().rearrange("p (t c) -> p t c", c=130)[:, :, 129:130], 1.0),
            inc=s_dv))
    t_beb = vop(lambda v: v.tensor_copy(becol_bf[:], becol[:]), waits=[t_cols])

    def gen_statsA():
        for t in range(NT):
            xs = t % 2
            tx = dma(xr[xs][:], dx[128 * t:128 * (t + 1), :], waits=acquire(("xr", xs)),
                     sem=s_x[xs])
            tr = vop(lambda v, t=t, xs=xs: v.reduce_sum(st_mu[:, t:t + 1], xr[xs][:],
                                                        axis=AX), waits=[tx])
            consumed(("xr", xs), tr)
            ss = t % 2
            tq = P.op("scalar", lambda s, xs=xs, ss=ss, t=t: s.activation(
                out=sf[ss][:], in_=xr[xs][:], func=AF.Square,
                accum_out=st_s2[:, t:t + 1]), waits=[tx] + acquire(("sf", ss)), inc=s_ac)
            consumed(("xr", xs), tq)
            consumed(("sf", ss), tq)
            yield


    wT_all = {"q": wqT, "k": wkT, "v": wvT}
    dw_all = {"q": dwq, "k": dwk, "v": dwv}
    bcol_all = {"q": bqcol, "k": bkcol, "v": bvcol}
    bias_all = {"q": biasq, "k": biask, "v": biasv}
    bias_tok = {}
    ev_tok = {}

    stage_free = {"any": None}
    wst = [stage_hi.ap()[:, 0:1024], stage_hi.ap()[:, 1024:2048]]
    pBbf = [pB[i].bitcast(BF16) for i in range(2)]

    def gen_wprep():
      for wn in ("q", "k", "v"):
        for sl in range(DC):
            xs = sl % 2
            tw = P.op("gpsimd", lambda g, wn=wn, sl=sl, xs=xs: g.dma_start(
                out=wst[xs], in_=dw_all[wn][128 * sl:128 * (sl + 1), :]),
                waits=acquire(("wst", xs)), inc=s_w[xs], amt=16)
            evs = []
            for half in range(2):
                ps = half
                tws = acquire(("pB", ps))
                tp = None
                for q in range(4):
                    c = 4 * half + q
                    tp = P.op("tensor", lambda te, xs=xs, c=c, ps=ps, q=q: te.transpose(
                        pBbf[ps][:, 128 * q:128 * (q + 1)], wst[xs][:, 128 * c:128 * (c + 1)],
                        ident[:]), waits=([tw, t_id] + tws) if q == 0 else (),
                        inc=s_pe if q == 3 else None)
                consumed(("wst", xs), tp)
                tv = P.op("scalar", lambda s, wn=wn, sl=sl, half=half, ps=ps: s.activation(
                    out=wT_all[wn][:, 1024 * sl + 512 * half:1024 * sl + 512 * half + 512],
                    in_=pBbf[ps][:, 0:512], func=AF.Identity), waits=[tp], inc=s_ac)
                consumed(("pB", ps), tv)
                evs.append(tv)
            ev_tok[(wn, sl)] = evs[-1]
            us = 0
            utws = acquire(("pB", us))
            tm = None
            for c in range(DC):
                tm = P.op("tensor", lambda te, wn=wn, sl=sl, c=c, us=us: te.matmul(
                    pB[us][:, 0:1],
                    wT_all[wn][:, 1024 * sl + 128 * c:1024 * sl + 128 * (c + 1)],
                    becol_bf[:, c:c + 1], start=(c == 0), stop=(c == DC - 1)),
                    waits=([evs[-1], t_beb] + utws) if c == 0 else (),
                    inc=s_pe if c == DC - 1 else None)
            if wn == "q":
                tb_ = vop(lambda v, wn=wn, sl=sl, us=us: v.tensor_scalar(
                    out=bias_all[wn][:, sl:sl + 1], in0=pB[us][:, 0:1],
                    scalar1=bcol_all[wn][:, sl:sl + 1], scalar2=0.125,
                    op0=ALU.add, op1=ALU.mult), waits=[tm, t_cols])
            else:
                tb_ = vop(lambda v, wn=wn, sl=sl, us=us: v.tensor_scalar(
                    out=bias_all[wn][:, sl:sl + 1], in0=pB[us][:, 0:1],
                    scalar1=bcol_all[wn][:, sl:sl + 1], scalar2=None,
                    op0=ALU.add), waits=[tm, t_cols])
            consumed(("pB", us), tb_)
            bias_tok[(wn, sl)] = tb_
            yield

    def zip_simple(g1, g2):
        d1 = d2 = False
        while not (d1 and d2):
            if not d1 and next(g1, StopIteration) is StopIteration:
                d1 = True
            if not d2 and next(g2, StopIteration) is StopIteration:
                d2 = True

    zip_simple(gen_statsA(), gen_wprep())
    stage_free["any"] = (s_pe, s_pe.n)
    t_statsA_ac = (s_ac, s_ac.n)
    t_allbias_pe = (s_pe, s_pe.n)

    tk = vop(lambda v: v.tensor_scalar(out=st_mu[:], in0=st_mu[:], scalar1=1.0 / D,
                                       scalar2=None, op0=ALU.mult), waits=[t_statsA_ac])
    tk = vop(lambda v: v.tensor_scalar(out=st_s2[:], in0=st_s2[:], scalar1=1.0 / D,
                                       scalar2=None, op0=ALU.mult))
    tk = vop(lambda v: v.tensor_tensor(st_t1[:], st_mu[:], st_mu[:], ALU.mult))
    tk = vop(lambda v: v.tensor_tensor(st_var[:], st_s2[:], st_t1[:], ALU.subtract))
    tk = vop(lambda v: v.tensor_scalar(out=st_t1[:], in0=st_var[:], scalar1=EPS,
                                       scalar2=None, op0=ALU.add))
    tk = vop(lambda v: v.reciprocal(st_t2[:], st_t1[:]))
    tk = vop(lambda v: v.tensor_tensor(st_t3[:], st_var[:], st_t2[:], ALU.mult))
    tk = vop(lambda v: v.tensor_scalar(out=st_t3[:], in0=st_t3[:], scalar1=EPS,
                                       scalar2=None, op0=ALU.add))
    tab = P.op("scalar", lambda s: s.activation(out=ab_all[:], in_=st_t3[:], func=AF.Sqrt),
               waits=[tk], inc=s_ac)
    tk = vop(lambda v: v.reciprocal(st_t4[:], st_t3[:]))
    tk = vop(lambda v: v.tensor_tensor(st_t4[:], st_t2[:], st_t4[:], ALU.mult))
    tb = P.op("scalar", lambda s: s.activation(out=b_all[:], in_=st_t4[:], func=AF.Sqrt),
              waits=[tk], inc=s_ac)

    def newton_sqrt(y, u, tmp1, tmp2, w0):
        t1 = vop(lambda v: v.tensor_tensor(tmp1[:], y[:], y[:], ALU.mult), waits=[w0])
        t1 = vop(lambda v: v.tensor_tensor(tmp1[:], u[:], tmp1[:], ALU.subtract))
        t1 = vop(lambda v: v.reciprocal(tmp2[:], y[:]))
        t1 = vop(lambda v: v.tensor_tensor(tmp1[:], tmp1[:], tmp2[:], ALU.mult))
        t1 = vop(lambda v: v.tensor_scalar(out=tmp1[:], in0=tmp1[:], scalar1=0.5,
                                           scalar2=None, op0=ALU.mult))
        return vop(lambda v: v.tensor_tensor(y[:], y[:], tmp1[:], ALU.add))

    t_ab = newton_sqrt(ab_all, st_t3, st_t1, st_t2, tab)
    t_b = newton_sqrt(b_all, st_t4, st_t1, st_t2, tb)
    tk = vop(lambda v: v.tensor_tensor(nmub[:], st_mu[:], b_all[:], ALU.mult))
    t_nmub = vop(lambda v: v.tensor_scalar(out=nmub[:], in0=nmub[:], scalar1=-1.0,
                                           scalar2=None, op0=ALU.mult))

    for t in range(NT):
        xs = t % 2
        tx = dma(xr[xs][:], dx[128 * t:128 * (t + 1), :], waits=acquire(("xr", xs)),
                 sem=s_x[xs])
        ss = t % 2
        ta = vop(lambda v, t=t, xs=xs, ss=ss: v.tensor_scalar(
            out=sf[ss][:], in0=xr[xs][:], scalar1=b_all[:, t:t + 1],
            scalar2=nmub[:, t:t + 1], op0=ALU.mult, op1=ALU.add),
            waits=[tx, t_b, t_nmub] + acquire(("sf", ss)))
        consumed(("xr", xs), ta)
        for half in range(2):
            ps = half
            tws = acquire(("pB", ps))
            tp = None
            for q in range(4):
                c = 4 * half + q
                last = (q == 3)
                tp = P.op("tensor", lambda te, ss=ss, c=c, ps=ps, q=q: te.transpose(
                    pB[ps][:, 128 * q:128 * (q + 1)], sf[ss][:, 128 * c:128 * (c + 1)],
                    identf[:]), waits=([ta, t_idf] + tws) if q == 0 else (),
                    inc=s_pe if last else None)
            consumed(("sf", ss), tp)
            dst = xutA.ap().rearrange("p (c t2 k) -> p c t2 k", c=DC, t2=NT)[:, 4 * half:4 * half + 4, t, :]
            tv = P.op("scalar", lambda s, dst=dst, ps=ps: s.activation(
                out=dst, in_=pB[ps][:, 0:512].rearrange("p (q k) -> p q k", q=4),
                func=AF.Identity), waits=[tp], inc=s_ac)
            consumed(("pB", ps), tv)
    t_xut = (s_dv, s_dv.n)

    gscale_tok = {}
    for wn in ("q", "k", "v"):
        for c in range(DC):
            ap = wT_all[wn].ap().rearrange("p (s c2 k) -> p s c2 k", s=DC, c2=DC)[:, :, c, :]
            if wn == "q":
                gscale_tok[(wn, c)] = vop(lambda v, ap=ap, c=c: v.tensor_scalar(
                    out=ap, in0=ap, scalar1=gcol[:, c:c + 1], scalar2=0.125,
                    op0=ALU.mult, op1=ALU.mult), waits=[t_allbias_pe, t_cols])
            else:
                gscale_tok[(wn, c)] = vop(lambda v, ap=ap, c=c: v.tensor_scalar(
                    out=ap, in0=ap, scalar1=gcol[:, c:c + 1], scalar2=None,
                    op0=ALU.mult), waits=[t_allbias_pe, t_cols])

    qaug_ready = [None] * H
    kaug_ready = [None] * H
    kaug_rel = [None] * H
    vtm_ready = [None] * 8
    mls_ready = [None] * H
    pass2_done_pe = [None] * H
    pass1_done_pe = [None] * H
    p1_lastexp = [None] * H
    ucopy = {}
    uevac = {}
    prev_scatter = [None]

    def emit_qkv_pair(j):
        a, b = 2 * j, 2 * j + 1
        sa, sb = a % 4, b % 4
        guard_a = [pass2_done_pe[a - 4], pass1_done_pe[a - 4]] if a >= 4 else []
        guard_b = [pass2_done_pe[b - 4], pass1_done_pe[b - 4]] if b >= 4 else []
        vguard = ([pass2_done_pe[2 * (j - 2)], pass2_done_pe[2 * (j - 2) + 1]]
                  if j >= 2 else [])
        for h, sh, gd in ((a, sa, guard_a), (b, sb, guard_b)):
            tvl = None
            for ch in range(4):
                ps = ch % 2
                tws = acquire(("pB", ps))
                tm = P.op("tensor", lambda te, h=h, ch=ch, ps=ps: te.matmul(
                    pB[ps][64:96, 0:512], relb[:, 32 * h:32 * h + 32],
                    mqt[:, 512 * ch:512 * (ch + 1)], start=True, stop=True,
                    tile_position=(0, 64)),
                    waits=[t_relb, t_mqt] + tws, inc=s_pe)
                tvl = P.op("scalar", lambda s, sh=sh, ch=ch, ps=ps: s.activation(
                    out=kaug[sh][64:96, 512 * ch:512 * (ch + 1)],
                    in_=pB[ps][64:96, 0:512], func=AF.Identity),
                    waits=[tm] + gd, inc=s_ac)
                consumed(("pB", ps), tvl)
            kaug_rel[h] = tvl

        for wn in ("v", "q", "k"):
            ev_hi = []
            for half in range(2):
                gw = [gscale_tok[(wn, c)] for c in range(DC)] if j == 0 else []
                tms = {}
                for sub in range(2):
                    tc = 2 * half + sub
                    ps = sub
                    tws = acquire(("pA", ps))
                    tm = None
                    for c in range(DC):
                        tm = P.op("tensor", lambda te, wn=wn, j=j, c=c, tc=tc, ps=ps: te.matmul(
                            pA[ps][:],
                            wT_all[wn][:, 1024 * j + 128 * c:1024 * j + 128 * (c + 1)],
                            xut[c][:, 512 * tc:512 * (tc + 1)],
                            start=(c == 0), stop=(c == DC - 1)),
                            waits=(gw + tws + [t_xut]) if c == 0 else (),
                            inc=s_pe if c == DC - 1 else None)
                    tms[sub] = tm
                if wn == "v":
                    ssl = half
                    tvh = {}
                    for sub in range(2):
                        tvh[sub] = P.op("scalar", lambda s, j=j, ps=sub, ssl=ssl, sub=sub: s.activation(
                            out=sf[ssl][:, 512 * sub:512 * sub + 512], in_=pA[ps][:],
                            func=AF.Identity, bias=biasv[:, j:j + 1]),
                            waits=[tms[sub], bias_tok[("v", j)]] +
                                  (acquire(("sf", ssl)) if sub == 0 else []), inc=s_ac)
                        consumed(("pA", sub), tvh[sub])
                    tv = tvh[1]
                    vslot = j % 2
                    tplast = None
                    for g in range(2):
                        pq = g
                        ptws = acquire(("pB", pq))
                        tp = None
                        for q in range(4):
                            qq = 4 * g + q
                            tp = P.op("tensor", lambda te, ssl=ssl, qq=qq, q=q, pq=pq: te.transpose(
                                pB[pq][:, 128 * q:128 * (q + 1)],
                                sf[ssl][:, 128 * qq:128 * (qq + 1)],
                                identf[:]), waits=([tvh[g], t_idf] + ptws) if q == 0 else (),
                                inc=s_pe if q == 3 else None)
                        vdst = vtm[vslot].ap().rearrange("p (t c) -> p t c", c=130)[
                            :, 8 * ssl + 4 * g:8 * ssl + 4 * g + 4, :]
                        psrc = pB[pq][:, 0:512].rearrange("p (q k) -> p q k", q=4)
                        tva = P.op("scalar", lambda s, vdst=vdst, psrc=psrc: s.activation(
                            out=vdst[:, :, 0:64], in_=psrc[:, :, 0:64],
                            func=AF.Identity), waits=[tp] + vguard, inc=s_ac)
                        tvv = P.op("scalar", lambda s, vdst=vdst, psrc=psrc: s.activation(
                            out=vdst[:, :, 65:129], in_=psrc[:, :, 64:128],
                            func=AF.Identity), inc=s_ac)
                        consumed(("pB", pq), tva)
                        consumed(("pB", pq), tvv)
                        tplast = tp
                    consumed(("sf", ssl), tplast)
                    vtm_ready[j] = tvv
                else:
                    dst = qaug[sa] if wn == "q" else kaug[sa]
                    bt = bias_tok[(wn, j)]
                    tv1 = None
                    for sub in range(2):
                        tc = 2 * half + sub
                        tv1 = P.op("scalar", lambda s, dst=dst, tc=tc, ps=sub, wn=wn, j=j: s.activation(
                            out=dst[0:64, 512 * tc:512 * tc + 512], in_=pA[ps][0:64, :],
                            func=AF.Identity, bias=bias_all[wn][0:64, j:j + 1]),
                            waits=[tms[sub], bt] + guard_a, inc=s_ac)
                        tv2 = P.op("scalar", lambda s, tc=tc, ps=sub, wn=wn, j=j: s.activation(
                            out=stage_hi[64:128, 512 * tc:512 * tc + 512],
                            in_=pA[ps][64:128, :],
                            func=AF.Identity, bias=bias_all[wn][64:128, j:j + 1]),
                            waits=[tms[sub], bt, stage_free["any"]], inc=s_ac)
                        consumed(("pA", sub), tv1)
                        consumed(("pA", sub), tv2)
                        ev_hi.append(tv2)
                    if half == 1:
                        if wn == "q":
                            qaug_ready[a] = tv1
                        else:
                            kaug_ready[a] = tv1
            if wn in ("q", "k"):
                dstb = qaug[sb] if wn == "q" else kaug[sb]
                td = P.op("sync", lambda g, dstb=dstb: g.dma_start(
                    out=dstb[0:64, :], in_=stage_hi[64:128, :]),
                    waits=ev_hi + guard_b, inc=s_sg, amt=16)
                stage_free["any"] = td
                if wn == "q":
                    qaug_ready[b] = td
                else:
                    kaug_ready[b] = td

    def gen_p1rows(h):
        sh = h % 4
        r = h % 2
        prev = mls_ready[h - 2] if h >= 2 else None
        slot_guard = [pass2_done_pe[h - 4]] if h >= 4 else []
        tz = vop(lambda v, r=r: v.memset(npm[r][:], 3.0e38), waits=[prev] if prev else [])
        qr = qaug_ready[h]
        kr = [kaug_ready[h], kaug_rel[h], ka_ones[sh], qa_mq[sh]]
        pc = 0
        for qt in range(NT):
            L = 128 * (qt + 1)
            npieces = (L + 511) // 512
            for p in range(npieces):
                lo = 512 * p
                w = min(512, L - lo)
                ps = pc % 2
                pc += 1
                tws = acquire(("pA", ps))
                has_mask = (lo + w == L)
                ol = L - 128 - lo if has_mask else -1
                tm = P.op("tensor", lambda te, sh=sh, qt=qt, lo=lo, w=w, ps=ps,
                          hm=has_mask: te.matmul(
                    pA[ps][:, 0:w], qaug[sh][0:96, 128 * qt:128 * qt + 128],
                    kaug[sh][0:96, lo:lo + w], start=True, stop=(not hm)),
                    waits=[qr] + kr + tws, inc=None if has_mask else s_pe)
                if has_mask:
                    tm = P.op("tensor", lambda te, ol=ol, ps=ps: te.matmul(
                        pA[ps][:, ol:ol + 128], ident[:], mask_tm[:],
                        start=False, stop=True), waits=[t_id, t_mtm], inc=s_pe)
                col = 4 * qt + p
                tr = vop(lambda v, r=r, col=col, ps=ps, w=w: v.tensor_reduce(
                    npm[r][:, col:col + 1], pA[ps][:, 0:w], axis=AX, op=ALU.max,
                    negate=True), waits=[tm])
                consumed(("pA", ps), tr)
                yield
        pass1_done_pe[h] = (s_pe, s_pe.n)

    def gen_p1mls(h):
        sh = h % 4
        r = h % 2
        slot_guard = [pass2_done_pe[h - 4]] if h >= 4 else []
        t1 = vop(lambda v, r=r: v.tensor_reduce(
            nmx16[r][:], npm[r].ap().rearrange("p (q c) -> p q c", c=4), axis=AX,
            op=ALU.min))
        yield
        ps = 0
        tws = acquire(("pB", ps))
        tt = P.op("tensor", lambda te, r=r, ps=ps: te.transpose(
            pB[ps][0:16, 0:128], nmx16[r][:], identf[:]),
            waits=[t1, t_idf] + tws, inc=s_pe)
        tv = P.op("scalar", lambda s, r=r, ps=ps: s.activation(
            out=nm_stage[r][:], in_=pB[ps][0:16, 0:128], func=AF.Identity),
            waits=[tt], inc=s_ac)
        consumed(("pB", ps), tv)
        yield
        td1 = P.op("sync", lambda g, sh=sh, r=r: g.dma_start(
            out=qaug[sh][96:97, :], in_=nm_stage[r][:]),
            waits=[tv, prev_scatter[0]] + slot_guard, inc=s_mls, amt=16)
        mls_ready[h] = td1
        prev_scatter[0] = td1
        yield

    def gen_pass2(h):
        sh = h % 4
        j = h // 2
        vslot = j % 2
        odd = h % 2
        wc = 0
        from collections import deque
        avq = deque()
        postq = deque()

        def tick():
            for it in postq:
                it[0] -= 1
            while postq and postq[0][0] <= 0:
                postq.popleft()[1]()

        def flushpost():
            while postq:
                postq.popleft()[1]()

        for nj in range(4):
            us = nj % 2
            if nj >= 2:
                utws = [ucopy[(h, nj - 2)]]
            elif h > 0:
                utws = [ucopy[(h - 1, nj + 2)]]
            else:
                utws = []
            for k0 in range(0, 4 * nj + 4, 2):
                ws = wc % 2
                tws = acquire(("pB", ws))
                tmh = {}
                for sub in range(2):
                    kt = k0 + sub
                    diag = (kt >= 4 * nj)
                    tm = P.op("tensor", lambda te, sh=sh, kt=kt, nj=nj, ws=ws, sub=sub,
                              diag=diag: te.matmul(
                        pB[ws][:, 512 * sub:512 * sub + 512],
                        kaug[sh][0:97, 128 * kt:128 * kt + 128],
                        qaug[sh][0:97, 512 * nj:512 * nj + 512], start=True,
                        stop=(not diag)),
                        waits=([mls_ready[h], kaug_ready[h], kaug_rel[h],
                                vtm_ready[j]] + tws) if sub == 0 else (),
                        inc=None if diag else s_pe)
                    if diag:
                        o = 128 * (kt - 4 * nj)
                        tm = P.op("tensor", lambda te, o=o, ws=ws, sub=sub: te.matmul(
                            pB[ws][:, 512 * sub:512 * sub + o + 128],
                            ident[:], mask_kb[:, 384 - o:512],
                            start=False, stop=True), waits=[t_mkb], inc=s_pe)
                    tmh[sub] = tm
                pts = wc % 3
                wc += 1
                ptws = acquire(("pt", pts))
                te_ = P.op("scalar", lambda s, ws=ws, pts=pts: s.activation(
                    out=pt[pts][:], in_=pB[ws][:], func=AF.Exp),
                    waits=[tmh[1]] + ptws, inc=s_ac)
                consumed(("pB", ws), te_)
                if len(avq) >= 2:
                    avq.popleft()()
                avq.append(lambda vs=vslot, h=h, k0=k0, us=us, pts=pts, nj=nj,
                           te_=te_, utws=(utws if k0 == 0 else []), odd=odd:
                           _av2(vs, h, k0, us, pts, nj, te_, utws, odd))
                tick()
                yield
            while avq:
                avq.popleft()()
            _norm_chain(h, nj, us, odd, (s_pe, s_pe.n), postq)
        flushpost()
        pass2_done_pe[h] = (s_pe, s_pe.n)

    def _av2(vs, h, k0, us, pts, nj, te_, utws, odd):
        for sub in range(2):
            kt = k0 + sub
            ta = P.op("tensor", lambda te, vs=vs, kt=kt, us=us, pts=pts, nj=nj,
                      sub=sub, odd=odd: te.matmul(
                pU[us][0:65, :],
                vtm[vs][:, 130 * kt + 65 * odd:130 * kt + 65 * odd + 65],
                pt[pts][:, 512 * sub:512 * sub + 512],
                start=(kt == 0), stop=(kt == 4 * nj + 3)),
                waits=([te_] + list(utws)) if sub == 0 else (),
                inc=s_pe if sub == 1 else None)
            if sub == 1:
                consumed(("pt", pts), ta)

    last_nz = [None]
    last_stt = {0: None, 1: None, "zb": None}

    def _norm_chain(h, nj, us, odd, av_tok, postq):
        while postq:
            postq.popleft()[1]()
        ub = nj % 2
        tcp = P.op("scalar", lambda s, us=us, ub=ub: s.activation(
            out=usb[ub][:], in_=pU[us][0:65, :], func=AF.Identity),
            waits=[av_tok, last_stt[ub]], inc=s_ac)
        ucopy[(h, nj)] = tcp

        def mid(tcp=tcp):
            tl = P.op("scalar", lambda s, ub=ub: s.activation(
                out=zrowr, in_=usb[ub][64:65, :], func=AF.Ln),
                waits=[tcp], inc=s_ac)
            tx = P.op("scalar", lambda s, ub=ub: s.activation(
                out=usb[ub][64:65, :], in_=zrowr, func=AF.Exp, scale=-1.0),
                waits=[tl], inc=s_ac)
            td3 = P.op("sync", lambda g, ub=ub: g.dma_start(
                out=zb[0:64, :],
                in_=usb[ub][64:65, :].unsqueeze(1).broadcast_to([1, 64, 512])),
                waits=[tx, last_stt["zb"], last_nz[0]], inc=s_nz, amt=16)
            postq.append([2, lambda td3=td3: fin(td3)])

        def fin(td3):
            ub2 = nj % 2
            if odd == 0:
                tst = P.op("gpsimd", lambda g, h=h, nj=nj, ub2=ub2: g.tensor_tensor(
                    ot[h // 2][0:64, 512 * nj:512 * nj + 512],
                    usb[ub2][0:64, :], zb[0:64, :], ALU.mult), waits=[td3], inc=s_gp)
                uevac[(h, nj)] = tst
                last_nz[0] = td3
                last_stt[nj % 2] = tst
                last_stt["zb"] = tst
            else:
                tst = P.op("gpsimd", lambda g, ub2=ub2: g.tensor_tensor(
                    zsh[:], usb[ub2][0:64, :], zb[0:64, :], ALU.mult),
                    waits=[td3, last_nz[0]], inc=s_gp)
                tsh = P.op("sync", lambda g, h=h, nj=nj: g.dma_start(
                    out=ot[h // 2][64:128, 512 * nj:512 * nj + 512], in_=zsh[:]),
                    waits=[tst], inc=s_nz, amt=16)
                uevac[(h, nj)] = tsh
                last_nz[0] = tsh
                last_stt[nj % 2] = tst
                last_stt["zb"] = tst

        postq.append([1, mid])

    p3tok = {}
    p3tok = {}

    def gen_p3prep():
        for sl in range(DC):
            xs = sl % 2
            tw = dma(xr[xs][:], dfw[128 * sl:128 * (sl + 1), :], waits=acquire(("xr", xs)),
                     sem=s_x[xs])
            tr = vop(lambda v, sl=sl, xs=xs: v.tensor_reduce(
                wsum[:, sl:sl + 1], xr[xs][:], axis=AX, op=ALU.add,
                apply_absolute_value=True), waits=[tw])
            consumed(("xr", xs), tr)
            yield
        t1 = vop(lambda v: v.tensor_reduce(wsum1[:], wsum[:], axis=AX, op=ALU.add))
        yield
        bs = 0
        btws = acquire(("pB", bs))
        tm = P.op("tensor", lambda te, bs=bs: te.matmul(
            pB[bs][0:1, 0:1], wsum1[:], onescol[:], start=True, stop=True),
            waits=[t1] + btws, inc=s_pe)
        t2 = vop(lambda v, bs=bs: v.tensor_scalar(
            out=al_st[:, 0:1], in0=pB[bs][0:1, 0:1], scalar1=0.7 / float(D * D),
            scalar2=None, op0=ALU.mult), waits=[tm])
        consumed(("pB", bs), t2)
        t3 = vop(lambda v: v.tensor_scalar(out=al_st[:, 1:2], in0=al_st[:, 0:1],
                                           scalar1=-1.0, scalar2=None, op0=ALU.mult),
                 waits=[t_rows])
        yield
        bs = 1
        btws = acquire(("pB", bs))
        tm = P.op("tensor", lambda te, bs=bs: te.matmul(
            pB[bs][:, 0:3], onesrow_f[:], al_st[:], start=True, stop=True),
            waits=[t3] + btws, inc=s_pe)
        t_bc3 = vop(lambda v, bs=bs: v.tensor_copy(bc3[:], pB[bs][:, 0:3]), waits=[tm])
        consumed(("pB", bs), t_bc3)
        yield

        tcb = vop(lambda v: v.tensor_tensor(brow_bo[:], brow_bo[:], brow_fb[:], ALU.add),
                  waits=[t_rows])
        p3tok["crow"] = vop(lambda v: v.tensor_copy(brow_c[:], brow_bo[:]))

        t_ph2 = t_pair7[0]
        tw_next = dma(xr[0][:], dwo[0:128, :], waits=acquire(("xr", 0)), sem=s_x[0])
        for sl in range(DC):
            xs = sl % 2
            tw = tw_next
            if sl + 1 < DC:
                nxs = (sl + 1) % 2
                tw_next = dma(xr[nxs][:], dwo[128 * (sl + 1):128 * (sl + 2), :],
                              waits=acquire(("xr", nxs)), sem=s_x[nxs])
            for half in range(2):
                ps = half
                tws = acquire(("pB", ps))
                tp = None
                for q in range(4):
                    c = 4 * half + q
                    tp = P.op("tensor", lambda te, xs=xs, c=c, ps=ps, q=q: te.transpose(
                        pB[ps][:, 128 * q:128 * (q + 1)], xr[xs][:, 128 * c:128 * (c + 1)],
                        identf[:]), waits=([tw, t_idf] + tws) if q == 0 else (),
                        inc=s_pe if q == 3 else None)
                consumed(("xr", xs), tp)
                dst = vwo.ap().rearrange("p (c s2 k) -> p c s2 k", c=DC, s2=DC)[:, 4 * half:4 * half + 4, sl, :]
                tv = P.op("scalar", lambda s, dst=dst, ps=ps: s.activation(
                    out=dst, in_=pB[ps][:, 0:512].rearrange("p (q k) -> p q k", q=4),
                    func=AF.Identity), waits=[tp, t_ph2], inc=s_ac)
                consumed(("pB", ps), tv)
                yield
        p3tok["wo"] = (s_dv, s_dv.n)

        tw_next = dma(xr[0][:], dfw[0:128, :], waits=acquire(("xr", 0)), sem=s_x[0])
        for sl in range(DC):
            xs = sl % 2
            tw = tw_next
            if sl + 1 < DC:
                nxs = (sl + 1) % 2
                tw_next = dma(xr[nxs][:], dfw[128 * (sl + 1):128 * (sl + 2), :],
                              waits=acquire(("xr", nxs)), sem=s_x[nxs])
            s0w = acquire(("sf", 0))
            s1w = acquire(("sf", 1))
            tg1 = vop(lambda v, xs=xs: v.tensor_scalar(
                out=sf[0][:], in0=xr[xs][:], scalar1=bc3[:, 0:1], scalar2=None,
                op0=ALU.is_gt), waits=[tw, t_bc3] + s0w)
            tg2 = vop(lambda v, xs=xs: v.tensor_scalar(
                out=sf[1][:], in0=xr[xs][:], scalar1=bc3[:, 1:2], scalar2=None,
                op0=ALU.is_lt), waits=[tw] + s1w)
            tg3 = vop(lambda v: v.tensor_tensor(sf[0][:], sf[0][:], sf[1][:],
                                                ALU.subtract))
            consumed(("xr", xs), tg2)
            consumed(("xr", xs), tg1)
            consumed(("sf", 1), tg3)
            tplast = None
            for half in range(2):
                ps = half
                tws = acquire(("pB", ps))
                tp = None
                for q in range(4):
                    c = 4 * half + q
                    tp = P.op("tensor", lambda te, c=c, ps=ps, q=q: te.transpose(
                        pB[ps][:, 128 * q:128 * (q + 1)], sf[0][:, 128 * c:128 * (c + 1)],
                        identf[:]), waits=([tg3, t_idf] + tws) if q == 0 else (),
                        inc=s_pe if q == 3 else None)
                tplast = tp
                dst = vft.ap().rearrange("p (c s2 k) -> p c s2 k", c=DC, s2=DC)[:, 4 * half:4 * half + 4, sl, :]
                tv = P.op("scalar", lambda s, dst=dst, ps=ps: s.activation(
                    out=dst, in_=pB[ps][:, 0:512].rearrange("p (q k) -> p q k", q=4),
                    func=AF.Identity, scale=bc3[:, 2:3]), waits=[tp, t_ph2], inc=s_ac)
                consumed(("pB", ps), tv)
                yield
            consumed(("sf", 0), tplast)
        p3tok["ft"] = (s_dv, s_dv.n)



    def run_all(g):
        for _ in g:
            pass

    ROWS_STEPS = 40
    P2_STEPS = 20
    HOLD = 2

    def zip_head(g1r, g1m, g2, g3):
        d1 = g1r is None
        dm = g1m is None
        d2 = g2 is None
        d3 = g3 is None
        rem2 = P2_STEPS if g2 is not None else 0
        credit = 0.0
        rate = (P2_STEPS - HOLD) / float(ROWS_STEPS)
        while not d1:
            if not d2 and rem2 > (HOLD if not dm else 0):
                credit += rate
                while credit >= 1.0 and rem2 > HOLD:
                    if next(g2, StopIteration) is StopIteration:
                        d2 = True
                        break
                    rem2 -= 1
                    credit -= 1.0
            if next(g1r, StopIteration) is StopIteration:
                d1 = True
            if not d3:
                if next(g3, StopIteration) is StopIteration:
                    d3 = True
        while not (dm and d2):
            if not d2:
                if next(g2, StopIteration) is StopIteration:
                    d2 = True
                else:
                    rem2 -= 1
            if not dm:
                if next(g1m, StopIteration) is StopIteration:
                    dm = True
            if not d3:
                if next(g3, StopIteration) is StopIteration:
                    d3 = True

    emit_qkv_pair(0)
    run_all(gen_p1rows(0))
    run_all(gen_p1mls(0))
    t_pair7 = [None]
    p3g = [None]
    for h in range(H):
        nh = h + 1
        if nh < H and nh % 2 == 0:
            emit_qkv_pair(nh // 2)
            if nh // 2 == 7:
                t_pair7[0] = (s_pe, s_pe.n)
                p3g[0] = gen_p3prep()
        zip_head(gen_p1rows(nh) if nh < H else None,
                 gen_p1mls(nh) if nh < H else None,
                 gen_pass2(h), p3g[0])
    if p3g[0] is not None:
        run_all(p3g[0])

    u_done = [uevac[(2 * c + 1, 3)] for c in range(DC)]
    tx_next = dma(xr[0][:], dx[0:128, :], waits=acquire(("xr", 0)), sem=s_x[0])
    for t in range(NT):
        xs = t % 2
        tx = tx_next
        if t + 1 < NT:
            nxs = (t + 1) % 2
            tx_next = dma(xr[nxs][:], dx[128 * (t + 1):128 * (t + 2), :],
                          waits=acquire(("xr", nxs)), sem=s_x[nxs])
        oslot = 0
        otws = acquire(("osb", oslot))
        st_tok = None
        for cc in range(2):
            aslot = cc
            fslot = cc
            atws = acquire(("pB", aslot))
            ta = None
            for c in range(DC):
                ta = P.op("tensor", lambda te, c=c, t=t, cc=cc, ps=aslot: te.matmul(
                    pB[ps][:, 0:512], ot[c][:, 128 * t:128 * (t + 1)],
                    woT[c][:, 512 * cc:512 * (cc + 1)], start=(c == 0), stop=False),
                    waits=([p3tok["wo"]] + u_done + atws) if c == 0 else ())
            ta = P.op("tensor", lambda te, t=t, cc=cc, ps=aslot: te.matmul(
                pB[ps][:, 0:512], onesrow_b[:], brow_c[0:1, 512 * cc:512 * (cc + 1)],
                start=False, stop=True), waits=[p3tok["crow"]], inc=s_pe)
            ftws = acquire(("pA", fslot))
            tf = None
            for c in range(DC):
                tf = P.op("tensor", lambda te, c=c, t=t, cc=cc, ps=fslot: te.matmul(
                    pA[ps][:], xut[c][:, 128 * t:128 * (t + 1)],
                    ftT[c][:, 512 * cc:512 * (cc + 1)], start=(c == 0),
                    stop=(c == DC - 1)),
                    waits=([p3tok["ft"], t_xut] + ftws) if c == 0 else (),
                    inc=s_pe if c == DC - 1 else None)
            tv1 = vop(lambda v, t=t, cc=cc, os=oslot, ps=fslot, xs=xs: v.scalar_tensor_tensor(
                out=osb[os][:, 512 * cc:512 * (cc + 1)], in0=pA[ps][:],
                scalar=ab_all[:, t:t + 1], in1=xr[xs][:, 512 * cc:512 * (cc + 1)],
                op0=ALU.mult, op1=ALU.add),
                waits=[tf, tx, t_ab] + otws + (u_done if t == 0 else []))
            otws = []
            consumed(("pA", fslot), tv1)
            tv2 = vop(lambda v, cc=cc, os=oslot, ps=aslot: v.tensor_tensor(
                osb[os][:, 512 * cc:512 * (cc + 1)], osb[os][:, 512 * cc:512 * (cc + 1)],
                pB[ps][:, 0:512], ALU.add), waits=[ta])
            consumed(("pB", aslot), tv2)
            consumed(("xr", xs), tv1)
            st_tok = tv2
        ts = store(dout[128 * t:128 * (t + 1), :], osb[oslot][:], waits=[st_tok],
                   sem=s_st[oslot])
        consumed(("osb", oslot), ts)

    if _DBG:
        allw = [(s_pe, s_pe.n), (s_dv, s_dv.n), (s_ac, s_ac.n), (s_gp, s_gp.n),
                (s_c, s_c.n), (s_x[0], s_x[0].n), (s_x[1], s_x[1].n),
                (s_sg, s_sg.n), (s_mls, s_mls.n), (s_rb, s_rb.n)]
        tap = {"st_mu": st_mu, "st_s2": st_s2, "st_var": st_var, "b_all": b_all,
               "st_t1": st_t1, "st_t2": st_t2, "st_t3": st_t3, "st_t4": st_t4,
               "wsum1": wsum1,
               "ab_all": ab_all, "nmub": nmub, "bc3": bc3, "wsum": wsum,
               "biasq": biasq, "biask": biask, "biasv": biasv, "brow_c": brow_c,
               "gcol": gcol, "becol": becol, "al_st": al_st}
        for c in range(DC):
            tap[f"xut{c}"] = xut[c]
            tap[f"ot{c}"] = ot[c]
            tap[f"woT{c}"] = woT[c]
            tap[f"ftT{c}"] = ftT[c]
        for i in range(4):
            tap[f"qaug{i}"] = qaug[i]
            tap[f"kaug{i}"] = kaug[i]
        for i in range(2):
            tap[f"vtm{i}"] = vtm[i]
            tap[f"npm{i}"] = npm[i]
            tap[f"nmx16{i}"] = nmx16[i]
        tap["wqT"] = wqT
        tap["wkT"] = wkT
        tap["wvT"] = wvT
        for nm_ in _DBG:
            t = tap[nm_]
            try:
                ap = t.ap()
            except TypeError:
                ap = t
            sh = list(ap.shape)
            dd = nc.dram_tensor(f"dbg_{nm_}", sh, ap.dtype, kind="ExternalOutput").ap()
            P.op("sync", lambda g, dd=dd, ap=ap: g.dma_start(out=dd, in_=ap),
                 waits=allw, inc=s_st, amt=16)

    P.op("sync", lambda g: g.nop(),
         waits=[(s_st[0], s_st[0].n), (s_st[1], s_st[1].n)])

    with ctx:
        with nc.Block() as block:
            @block.sync
            def _(eng):
                P.emit("sync", eng)

            @block.tensor
            def _(eng):
                P.emit("tensor", eng)

            @block.vector
            def _(eng):
                P.emit("vector", eng, self_sem=s_dv)

            @block.scalar
            def _(eng):
                P.emit("scalar", eng, self_sem=s_ac)

            @block.gpsimd
            def _(eng):
                P.emit("gpsimd", eng, self_sem=s_gp)
    return nc


def kernel(L0_attn_x, L0_attn_Wq, L0_attn_bq, L0_attn_Wk, L0_attn_bk,
           L0_attn_Wv, L0_attn_bv, L0_attn_Wo, L0_attn_bo, L0_attn_gamma,
           L0_attn_beta, L0_attn_rel_bias, L0_ffn_weight, L0_ffn_alpha,
           L0_ffn_bias):
    f = lambda a: np.ascontiguousarray(np.asarray(a, dtype=np.float32))
    x = f(L0_attn_x)
    wq, wk, wv, wo, fw = f(L0_attn_Wq), f(L0_attn_Wk), f(L0_attn_Wv), f(L0_attn_Wo), f(L0_ffn_weight)
    bq, bk, bv = f(L0_attn_bq), f(L0_attn_bk), f(L0_attn_bv)
    bo = f(L0_attn_bo).reshape(1, D)
    fb = f(L0_ffn_bias).reshape(1, D)
    ga, be = f(L0_attn_gamma), f(L0_attn_beta)
    al = f(L0_ffn_alpha).reshape(1, 1)
    relT = np.ascontiguousarray(f(L0_attn_rel_bias).transpose(0, 2, 1))

    mqt = _interp_matrix_T()
    ident_bf = np.eye(128, dtype=np.float32).astype(ml_dtypes.bfloat16)
    ident_f32 = np.eye(128, dtype=np.float32)
    mask_tm = np.triu(np.full((128, 128), BIG_NEG, dtype=np.float32), 1).astype(ml_dtypes.bfloat16)
    mm, vv = np.meshgrid(np.arange(128), np.arange(512), indexing="ij")
    mask_kb = np.where(mm + 384 > vv, np.float32(BIG_NEG),
                       np.float32(0.0)).astype(ml_dtypes.bfloat16)

    nc = build_nc()
    base = dict(wq=wq, wk=wk, wv=wv, wo=wo, fw=fw, bq=bq, bk=bk, bv=bv, bo=bo,
                fb=fb, gamma=ga, beta=be, alpha=al, relT=relT, d_mqt=mqt,
                ident_bf=ident_bf, ident_f32=ident_f32, d_mask_tm=mask_tm,
                d_mask_kb=mask_kb)
    in_maps = [dict(base, x=np.ascontiguousarray(x[i])) for i in range(8)]
    res = run_bass_kernel_spmd(nc, in_maps, core_ids=list(range(8)))
    if _DBG:
        globals()["_dbg_res"] = res.results
    return np.stack([r["out"] for r in res.results], axis=0)


# version 38
# speedup vs baseline: 1.0180x; 1.0180x over previous
import numpy as np
import ml_dtypes
import concourse.bass as bass
import concourse.mybir as mybir
from concourse.bass_utils import run_bass_kernel_spmd
from contextlib import ExitStack

F32 = mybir.dt.float32
BF16 = mybir.dt.bfloat16
ALU = mybir.AluOpType
AF = mybir.ActivationFunctionType
AX = mybir.AxisListType.X

N = 2048
D = 1024
H = 16
NT = N // 128
DC = D // 128
BIG_NEG = -1.0e30
EPS = 1.0e-5

ENGINES = ("sync", "tensor", "vector", "scalar", "gpsimd")

_DBG = ()


class _Sem:
    __slots__ = ("h", "n")

    def __init__(self, h):
        self.h = h
        self.n = 0


class _Prog:
    def __init__(self):
        self.stream = {e: [] for e in ENGINES}

    def op(self, eng, fn, waits=(), inc=None, amt=1):
        ws = tuple(w for w in waits if w is not None)
        if inc is not None:
            inc.n += amt
        self.stream[eng].append((ws, fn, inc, amt, inc.n if inc else 0))
        return (inc, inc.n) if inc is not None else None

    def emit(self, eng_name, eng, self_sem=None):
        waited = {}
        prev_cnt = None
        for ws, fn, inc, amt, post in self.stream[eng_name]:
            if self_sem is not None and prev_cnt is not None:
                k = id(self_sem)
                if waited.get(k, -1) < prev_cnt:
                    eng.wait_ge(self_sem.h, prev_cnt)
                    waited[k] = prev_cnt
            for sem, val in ws:
                k = id(sem)
                if waited.get(k, -1) < val:
                    eng.wait_ge(sem.h, val)
                    waited[k] = val
            inst = fn(eng)
            if inc is not None:
                inst.then_inc(inc.h, amt)
                if inc is self_sem:
                    prev_cnt = post


def _interp_matrix_T():
    s = 32
    pos = np.arange(N, dtype=np.float32) * ((s - 1) / (N - 1))
    i0 = np.floor(pos).astype(np.int32)
    i1 = np.minimum(i0 + 1, s - 1)
    frac = (pos - i0.astype(np.float32)).astype(np.float32)
    M = np.zeros((N, s), dtype=np.float32)
    rows = np.arange(N)
    np.add.at(M, (rows, i0), 1.0 - frac)
    np.add.at(M, (rows, i1), frac)
    M[N - 1, :] = 0.0
    return np.ascontiguousarray(M.T).astype(ml_dtypes.bfloat16)


def build_nc():
    nc = bass.Bass("TRN2", target_bir_lowering=False)

    dx = nc.dram_tensor("x", [N, D], F32, kind="ExternalInput").ap()
    dwq = nc.dram_tensor("wq", [D, D], F32, kind="ExternalInput").ap()
    dwk = nc.dram_tensor("wk", [D, D], F32, kind="ExternalInput").ap()
    dwv = nc.dram_tensor("wv", [D, D], F32, kind="ExternalInput").ap()
    dwo = nc.dram_tensor("wo", [D, D], F32, kind="ExternalInput").ap()
    dfw = nc.dram_tensor("fw", [D, D], F32, kind="ExternalInput").ap()
    dbq = nc.dram_tensor("bq", [D], F32, kind="ExternalInput").ap()
    dbk = nc.dram_tensor("bk", [D], F32, kind="ExternalInput").ap()
    dbv = nc.dram_tensor("bv", [D], F32, kind="ExternalInput").ap()
    dbo = nc.dram_tensor("bo", [1, D], F32, kind="ExternalInput").ap()
    dfb = nc.dram_tensor("fb", [1, D], F32, kind="ExternalInput").ap()
    dga = nc.dram_tensor("gamma", [D], F32, kind="ExternalInput").ap()
    dbe = nc.dram_tensor("beta", [D], F32, kind="ExternalInput").ap()
    dal = nc.dram_tensor("alpha", [1, 1], F32, kind="ExternalInput").ap()
    drel = nc.dram_tensor("relT", [H, 32, 32], F32, kind="ExternalInput").ap()
    dmqt = nc.dram_tensor("d_mqt", [32, N], BF16, kind="ExternalInput").ap()
    dident = nc.dram_tensor("ident_bf", [128, 128], BF16, kind="ExternalInput").ap()
    didf = nc.dram_tensor("ident_f32", [128, 128], F32, kind="ExternalInput").ap()
    dmtm = nc.dram_tensor("d_mask_tm", [128, 128], BF16, kind="ExternalInput").ap()
    dmkb = nc.dram_tensor("d_mask_kb", [128, 512], BF16, kind="ExternalInput").ap()
    dout = nc.dram_tensor("out", [N, D], F32, kind="ExternalOutput").ap()

    ctx = ExitStack()
    e = ctx.enter_context

    xutA = e(nc.sbuf_tensor("xutA", [128, DC * N], BF16))
    xut = [xutA.ap()[:, c * N:(c + 1) * N] for c in range(DC)]
    ot = [e(nc.sbuf_tensor(f"ot{c}", [128, N], BF16)) for c in range(DC)]
    mqt = e(nc.sbuf_tensor("mqt", [32, N], BF16))
    ident = e(nc.sbuf_tensor("ident", [128, 128], BF16))
    identf = e(nc.sbuf_tensor("identf", [128, 128], F32))
    mask_tm = e(nc.sbuf_tensor("mask_tm", [128, 128], BF16))
    mask_kb = e(nc.sbuf_tensor("mask_kb", [128, 512], BF16))
    gcol = e(nc.sbuf_tensor("gcol", [128, DC], F32))
    becol = e(nc.sbuf_tensor("becol", [128, DC], F32))
    becol_bf = e(nc.sbuf_tensor("becol_bf", [128, DC], BF16))
    bqcol = e(nc.sbuf_tensor("bqcol", [128, DC], F32))
    bkcol = e(nc.sbuf_tensor("bkcol", [128, DC], F32))
    bvcol = e(nc.sbuf_tensor("bvcol", [128, DC], F32))
    biasq = e(nc.sbuf_tensor("biasq", [128, DC], F32))
    biask = e(nc.sbuf_tensor("biask", [128, DC], F32))
    biasv = e(nc.sbuf_tensor("biasv", [128, DC], F32))
    relb = e(nc.sbuf_tensor("relb", [32, 512], BF16))
    onescol = e(nc.sbuf_tensor("onescol", [128, 1], F32))
    onesrow_b = e(nc.sbuf_tensor("onesrow_b", [1, 128], BF16))
    onesrow_f = e(nc.sbuf_tensor("onesrow_f", [1, 128], F32))
    st_mu = e(nc.sbuf_tensor("st_mu", [128, NT], F32))
    st_s2 = e(nc.sbuf_tensor("st_s2", [128, NT], F32))
    st_var = e(nc.sbuf_tensor("st_var", [128, NT], F32))
    st_t1 = e(nc.sbuf_tensor("st_t1", [128, NT], F32))
    st_t2 = e(nc.sbuf_tensor("st_t2", [128, NT], F32))
    st_t3 = e(nc.sbuf_tensor("st_t3", [128, NT], F32))
    st_t4 = e(nc.sbuf_tensor("st_t4", [128, NT], F32))
    b_all = e(nc.sbuf_tensor("b_all", [128, NT], F32))
    ab_all = e(nc.sbuf_tensor("ab_all", [128, NT], F32))
    nmub = e(nc.sbuf_tensor("nmub", [128, NT], F32))
    npm = [e(nc.sbuf_tensor(f"npm{i}", [128, 64], F32)) for i in range(2)]
    nmx16 = [e(nc.sbuf_tensor(f"nmx16{i}", [128, NT], F32)) for i in range(2)]
    nm_stage = [e(nc.sbuf_tensor(f"nmstage{i}", [16, 128], BF16)) for i in range(2)]
    usb = [e(nc.sbuf_tensor(f"usb{i}", [65, 512], F32)) for i in range(2)]
    zb = e(nc.sbuf_tensor("zb", [65, 512], F32))
    zrowr = zb.ap()[64:65, :]
    zsh = e(nc.sbuf_tensor("zsh", [64, 512], BF16))
    qaug = [e(nc.sbuf_tensor(f"qaug{i}", [98, N], BF16)) for i in range(4)]
    kaug = [e(nc.sbuf_tensor(f"kaug{i}", [98, N], BF16)) for i in range(4)]
    vtm = [e(nc.sbuf_tensor(f"vtm{i}", [128, NT * 130], BF16)) for i in range(2)]
    pt = [e(nc.sbuf_tensor(f"pt{i}", [128, 1024], BF16)) for i in range(2)]
    sf = [e(nc.sbuf_tensor(f"sf{i}", [128, 1024], F32)) for i in range(2)]
    xr = [e(nc.sbuf_tensor(f"xr{i}", [128, 1024], F32)) for i in range(2)]
    osb = [e(nc.sbuf_tensor(f"osb{i}", [128, 1024], F32)) for i in range(1)]
    stage_hi = e(nc.sbuf_tensor("stage_hi", [128, N], BF16))
    brow_bo = e(nc.sbuf_tensor("brow_bo", [1, D], F32))
    brow_fb = e(nc.sbuf_tensor("brow_fb", [1, D], F32))
    brow_c = e(nc.sbuf_tensor("brow_c", [1, D], BF16))
    al_st = e(nc.sbuf_tensor("al_st", [1, 3], F32))
    bc3 = e(nc.sbuf_tensor("bc3", [128, 3], F32))
    wsum = e(nc.sbuf_tensor("wsum", [128, DC], F32))
    wsum1 = e(nc.sbuf_tensor("wsum1", [128, 1], F32))
    wqT = e(nc.sbuf_tensor("wqT", [128, DC * 1024], BF16))
    wkT = e(nc.sbuf_tensor("wkT", [128, DC * 1024], BF16))
    wvT = e(nc.sbuf_tensor("wvT", [128, DC * 1024], BF16))
    vwo = nc.virtual_sbuf_tensor("vwo", [128, DC * 1024], BF16)
    nc.back_virtual_tensor_with(vwo, wqT.tensor if hasattr(wqT, "tensor") else wqT)
    vft = nc.virtual_sbuf_tensor("vft", [128, DC * 1024], BF16)
    nc.back_virtual_tensor_with(vft, wkT.tensor if hasattr(wkT, "tensor") else wkT)
    woT = [vwo.ap()[:, 1024 * c:1024 * (c + 1)] for c in range(DC)]
    ftT = [vft.ap()[:, 1024 * c:1024 * (c + 1)] for c in range(DC)]
    pt.append(osb[0].ap().bitcast(BF16)[:, 0:1024])

    pA = [e(nc.psum_tensor(f"pA{i}", [128, 512], F32)) for i in range(2)]
    pBw = e(nc.psum_tensor("pBw", [128, 2048], F32))
    pB = [pBw.ap()[:, 1024 * i:1024 * (i + 1)] for i in range(2)]
    pU = [e(nc.psum_tensor(f"pU{i}", [65, 512], F32)) for i in range(2)]

    s_c = _Sem(e(nc.semaphore("s_c")))
    s_x = [_Sem(e(nc.semaphore("s_x0"))), _Sem(e(nc.semaphore("s_x1")))]
    s_sg = _Sem(e(nc.semaphore("s_sg")))
    s_mls = _Sem(e(nc.semaphore("s_mls")))
    s_rb = _Sem(e(nc.semaphore("s_rb")))
    s_nz = _Sem(e(nc.semaphore("s_nz")))
    s_w = [_Sem(e(nc.semaphore("s_w0"))), _Sem(e(nc.semaphore("s_w1")))]
    s_st = [_Sem(e(nc.semaphore("s_st0"))), _Sem(e(nc.semaphore("s_st1")))]
    s_pe = _Sem(e(nc.semaphore("s_pe")))
    s_dv = _Sem(e(nc.semaphore("s_dv")))
    s_ac = _Sem(e(nc.semaphore("s_ac")))
    s_gp = _Sem(e(nc.semaphore("s_gp")))

    P = _Prog()

    def dma(dst, src, waits=(), sem=None):
        return P.op("sync", lambda g, d=dst, s=src: g.dma_start(out=d, in_=s),
                    waits=waits, inc=sem, amt=16)

    def store(dst, src, waits=(), sem=None):
        return P.op("sync", lambda g, d=dst, s=src: g.dma_start(out=d, in_=s),
                    waits=waits, inc=sem, amt=16)

    def vop(fn, waits=(), inc=s_dv):
        return P.op("vector", fn, waits=waits, inc=inc)

    free_tok = {}

    def acquire(key):
        toks = free_tok.get(key, ())
        free_tok[key] = []
        return list(toks)

    def consumed(key, tok):
        free_tok.setdefault(key, []).append(tok)

    dma(mqt[:], dmqt, sem=s_c)
    dma(ident[:], dident, sem=s_c)
    dma(identf[:], didf, sem=s_c)
    dma(mask_tm[:], dmtm, sem=s_c)
    dma(mask_kb[:], dmkb, sem=s_c)
    for h in range(H):
        P.op("gpsimd", lambda g, h=h: g.dma_start(out=relb[:, 32 * h:32 * h + 32],
                                                  in_=drel[h]), inc=s_rb, amt=16)
    t_relb = (s_rb, s_rb.n)
    for c in range(DC):
        dma(gcol[:, c:c + 1], dga[c * 128:(c + 1) * 128].rearrange("(p o) -> p o", o=1), sem=s_c)
        dma(becol[:, c:c + 1], dbe[c * 128:(c + 1) * 128].rearrange("(p o) -> p o", o=1), sem=s_c)
        dma(bqcol[:, c:c + 1], dbq[c * 128:(c + 1) * 128].rearrange("(p o) -> p o", o=1), sem=s_c)
        dma(bkcol[:, c:c + 1], dbk[c * 128:(c + 1) * 128].rearrange("(p o) -> p o", o=1), sem=s_c)
        dma(bvcol[:, c:c + 1], dbv[c * 128:(c + 1) * 128].rearrange("(p o) -> p o", o=1), sem=s_c)
    dma(brow_bo[:], dbo, sem=s_c)
    dma(brow_fb[:], dfb, sem=s_c)
    dma(al_st[:, 2:3], dal, sem=s_c)
    for i in range(4):
        dma(qaug[i][64:96, :], dmqt, sem=s_c)
    t_const = (s_c, s_c.n)
    t_mqt = t_id = t_idf = t_mtm = t_mkb = t_cols = t_rows = t_const
    qa_mq = [t_const] * 4

    P.op("vector", lambda v: v.memset(onescol[:], 1.0))
    P.op("vector", lambda v: v.memset(onesrow_b[:], 1.0))
    P.op("vector", lambda v: v.memset(onesrow_f[:], 1.0))
    ka_ones = [P.op("vector", lambda v, i=i: v.memset(kaug[i][96:98, :], 1.0), inc=s_dv)
               for i in range(4)]
    vt_ones = []
    for i in range(2):
        P.op("vector", lambda v, i=i: v.memset(
            vtm[i].ap().rearrange("p (t c) -> p t c", c=130)[:, :, 64:65], 1.0))
        vt_ones.append(P.op("vector", lambda v, i=i: v.memset(
            vtm[i].ap().rearrange("p (t c) -> p t c", c=130)[:, :, 129:130], 1.0),
            inc=s_dv))
    t_beb = vop(lambda v: v.tensor_copy(becol_bf[:], becol[:]), waits=[t_cols])

    def gen_statsA():
        for t in range(NT):
            xs = t % 2
            tx = dma(xr[xs][:], dx[128 * t:128 * (t + 1), :], waits=acquire(("xr", xs)),
                     sem=s_x[xs])
            tr = vop(lambda v, t=t, xs=xs: v.reduce_sum(st_mu[:, t:t + 1], xr[xs][:],
                                                        axis=AX), waits=[tx])
            consumed(("xr", xs), tr)
            ss = t % 2
            tq = P.op("scalar", lambda s, xs=xs, ss=ss, t=t: s.activation(
                out=sf[ss][:], in_=xr[xs][:], func=AF.Square,
                accum_out=st_s2[:, t:t + 1]), waits=[tx] + acquire(("sf", ss)), inc=s_ac)
            consumed(("xr", xs), tq)
            consumed(("sf", ss), tq)
            yield


    wT_all = {"q": wqT, "k": wkT, "v": wvT}
    dw_all = {"q": dwq, "k": dwk, "v": dwv}
    bcol_all = {"q": bqcol, "k": bkcol, "v": bvcol}
    bias_all = {"q": biasq, "k": biask, "v": biasv}
    bias_tok = {}
    ev_tok = {}

    stage_free = {"any": None}
    wst = [stage_hi.ap()[:, 0:1024], stage_hi.ap()[:, 1024:2048]]
    pBbf = [pB[i].bitcast(BF16) for i in range(2)]

    def gen_wprep():
      for wn in ("q", "k", "v"):
        for sl in range(DC):
            xs = sl % 2
            tw = P.op("gpsimd", lambda g, wn=wn, sl=sl, xs=xs: g.dma_start(
                out=wst[xs], in_=dw_all[wn][128 * sl:128 * (sl + 1), :]),
                waits=acquire(("wst", xs)), inc=s_w[xs], amt=16)
            evs = []
            for half in range(2):
                ps = half
                tws = acquire(("pB", ps))
                tp = None
                for q in range(4):
                    c = 4 * half + q
                    tp = P.op("tensor", lambda te, xs=xs, c=c, ps=ps, q=q: te.transpose(
                        pBbf[ps][:, 128 * q:128 * (q + 1)], wst[xs][:, 128 * c:128 * (c + 1)],
                        ident[:]), waits=([tw, t_id] + tws) if q == 0 else (),
                        inc=s_pe if q == 3 else None)
                consumed(("wst", xs), tp)
                tv = P.op("scalar", lambda s, wn=wn, sl=sl, half=half, ps=ps: s.activation(
                    out=wT_all[wn][:, 1024 * sl + 512 * half:1024 * sl + 512 * half + 512],
                    in_=pBbf[ps][:, 0:512], func=AF.Identity), waits=[tp], inc=s_ac)
                consumed(("pB", ps), tv)
                evs.append(tv)
            ev_tok[(wn, sl)] = evs[-1]
            us = 0
            utws = acquire(("pB", us))
            tm = None
            for c in range(DC):
                tm = P.op("tensor", lambda te, wn=wn, sl=sl, c=c, us=us: te.matmul(
                    pB[us][:, 0:1],
                    wT_all[wn][:, 1024 * sl + 128 * c:1024 * sl + 128 * (c + 1)],
                    becol_bf[:, c:c + 1], start=(c == 0), stop=(c == DC - 1)),
                    waits=([evs[-1], t_beb] + utws) if c == 0 else (),
                    inc=s_pe if c == DC - 1 else None)
            if wn == "q":
                tb_ = vop(lambda v, wn=wn, sl=sl, us=us: v.tensor_scalar(
                    out=bias_all[wn][:, sl:sl + 1], in0=pB[us][:, 0:1],
                    scalar1=bcol_all[wn][:, sl:sl + 1], scalar2=0.125,
                    op0=ALU.add, op1=ALU.mult), waits=[tm, t_cols])
            else:
                tb_ = vop(lambda v, wn=wn, sl=sl, us=us: v.tensor_scalar(
                    out=bias_all[wn][:, sl:sl + 1], in0=pB[us][:, 0:1],
                    scalar1=bcol_all[wn][:, sl:sl + 1], scalar2=None,
                    op0=ALU.add), waits=[tm, t_cols])
            consumed(("pB", us), tb_)
            bias_tok[(wn, sl)] = tb_
            yield

    _gw = gen_wprep()
    for _ in gen_statsA():
        next(_gw, None)
    t_statsA_ac = (s_ac, s_ac.n)

    tk = vop(lambda v: v.tensor_scalar(out=st_mu[:], in0=st_mu[:], scalar1=1.0 / D,
                                       scalar2=None, op0=ALU.mult), waits=[t_statsA_ac])
    tk = vop(lambda v: v.tensor_scalar(out=st_s2[:], in0=st_s2[:], scalar1=1.0 / D,
                                       scalar2=None, op0=ALU.mult))
    tk = vop(lambda v: v.tensor_tensor(st_t1[:], st_mu[:], st_mu[:], ALU.mult))
    tk = vop(lambda v: v.tensor_tensor(st_var[:], st_s2[:], st_t1[:], ALU.subtract))
    tk = vop(lambda v: v.tensor_scalar(out=st_t1[:], in0=st_var[:], scalar1=EPS,
                                       scalar2=None, op0=ALU.add))
    tk = vop(lambda v: v.reciprocal(st_t2[:], st_t1[:]))
    tk = vop(lambda v: v.tensor_tensor(st_t3[:], st_var[:], st_t2[:], ALU.mult))
    tk = vop(lambda v: v.tensor_scalar(out=st_t3[:], in0=st_t3[:], scalar1=EPS,
                                       scalar2=None, op0=ALU.add))
    tab = P.op("scalar", lambda s: s.activation(out=ab_all[:], in_=st_t3[:], func=AF.Sqrt),
               waits=[tk], inc=s_ac)
    tk = vop(lambda v: v.reciprocal(st_t4[:], st_t3[:]))
    tk = vop(lambda v: v.tensor_tensor(st_t4[:], st_t2[:], st_t4[:], ALU.mult))
    tb = P.op("scalar", lambda s: s.activation(out=b_all[:], in_=st_t4[:], func=AF.Sqrt),
              waits=[tk], inc=s_ac)

    def newton_sqrt(y, u, tmp1, tmp2, w0):
        t1 = vop(lambda v: v.tensor_tensor(tmp1[:], y[:], y[:], ALU.mult), waits=[w0])
        t1 = vop(lambda v: v.tensor_tensor(tmp1[:], u[:], tmp1[:], ALU.subtract))
        t1 = vop(lambda v: v.reciprocal(tmp2[:], y[:]))
        t1 = vop(lambda v: v.tensor_tensor(tmp1[:], tmp1[:], tmp2[:], ALU.mult))
        t1 = vop(lambda v: v.tensor_scalar(out=tmp1[:], in0=tmp1[:], scalar1=0.5,
                                           scalar2=None, op0=ALU.mult))
        return vop(lambda v: v.tensor_tensor(y[:], y[:], tmp1[:], ALU.add))

    t_ab = newton_sqrt(ab_all, st_t3, st_t1, st_t2, tab)
    t_b = newton_sqrt(b_all, st_t4, st_t1, st_t2, tb)
    tk = vop(lambda v: v.tensor_tensor(nmub[:], st_mu[:], b_all[:], ALU.mult))
    t_nmub = vop(lambda v: v.tensor_scalar(out=nmub[:], in0=nmub[:], scalar1=-1.0,
                                           scalar2=None, op0=ALU.mult))

    def gen_passB():
        for t in range(NT):
            xs = t % 2
            tx = dma(xr[xs][:], dx[128 * t:128 * (t + 1), :], waits=acquire(("xr", xs)),
                     sem=s_x[xs])
            ss = t % 2
            ta = vop(lambda v, t=t, xs=xs, ss=ss: v.tensor_scalar(
                out=sf[ss][:], in0=xr[xs][:], scalar1=b_all[:, t:t + 1],
                scalar2=nmub[:, t:t + 1], op0=ALU.mult, op1=ALU.add),
                waits=[tx, t_b, t_nmub] + acquire(("sf", ss)))
            consumed(("xr", xs), ta)
            for half in range(2):
                ps = half
                tws = acquire(("pB", ps))
                tp = None
                for q in range(4):
                    c = 4 * half + q
                    last = (q == 3)
                    tp = P.op("tensor", lambda te, ss=ss, c=c, ps=ps, q=q: te.transpose(
                        pB[ps][:, 128 * q:128 * (q + 1)], sf[ss][:, 128 * c:128 * (c + 1)],
                        identf[:]), waits=([ta, t_idf] + tws) if q == 0 else (),
                        inc=s_pe if last else None)
                consumed(("sf", ss), tp)
                dst = xutA.ap().rearrange("p (c t2 k) -> p c t2 k", c=DC, t2=NT)[:, 4 * half:4 * half + 4, t, :]
                tv = P.op("scalar", lambda s, dst=dst, ps=ps: s.activation(
                    out=dst, in_=pB[ps][:, 0:512].rearrange("p (q k) -> p q k", q=4),
                    func=AF.Identity), waits=[tp], inc=s_ac)
                consumed(("pB", ps), tv)
            yield

    for _ in gen_passB():
        next(_gw, None)
        next(_gw, None)
    for _ in _gw:
        pass
    stage_free["any"] = (s_pe, s_pe.n)
    t_allbias_pe = (s_pe, s_pe.n)
    t_xut = (s_dv, s_dv.n)

    gscale_tok = {}
    for wn in ("q", "k", "v"):
        for c in range(DC):
            ap = wT_all[wn].ap().rearrange("p (s c2 k) -> p s c2 k", s=DC, c2=DC)[:, :, c, :]
            if wn == "q":
                gscale_tok[(wn, c)] = vop(lambda v, ap=ap, c=c: v.tensor_scalar(
                    out=ap, in0=ap, scalar1=gcol[:, c:c + 1], scalar2=0.125,
                    op0=ALU.mult, op1=ALU.mult), waits=[t_allbias_pe, t_cols])
            else:
                gscale_tok[(wn, c)] = vop(lambda v, ap=ap, c=c: v.tensor_scalar(
                    out=ap, in0=ap, scalar1=gcol[:, c:c + 1], scalar2=None,
                    op0=ALU.mult), waits=[t_allbias_pe, t_cols])

    qaug_ready = [None] * H
    kaug_ready = [None] * H
    kaug_rel = [None] * H
    vtm_ready = [None] * 8
    mls_ready = [None] * H
    pass2_done_pe = [None] * H
    pass1_done_pe = [None] * H
    p1_lastexp = [None] * H
    ucopy = {}
    uevac = {}
    prev_scatter = [None]

    def emit_qkv_pair(j):
        a, b = 2 * j, 2 * j + 1
        sa, sb = a % 4, b % 4
        guard_a = [pass2_done_pe[a - 4], pass1_done_pe[a - 4]] if a >= 4 else []
        guard_b = [pass2_done_pe[b - 4], pass1_done_pe[b - 4]] if b >= 4 else []
        vguard = ([pass2_done_pe[2 * (j - 2)], pass2_done_pe[2 * (j - 2) + 1]]
                  if j >= 2 else [])
        bt_steps = []
        for h, sh, gd in ((a, sa, guard_a), (b, sb, guard_b)):
            for ch in range(4):
                bt_steps.append((h, sh, gd, ch))
        bt_k = [0]

        def emit_bt(n=2):
            for _ in range(n):
                if bt_k[0] >= len(bt_steps):
                    return
                h, sh, gd, ch = bt_steps[bt_k[0]]
                bt_k[0] += 1
                ps = bt_k[0] % 2
                tws = acquire(("pB", ps))
                tm = P.op("tensor", lambda te, h=h, ch=ch, ps=ps: te.matmul(
                    pB[ps][64:96, 0:512], relb[:, 32 * h:32 * h + 32],
                    mqt[:, 512 * ch:512 * (ch + 1)], start=True, stop=True,
                    tile_position=(0, 64)),
                    waits=[t_relb, t_mqt] + tws, inc=s_pe)
                tvl = P.op("scalar", lambda s, sh=sh, ch=ch, ps=ps: s.activation(
                    out=kaug[sh][64:96, 512 * ch:512 * (ch + 1)],
                    in_=pB[ps][64:96, 0:512], func=AF.Identity),
                    waits=[tm] + gd, inc=s_ac)
                consumed(("pB", ps), tvl)
                if ch == 3:
                    kaug_rel[h] = tvl

        emit_bt(2)
        for wn in ("v", "q", "k"):
            emit_bt(2)
            ev_hi = []
            for half in range(2):
                gw = [gscale_tok[(wn, c)] for c in range(DC)] if j == 0 else []
                tms = {}
                for sub in range(2):
                    tc = 2 * half + sub
                    ps = sub
                    tws = acquire(("pA", ps))
                    tm = None
                    for c in range(DC):
                        tm = P.op("tensor", lambda te, wn=wn, j=j, c=c, tc=tc, ps=ps: te.matmul(
                            pA[ps][:],
                            wT_all[wn][:, 1024 * j + 128 * c:1024 * j + 128 * (c + 1)],
                            xut[c][:, 512 * tc:512 * (tc + 1)],
                            start=(c == 0), stop=(c == DC - 1)),
                            waits=(gw + tws + [t_xut]) if c == 0 else (),
                            inc=s_pe if c == DC - 1 else None)
                    tms[sub] = tm
                if wn == "v":
                    ssl = half
                    tvh = {}
                    for sub in range(2):
                        tvh[sub] = P.op("scalar", lambda s, j=j, ps=sub, ssl=ssl, sub=sub: s.activation(
                            out=sf[ssl][:, 512 * sub:512 * sub + 512], in_=pA[ps][:],
                            func=AF.Identity, bias=biasv[:, j:j + 1]),
                            waits=[tms[sub], bias_tok[("v", j)]] +
                                  (acquire(("sf", ssl)) if sub == 0 else []), inc=s_ac)
                        consumed(("pA", sub), tvh[sub])
                    tv = tvh[1]
                    vslot = j % 2
                    tplast = None
                    for g in range(2):
                        pq = g
                        ptws = acquire(("pB", pq))
                        tp = None
                        for q in range(4):
                            qq = 4 * g + q
                            tp = P.op("tensor", lambda te, ssl=ssl, qq=qq, q=q, pq=pq: te.transpose(
                                pB[pq][:, 128 * q:128 * (q + 1)],
                                sf[ssl][:, 128 * qq:128 * (qq + 1)],
                                identf[:]), waits=([tvh[g], t_idf] + ptws) if q == 0 else (),
                                inc=s_pe if q == 3 else None)
                        vdst = vtm[vslot].ap().rearrange("p (t c) -> p t c", c=130)[
                            :, 8 * ssl + 4 * g:8 * ssl + 4 * g + 4, :]
                        psrc = pB[pq][:, 0:512].rearrange("p (q k) -> p q k", q=4)
                        tva = P.op("scalar", lambda s, vdst=vdst, psrc=psrc: s.activation(
                            out=vdst[:, :, 0:64], in_=psrc[:, :, 0:64],
                            func=AF.Identity), waits=[tp] + vguard, inc=s_ac)
                        tvv = P.op("scalar", lambda s, vdst=vdst, psrc=psrc: s.activation(
                            out=vdst[:, :, 65:129], in_=psrc[:, :, 64:128],
                            func=AF.Identity), inc=s_ac)
                        consumed(("pB", pq), tva)
                        consumed(("pB", pq), tvv)
                        tplast = tp
                    consumed(("sf", ssl), tplast)
                    vtm_ready[j] = tvv
                else:
                    dst = qaug[sa] if wn == "q" else kaug[sa]
                    bt = bias_tok[(wn, j)]
                    tv1 = None
                    for sub in range(2):
                        tc = 2 * half + sub
                        tv1 = P.op("scalar", lambda s, dst=dst, tc=tc, ps=sub, wn=wn, j=j: s.activation(
                            out=dst[0:64, 512 * tc:512 * tc + 512], in_=pA[ps][0:64, :],
                            func=AF.Identity, bias=bias_all[wn][0:64, j:j + 1]),
                            waits=[tms[sub], bt] + guard_a, inc=s_ac)
                        tv2 = P.op("scalar", lambda s, tc=tc, ps=sub, wn=wn, j=j: s.activation(
                            out=stage_hi[64:128, 512 * tc:512 * tc + 512],
                            in_=pA[ps][64:128, :],
                            func=AF.Identity, bias=bias_all[wn][64:128, j:j + 1]),
                            waits=[tms[sub], bt, stage_free["any"]], inc=s_ac)
                        consumed(("pA", sub), tv1)
                        consumed(("pA", sub), tv2)
                        ev_hi.append(tv2)
                    if half == 1:
                        if wn == "q":
                            qaug_ready[a] = tv1
                        else:
                            kaug_ready[a] = tv1
            if wn in ("q", "k"):
                dstb = qaug[sb] if wn == "q" else kaug[sb]
                td = P.op("sync", lambda g, dstb=dstb: g.dma_start(
                    out=dstb[0:64, :], in_=stage_hi[64:128, :]),
                    waits=ev_hi + guard_b, inc=s_sg, amt=16)
                stage_free["any"] = td
                if wn == "q":
                    qaug_ready[b] = td
                else:
                    kaug_ready[b] = td
        emit_bt(8)

    def gen_p1rows(h):
        sh = h % 4
        r = h % 2
        prev = mls_ready[h - 2] if h >= 2 else None
        slot_guard = [pass2_done_pe[h - 4]] if h >= 4 else []
        tz = vop(lambda v, r=r: v.memset(npm[r][:], 3.0e38), waits=[prev] if prev else [])
        qr = qaug_ready[h]
        kr = [kaug_ready[h], kaug_rel[h], ka_ones[sh], qa_mq[sh]]
        pc = 0
        for qt in range(NT):
            L = 128 * (qt + 1)
            npieces = (L + 511) // 512
            for p in range(npieces):
                lo = 512 * p
                w = min(512, L - lo)
                ps = pc % 2
                pc += 1
                tws = acquire(("pA", ps))
                has_mask = (lo + w == L)
                ol = L - 128 - lo if has_mask else -1
                tm = P.op("tensor", lambda te, sh=sh, qt=qt, lo=lo, w=w, ps=ps,
                          hm=has_mask: te.matmul(
                    pA[ps][:, 0:w], qaug[sh][0:96, 128 * qt:128 * qt + 128],
                    kaug[sh][0:96, lo:lo + w], start=True, stop=(not hm)),
                    waits=[qr] + kr + tws, inc=None if has_mask else s_pe)
                if has_mask:
                    tm = P.op("tensor", lambda te, ol=ol, ps=ps: te.matmul(
                        pA[ps][:, ol:ol + 128], ident[:], mask_tm[:],
                        start=False, stop=True), waits=[t_id, t_mtm], inc=s_pe)
                col = 4 * qt + p
                tr = vop(lambda v, r=r, col=col, ps=ps, w=w: v.tensor_reduce(
                    npm[r][:, col:col + 1], pA[ps][:, 0:w], axis=AX, op=ALU.max,
                    negate=True), waits=[tm])
                consumed(("pA", ps), tr)
                yield
        pass1_done_pe[h] = (s_pe, s_pe.n)

    def gen_p1mls(h):
        sh = h % 4
        r = h % 2
        slot_guard = [pass2_done_pe[h - 4]] if h >= 4 else []
        t1 = vop(lambda v, r=r: v.tensor_reduce(
            nmx16[r][:], npm[r].ap().rearrange("p (q c) -> p q c", c=4), axis=AX,
            op=ALU.min))
        yield
        ps = 0
        tws = acquire(("pB", ps))
        tt = P.op("tensor", lambda te, r=r, ps=ps: te.transpose(
            pB[ps][0:16, 0:128], nmx16[r][:], identf[:]),
            waits=[t1, t_idf] + tws, inc=s_pe)
        tv = P.op("scalar", lambda s, r=r, ps=ps: s.activation(
            out=nm_stage[r][:], in_=pB[ps][0:16, 0:128], func=AF.Identity),
            waits=[tt], inc=s_ac)
        consumed(("pB", ps), tv)
        yield
        td1 = P.op("sync", lambda g, sh=sh, r=r: g.dma_start(
            out=qaug[sh][96:97, :], in_=nm_stage[r][:]),
            waits=[tv, prev_scatter[0]] + slot_guard, inc=s_mls, amt=16)
        mls_ready[h] = td1
        prev_scatter[0] = td1
        yield

    def gen_pass2(h):
        sh = h % 4
        j = h // 2
        vslot = j % 2
        odd = h % 2
        wc = 0
        from collections import deque
        avq = deque()
        postq = deque()

        def tick():
            for it in postq:
                it[0] -= 1
            while postq and postq[0][0] <= 0:
                postq.popleft()[1]()

        def flushpost():
            while postq:
                postq.popleft()[1]()

        for nj in range(4):
            us = nj % 2
            if nj >= 2:
                utws = [ucopy[(h, nj - 2)]]
            elif h > 0:
                utws = [ucopy[(h - 1, nj + 2)]]
            else:
                utws = []
            for k0 in range(0, 4 * nj + 4, 2):
                ws = wc % 2
                tws = acquire(("pB", ws))
                tmh = {}
                for sub in range(2):
                    kt = k0 + sub
                    diag = (kt >= 4 * nj)
                    tm = P.op("tensor", lambda te, sh=sh, kt=kt, nj=nj, ws=ws, sub=sub,
                              diag=diag: te.matmul(
                        pB[ws][:, 512 * sub:512 * sub + 512],
                        kaug[sh][0:97, 128 * kt:128 * kt + 128],
                        qaug[sh][0:97, 512 * nj:512 * nj + 512], start=True,
                        stop=(not diag)),
                        waits=([mls_ready[h], kaug_ready[h], kaug_rel[h],
                                vtm_ready[j]] + tws) if sub == 0 else (),
                        inc=None if diag else s_pe)
                    if diag:
                        o = 128 * (kt - 4 * nj)
                        tm = P.op("tensor", lambda te, o=o, ws=ws, sub=sub: te.matmul(
                            pB[ws][:, 512 * sub:512 * sub + o + 128],
                            ident[:], mask_kb[:, 384 - o:512],
                            start=False, stop=True), waits=[t_mkb], inc=s_pe)
                    tmh[sub] = tm
                pts = wc % 3
                wc += 1
                ptws = acquire(("pt", pts))
                te_ = P.op("scalar", lambda s, ws=ws, pts=pts: s.activation(
                    out=pt[pts][:], in_=pB[ws][:], func=AF.Exp),
                    waits=[tmh[1]] + ptws, inc=s_ac)
                consumed(("pB", ws), te_)
                if len(avq) >= 2:
                    avq.popleft()()
                avq.append(lambda vs=vslot, h=h, k0=k0, us=us, pts=pts, nj=nj,
                           te_=te_, utws=(utws if k0 == 0 else []), odd=odd:
                           _av2(vs, h, k0, us, pts, nj, te_, utws, odd))
                tick()
                yield
            while avq:
                avq.popleft()()
            _norm_chain(h, nj, us, odd, (s_pe, s_pe.n), postq)
        flushpost()
        pass2_done_pe[h] = (s_pe, s_pe.n)

    def _av2(vs, h, k0, us, pts, nj, te_, utws, odd):
        for sub in range(2):
            kt = k0 + sub
            ta = P.op("tensor", lambda te, vs=vs, kt=kt, us=us, pts=pts, nj=nj,
                      sub=sub, odd=odd: te.matmul(
                pU[us][0:65, :],
                vtm[vs][:, 130 * kt + 65 * odd:130 * kt + 65 * odd + 65],
                pt[pts][:, 512 * sub:512 * sub + 512],
                start=(kt == 0), stop=(kt == 4 * nj + 3)),
                waits=([te_] + list(utws)) if sub == 0 else (),
                inc=s_pe if sub == 1 else None)
            if sub == 1:
                consumed(("pt", pts), ta)

    last_nz = [None]
    last_stt = {0: None, 1: None, "zb": None}

    def _norm_chain(h, nj, us, odd, av_tok, postq):
        while postq:
            postq.popleft()[1]()
        ub = nj % 2
        tcp = P.op("scalar", lambda s, us=us, ub=ub: s.activation(
            out=usb[ub][:], in_=pU[us][0:65, :], func=AF.Identity),
            waits=[av_tok, last_stt[ub]], inc=s_ac)
        ucopy[(h, nj)] = tcp

        def mid(tcp=tcp):
            tl = P.op("scalar", lambda s, ub=ub: s.activation(
                out=zrowr, in_=usb[ub][64:65, :], func=AF.Ln),
                waits=[tcp], inc=s_ac)
            tx = P.op("scalar", lambda s, ub=ub: s.activation(
                out=usb[ub][64:65, :], in_=zrowr, func=AF.Exp, scale=-1.0),
                waits=[tl], inc=s_ac)
            td3 = P.op("sync", lambda g, ub=ub: g.dma_start(
                out=zb[0:64, :],
                in_=usb[ub][64:65, :].unsqueeze(1).broadcast_to([1, 64, 512])),
                waits=[tx, last_stt["zb"], last_nz[0]], inc=s_nz, amt=16)
            postq.append([2, lambda td3=td3: fin(td3)])

        def fin(td3):
            ub2 = nj % 2
            if odd == 0:
                tst = P.op("gpsimd", lambda g, h=h, nj=nj, ub2=ub2: g.tensor_tensor(
                    ot[h // 2][0:64, 512 * nj:512 * nj + 512],
                    usb[ub2][0:64, :], zb[0:64, :], ALU.mult), waits=[td3], inc=s_gp)
                uevac[(h, nj)] = tst
                last_nz[0] = td3
                last_stt[nj % 2] = tst
                last_stt["zb"] = tst
            else:
                tst = P.op("gpsimd", lambda g, ub2=ub2: g.tensor_tensor(
                    zsh[:], usb[ub2][0:64, :], zb[0:64, :], ALU.mult),
                    waits=[td3, last_nz[0]], inc=s_gp)
                tsh = P.op("sync", lambda g, h=h, nj=nj: g.dma_start(
                    out=ot[h // 2][64:128, 512 * nj:512 * nj + 512], in_=zsh[:]),
                    waits=[tst], inc=s_nz, amt=16)
                uevac[(h, nj)] = tsh
                last_nz[0] = tsh
                last_stt[nj % 2] = tst
                last_stt["zb"] = tst

        postq.append([1, mid])

    p3tok = {}
    p3tok = {}

    def gen_p3a():
        for sl in range(DC):
            xs = sl % 2
            tw = dma(xr[xs][:], dfw[128 * sl:128 * (sl + 1), :], waits=acquire(("xr", xs)),
                     sem=s_x[xs])
            tr = vop(lambda v, sl=sl, xs=xs: v.tensor_reduce(
                wsum[:, sl:sl + 1], xr[xs][:], axis=AX, op=ALU.add,
                apply_absolute_value=True), waits=[tw])
            consumed(("xr", xs), tr)
            yield
        t1 = vop(lambda v: v.tensor_reduce(wsum1[:], wsum[:], axis=AX, op=ALU.add))
        yield
        yield
        yield
        bs = 0
        btws = acquire(("pB", bs))
        tm = P.op("tensor", lambda te, bs=bs: te.matmul(
            pB[bs][0:1, 0:1], wsum1[:], onescol[:], start=True, stop=True),
            waits=[t1] + btws, inc=s_pe)
        t2 = vop(lambda v, bs=bs: v.tensor_scalar(
            out=al_st[:, 0:1], in0=pB[bs][0:1, 0:1], scalar1=0.7 / float(D * D),
            scalar2=None, op0=ALU.mult), waits=[tm])
        consumed(("pB", bs), t2)
        t3 = vop(lambda v: v.tensor_scalar(out=al_st[:, 1:2], in0=al_st[:, 0:1],
                                           scalar1=-1.0, scalar2=None, op0=ALU.mult),
                 waits=[t_rows])
        yield
        bs = 1
        btws = acquire(("pB", bs))
        tm = P.op("tensor", lambda te, bs=bs: te.matmul(
            pB[bs][:, 0:3], onesrow_f[:], al_st[:], start=True, stop=True),
            waits=[t3] + btws, inc=s_pe)
        t_bc3 = vop(lambda v, bs=bs: v.tensor_copy(bc3[:], pB[bs][:, 0:3]), waits=[tm])
        consumed(("pB", bs), t_bc3)
        p3tok["bc3"] = t_bc3
        yield

    def gen_p3b():
        t_bc3 = p3tok["bc3"]
        tcb = vop(lambda v: v.tensor_tensor(brow_bo[:], brow_bo[:], brow_fb[:], ALU.add),
                  waits=[t_rows])
        p3tok["crow"] = vop(lambda v: v.tensor_copy(brow_c[:], brow_bo[:]))

        t_ph2 = t_pair7[0]
        tw_next = dma(xr[0][:], dwo[0:128, :], waits=acquire(("xr", 0)), sem=s_x[0])
        for sl in range(DC):
            xs = sl % 2
            tw = tw_next
            if sl + 1 < DC:
                nxs = (sl + 1) % 2
                tw_next = dma(xr[nxs][:], dwo[128 * (sl + 1):128 * (sl + 2), :],
                              waits=acquire(("xr", nxs)), sem=s_x[nxs])
            for half in range(2):
                ps = half
                tws = acquire(("pB", ps))
                tp = None
                for q in range(4):
                    c = 4 * half + q
                    tp = P.op("tensor", lambda te, xs=xs, c=c, ps=ps, q=q: te.transpose(
                        pB[ps][:, 128 * q:128 * (q + 1)], xr[xs][:, 128 * c:128 * (c + 1)],
                        identf[:]), waits=([tw, t_idf] + tws) if q == 0 else (),
                        inc=s_pe if q == 3 else None)
                consumed(("xr", xs), tp)
                dst = vwo.ap().rearrange("p (c s2 k) -> p c s2 k", c=DC, s2=DC)[:, 4 * half:4 * half + 4, sl, :]
                tv = P.op("scalar", lambda s, dst=dst, ps=ps: s.activation(
                    out=dst, in_=pB[ps][:, 0:512].rearrange("p (q k) -> p q k", q=4),
                    func=AF.Identity), waits=[tp, t_ph2], inc=s_ac)
                consumed(("pB", ps), tv)
                yield
        p3tok["wo"] = (s_dv, s_dv.n)

        tw_next = dma(xr[0][:], dfw[0:128, :], waits=acquire(("xr", 0)), sem=s_x[0])
        for sl in range(DC):
            xs = sl % 2
            tw = tw_next
            if sl + 1 < DC:
                nxs = (sl + 1) % 2
                tw_next = dma(xr[nxs][:], dfw[128 * (sl + 1):128 * (sl + 2), :],
                              waits=acquire(("xr", nxs)), sem=s_x[nxs])
            s0w = acquire(("sf", 0))
            s1w = acquire(("sf", 1))
            tg1 = vop(lambda v, xs=xs: v.tensor_scalar(
                out=sf[0][:], in0=xr[xs][:], scalar1=bc3[:, 0:1], scalar2=None,
                op0=ALU.is_gt), waits=[tw, t_bc3] + s0w)
            tg2 = vop(lambda v, xs=xs: v.tensor_scalar(
                out=sf[1][:], in0=xr[xs][:], scalar1=bc3[:, 1:2], scalar2=None,
                op0=ALU.is_lt), waits=[tw] + s1w)
            tg3 = vop(lambda v: v.tensor_tensor(sf[0][:], sf[0][:], sf[1][:],
                                                ALU.subtract))
            consumed(("xr", xs), tg2)
            consumed(("xr", xs), tg1)
            consumed(("sf", 1), tg3)
            tplast = None
            for half in range(2):
                ps = half
                tws = acquire(("pB", ps))
                tp = None
                for q in range(4):
                    c = 4 * half + q
                    tp = P.op("tensor", lambda te, c=c, ps=ps, q=q: te.transpose(
                        pB[ps][:, 128 * q:128 * (q + 1)], sf[0][:, 128 * c:128 * (c + 1)],
                        identf[:]), waits=([tg3, t_idf] + tws) if q == 0 else (),
                        inc=s_pe if q == 3 else None)
                tplast = tp
                dst = vft.ap().rearrange("p (c s2 k) -> p c s2 k", c=DC, s2=DC)[:, 4 * half:4 * half + 4, sl, :]
                tv = P.op("scalar", lambda s, dst=dst, ps=ps: s.activation(
                    out=dst, in_=pB[ps][:, 0:512].rearrange("p (q k) -> p q k", q=4),
                    func=AF.Identity, scale=bc3[:, 2:3]), waits=[tp, t_ph2], inc=s_ac)
                consumed(("pB", ps), tv)
                yield
            consumed(("sf", 0), tplast)
        p3tok["ft"] = (s_dv, s_dv.n)



    def run_all(g):
        for _ in g:
            pass

    ROWS_STEPS = 40
    P2_STEPS = 20
    HOLD = 4

    def zip_head(g1r, g1m, g2, g3):
        d1 = g1r is None
        dm = g1m is None
        d2 = g2 is None
        d3 = g3 is None
        rem2 = P2_STEPS if g2 is not None else 0
        credit = 0.0
        rate = (P2_STEPS - HOLD) / float(ROWS_STEPS)
        while not d1:
            if not d2 and rem2 > (HOLD if not dm else 0):
                credit += rate
                while credit >= 1.0 and rem2 > HOLD:
                    if next(g2, StopIteration) is StopIteration:
                        d2 = True
                        break
                    rem2 -= 1
                    credit -= 1.0
            if next(g1r, StopIteration) is StopIteration:
                d1 = True
            if not d3:
                if next(g3, StopIteration) is StopIteration:
                    d3 = True
        while not (dm and d2):
            if not d2:
                if next(g2, StopIteration) is StopIteration:
                    d2 = True
                else:
                    rem2 -= 1
            if not dm:
                if next(g1m, StopIteration) is StopIteration:
                    dm = True
            if not d3:
                if next(g3, StopIteration) is StopIteration:
                    d3 = True

    emit_qkv_pair(0)
    run_all(gen_p1rows(0))
    run_all(gen_p1mls(0))
    t_pair7 = [None]
    p3g = [None]

    def chain_gens(g1, g2):
        if g1 is not None:
            for _ in g1:
                yield
        for _ in g2:
            yield

    for h in range(H):
        nh = h + 1
        if nh < H and nh % 2 == 0:
            emit_qkv_pair(nh // 2)
            if nh // 2 == 6:
                p3g[0] = gen_p3a()
            if nh // 2 == 7:
                t_pair7[0] = (s_pe, s_pe.n)
                p3g[0] = chain_gens(p3g[0], gen_p3b())
        zip_head(gen_p1rows(nh) if nh < H else None,
                 gen_p1mls(nh) if nh < H else None,
                 gen_pass2(h), p3g[0])
    if p3g[0] is not None:
        run_all(p3g[0])

    u_done = [uevac[(2 * c + 1, 3)] for c in range(DC)]
    tx_next = dma(xr[0][:], dx[0:128, :], waits=acquire(("xr", 0)), sem=s_x[0])
    for t in range(NT):
        xs = t % 2
        tx = tx_next
        if t + 1 < NT:
            nxs = (t + 1) % 2
            tx_next = dma(xr[nxs][:], dx[128 * (t + 1):128 * (t + 2), :],
                          waits=acquire(("xr", nxs)), sem=s_x[nxs])
        oslot = 0
        otws = acquire(("osb", oslot))
        st_tok = None
        for cc in range(2):
            aslot = cc
            fslot = cc
            atws = acquire(("pB", aslot))
            ta = None
            for c in range(DC):
                ta = P.op("tensor", lambda te, c=c, t=t, cc=cc, ps=aslot: te.matmul(
                    pB[ps][:, 0:512], ot[c][:, 128 * t:128 * (t + 1)],
                    woT[c][:, 512 * cc:512 * (cc + 1)], start=(c == 0), stop=False),
                    waits=([p3tok["wo"]] + u_done + atws) if c == 0 else ())
            ta = P.op("tensor", lambda te, t=t, cc=cc, ps=aslot: te.matmul(
                pB[ps][:, 0:512], onesrow_b[:], brow_c[0:1, 512 * cc:512 * (cc + 1)],
                start=False, stop=True), waits=[p3tok["crow"]], inc=s_pe)
            ftws = acquire(("pA", fslot))
            tf = None
            for c in range(DC):
                tf = P.op("tensor", lambda te, c=c, t=t, cc=cc, ps=fslot: te.matmul(
                    pA[ps][:], xut[c][:, 128 * t:128 * (t + 1)],
                    ftT[c][:, 512 * cc:512 * (cc + 1)], start=(c == 0),
                    stop=(c == DC - 1)),
                    waits=([p3tok["ft"], t_xut] + ftws) if c == 0 else (),
                    inc=s_pe if c == DC - 1 else None)
            tv1 = vop(lambda v, t=t, cc=cc, os=oslot, ps=fslot, xs=xs: v.scalar_tensor_tensor(
                out=osb[os][:, 512 * cc:512 * (cc + 1)], in0=pA[ps][:],
                scalar=ab_all[:, t:t + 1], in1=xr[xs][:, 512 * cc:512 * (cc + 1)],
                op0=ALU.mult, op1=ALU.add),
                waits=[tf, tx, t_ab] + otws + (u_done if t == 0 else []))
            otws = []
            consumed(("pA", fslot), tv1)
            tv2 = vop(lambda v, cc=cc, os=oslot, ps=aslot: v.tensor_tensor(
                osb[os][:, 512 * cc:512 * (cc + 1)], osb[os][:, 512 * cc:512 * (cc + 1)],
                pB[ps][:, 0:512], ALU.add), waits=[ta])
            consumed(("pB", aslot), tv2)
            consumed(("xr", xs), tv1)
            st_tok = tv2
        ts = store(dout[128 * t:128 * (t + 1), :], osb[oslot][:], waits=[st_tok],
                   sem=s_st[oslot])
        consumed(("osb", oslot), ts)

    if _DBG:
        allw = [(s_pe, s_pe.n), (s_dv, s_dv.n), (s_ac, s_ac.n), (s_gp, s_gp.n),
                (s_c, s_c.n), (s_x[0], s_x[0].n), (s_x[1], s_x[1].n),
                (s_sg, s_sg.n), (s_mls, s_mls.n), (s_rb, s_rb.n)]
        tap = {"st_mu": st_mu, "st_s2": st_s2, "st_var": st_var, "b_all": b_all,
               "st_t1": st_t1, "st_t2": st_t2, "st_t3": st_t3, "st_t4": st_t4,
               "wsum1": wsum1,
               "ab_all": ab_all, "nmub": nmub, "bc3": bc3, "wsum": wsum,
               "biasq": biasq, "biask": biask, "biasv": biasv, "brow_c": brow_c,
               "gcol": gcol, "becol": becol, "al_st": al_st}
        for c in range(DC):
            tap[f"xut{c}"] = xut[c]
            tap[f"ot{c}"] = ot[c]
            tap[f"woT{c}"] = woT[c]
            tap[f"ftT{c}"] = ftT[c]
        for i in range(4):
            tap[f"qaug{i}"] = qaug[i]
            tap[f"kaug{i}"] = kaug[i]
        for i in range(2):
            tap[f"vtm{i}"] = vtm[i]
            tap[f"npm{i}"] = npm[i]
            tap[f"nmx16{i}"] = nmx16[i]
        tap["wqT"] = wqT
        tap["wkT"] = wkT
        tap["wvT"] = wvT
        for nm_ in _DBG:
            t = tap[nm_]
            try:
                ap = t.ap()
            except TypeError:
                ap = t
            sh = list(ap.shape)
            dd = nc.dram_tensor(f"dbg_{nm_}", sh, ap.dtype, kind="ExternalOutput").ap()
            P.op("sync", lambda g, dd=dd, ap=ap: g.dma_start(out=dd, in_=ap),
                 waits=allw, inc=s_st, amt=16)

    P.op("sync", lambda g: g.nop(),
         waits=[(s_st[0], s_st[0].n), (s_st[1], s_st[1].n)])

    with ctx:
        with nc.Block() as block:
            @block.sync
            def _(eng):
                P.emit("sync", eng)

            @block.tensor
            def _(eng):
                P.emit("tensor", eng)

            @block.vector
            def _(eng):
                P.emit("vector", eng, self_sem=s_dv)

            @block.scalar
            def _(eng):
                P.emit("scalar", eng, self_sem=s_ac)

            @block.gpsimd
            def _(eng):
                P.emit("gpsimd", eng, self_sem=s_gp)
    return nc


def kernel(L0_attn_x, L0_attn_Wq, L0_attn_bq, L0_attn_Wk, L0_attn_bk,
           L0_attn_Wv, L0_attn_bv, L0_attn_Wo, L0_attn_bo, L0_attn_gamma,
           L0_attn_beta, L0_attn_rel_bias, L0_ffn_weight, L0_ffn_alpha,
           L0_ffn_bias):
    f = lambda a: np.ascontiguousarray(np.asarray(a, dtype=np.float32))
    x = f(L0_attn_x)
    wq, wk, wv, wo, fw = f(L0_attn_Wq), f(L0_attn_Wk), f(L0_attn_Wv), f(L0_attn_Wo), f(L0_ffn_weight)
    bq, bk, bv = f(L0_attn_bq), f(L0_attn_bk), f(L0_attn_bv)
    bo = f(L0_attn_bo).reshape(1, D)
    fb = f(L0_ffn_bias).reshape(1, D)
    ga, be = f(L0_attn_gamma), f(L0_attn_beta)
    al = f(L0_ffn_alpha).reshape(1, 1)
    relT = np.ascontiguousarray(f(L0_attn_rel_bias).transpose(0, 2, 1))

    mqt = _interp_matrix_T()
    ident_bf = np.eye(128, dtype=np.float32).astype(ml_dtypes.bfloat16)
    ident_f32 = np.eye(128, dtype=np.float32)
    mask_tm = np.triu(np.full((128, 128), BIG_NEG, dtype=np.float32), 1).astype(ml_dtypes.bfloat16)
    mm, vv = np.meshgrid(np.arange(128), np.arange(512), indexing="ij")
    mask_kb = np.where(mm + 384 > vv, np.float32(BIG_NEG),
                       np.float32(0.0)).astype(ml_dtypes.bfloat16)

    nc = build_nc()
    base = dict(wq=wq, wk=wk, wv=wv, wo=wo, fw=fw, bq=bq, bk=bk, bv=bv, bo=bo,
                fb=fb, gamma=ga, beta=be, alpha=al, relT=relT, d_mqt=mqt,
                ident_bf=ident_bf, ident_f32=ident_f32, d_mask_tm=mask_tm,
                d_mask_kb=mask_kb)
    in_maps = [dict(base, x=np.ascontiguousarray(x[i])) for i in range(8)]
    res = run_bass_kernel_spmd(nc, in_maps, core_ids=list(range(8)))
    if _DBG:
        globals()["_dbg_res"] = res.results
    return np.stack([r["out"] for r in res.results], axis=0)


# version 39
# speedup vs baseline: 1.0572x; 1.0385x over previous
import numpy as np
import ml_dtypes
import concourse.bass as bass
import concourse.mybir as mybir
from concourse.bass_utils import run_bass_kernel_spmd
from contextlib import ExitStack

F32 = mybir.dt.float32
BF16 = mybir.dt.bfloat16
ALU = mybir.AluOpType
AF = mybir.ActivationFunctionType
AX = mybir.AxisListType.X

N = 2048
D = 1024
H = 16
NT = N // 128
DC = D // 128
BIG_NEG = -1.0e30
EPS = 1.0e-5

ENGINES = ("sync", "tensor", "vector", "scalar", "gpsimd")

_DBG = ()


class _Sem:
    __slots__ = ("h", "n")

    def __init__(self, h):
        self.h = h
        self.n = 0


class _Prog:
    def __init__(self):
        self.stream = {e: [] for e in ENGINES}

    def op(self, eng, fn, waits=(), inc=None, amt=1):
        ws = tuple(w for w in waits if w is not None)
        if inc is not None:
            inc.n += amt
        self.stream[eng].append((ws, fn, inc, amt, inc.n if inc else 0))
        return (inc, inc.n) if inc is not None else None

    def emit(self, eng_name, eng, self_sem=None):
        waited = {}
        prev_cnt = None
        for ws, fn, inc, amt, post in self.stream[eng_name]:
            if self_sem is not None and prev_cnt is not None:
                k = id(self_sem)
                if waited.get(k, -1) < prev_cnt:
                    eng.wait_ge(self_sem.h, prev_cnt)
                    waited[k] = prev_cnt
            for sem, val in ws:
                k = id(sem)
                if waited.get(k, -1) < val:
                    eng.wait_ge(sem.h, val)
                    waited[k] = val
            inst = fn(eng)
            if inc is not None:
                inst.then_inc(inc.h, amt)
                if inc is self_sem:
                    prev_cnt = post


def _interp_matrix_T():
    s = 32
    pos = np.arange(N, dtype=np.float32) * ((s - 1) / (N - 1))
    i0 = np.floor(pos).astype(np.int32)
    i1 = np.minimum(i0 + 1, s - 1)
    frac = (pos - i0.astype(np.float32)).astype(np.float32)
    M = np.zeros((N, s), dtype=np.float32)
    rows = np.arange(N)
    np.add.at(M, (rows, i0), 1.0 - frac)
    np.add.at(M, (rows, i1), frac)
    M[N - 1, :] = 0.0
    return np.ascontiguousarray(M.T).astype(ml_dtypes.bfloat16)


def build_nc():
    nc = bass.Bass("TRN2", target_bir_lowering=False)

    dx = nc.dram_tensor("x", [N, D], F32, kind="ExternalInput").ap()
    dwq = nc.dram_tensor("wq", [D, D], F32, kind="ExternalInput").ap()
    dwk = nc.dram_tensor("wk", [D, D], F32, kind="ExternalInput").ap()
    dwv = nc.dram_tensor("wv", [D, D], F32, kind="ExternalInput").ap()
    dwo = nc.dram_tensor("wo", [D, D], F32, kind="ExternalInput").ap()
    dfw = nc.dram_tensor("fw", [D, D], F32, kind="ExternalInput").ap()
    dbq = nc.dram_tensor("bq", [D], F32, kind="ExternalInput").ap()
    dbk = nc.dram_tensor("bk", [D], F32, kind="ExternalInput").ap()
    dbv = nc.dram_tensor("bv", [D], F32, kind="ExternalInput").ap()
    dbo = nc.dram_tensor("bo", [1, D], F32, kind="ExternalInput").ap()
    dfb = nc.dram_tensor("fb", [1, D], F32, kind="ExternalInput").ap()
    dga = nc.dram_tensor("gamma", [D], F32, kind="ExternalInput").ap()
    dbe = nc.dram_tensor("beta", [D], F32, kind="ExternalInput").ap()
    dal = nc.dram_tensor("alpha", [1, 1], F32, kind="ExternalInput").ap()
    drel = nc.dram_tensor("relT", [H, 32, 32], F32, kind="ExternalInput").ap()
    dmqt = nc.dram_tensor("d_mqt", [32, N], BF16, kind="ExternalInput").ap()
    dident = nc.dram_tensor("ident_bf", [128, 128], BF16, kind="ExternalInput").ap()
    didf = nc.dram_tensor("ident_f32", [128, 128], F32, kind="ExternalInput").ap()
    dmtm = nc.dram_tensor("d_mask_tm", [128, 128], BF16, kind="ExternalInput").ap()
    dmkb = nc.dram_tensor("d_mask_kb", [128, 512], BF16, kind="ExternalInput").ap()
    dout = nc.dram_tensor("out", [N, D], F32, kind="ExternalOutput").ap()

    ctx = ExitStack()
    e = ctx.enter_context

    xutA = e(nc.sbuf_tensor("xutA", [128, DC * N], BF16))
    xut = [xutA.ap()[:, c * N:(c + 1) * N] for c in range(DC)]
    ot = [e(nc.sbuf_tensor(f"ot{c}", [128, N], BF16)) for c in range(DC)]
    mqt = e(nc.sbuf_tensor("mqt", [32, N], BF16))
    ident = e(nc.sbuf_tensor("ident", [128, 128], BF16))
    identf = e(nc.sbuf_tensor("identf", [128, 128], F32))
    mask_tm = e(nc.sbuf_tensor("mask_tm", [128, 128], BF16))
    mask_kb = e(nc.sbuf_tensor("mask_kb", [128, 512], BF16))
    gcol = e(nc.sbuf_tensor("gcol", [128, DC], F32))
    becol = e(nc.sbuf_tensor("becol", [128, DC], F32))
    becol_bf = e(nc.sbuf_tensor("becol_bf", [128, DC], BF16))
    bqcol = e(nc.sbuf_tensor("bqcol", [128, DC], F32))
    bkcol = e(nc.sbuf_tensor("bkcol", [128, DC], F32))
    bvcol = e(nc.sbuf_tensor("bvcol", [128, DC], F32))
    biasq = e(nc.sbuf_tensor("biasq", [128, DC], F32))
    biask = e(nc.sbuf_tensor("biask", [128, DC], F32))
    biasv = e(nc.sbuf_tensor("biasv", [128, DC], F32))
    relb = e(nc.sbuf_tensor("relb", [32, 512], BF16))
    onescol = e(nc.sbuf_tensor("onescol", [128, 1], F32))
    onesrow_b = e(nc.sbuf_tensor("onesrow_b", [1, 128], BF16))
    onesrow_f = e(nc.sbuf_tensor("onesrow_f", [1, 128], F32))
    st_mu = e(nc.sbuf_tensor("st_mu", [128, NT], F32))
    st_s2 = e(nc.sbuf_tensor("st_s2", [128, NT], F32))
    st_var = e(nc.sbuf_tensor("st_var", [128, NT], F32))
    st_t1 = e(nc.sbuf_tensor("st_t1", [128, NT], F32))
    st_t2 = e(nc.sbuf_tensor("st_t2", [128, NT], F32))
    st_t3 = e(nc.sbuf_tensor("st_t3", [128, NT], F32))
    st_t4 = e(nc.sbuf_tensor("st_t4", [128, NT], F32))
    b_all = e(nc.sbuf_tensor("b_all", [128, NT], F32))
    ab_all = e(nc.sbuf_tensor("ab_all", [128, NT], F32))
    nmub = e(nc.sbuf_tensor("nmub", [128, NT], F32))
    npm = [e(nc.sbuf_tensor(f"npm{i}", [128, 64], F32)) for i in range(2)]
    nmx16 = [e(nc.sbuf_tensor(f"nmx16{i}", [128, NT], F32)) for i in range(2)]
    nm_stage = [e(nc.sbuf_tensor(f"nmstage{i}", [16, 128], BF16)) for i in range(2)]
    usb = [e(nc.sbuf_tensor(f"usb{i}", [65, 512], F32)) for i in range(2)]
    zb = e(nc.sbuf_tensor("zb", [65, 512], F32))
    zrowr = zb.ap()[64:65, :]
    zsh = e(nc.sbuf_tensor("zsh", [64, 512], BF16))
    qaug = [e(nc.sbuf_tensor(f"qaug{i}", [98, N], BF16)) for i in range(4)]
    kaug = [e(nc.sbuf_tensor(f"kaug{i}", [98, N], BF16)) for i in range(4)]
    vtm = [e(nc.sbuf_tensor(f"vtm{i}", [128, NT * 130], BF16)) for i in range(2)]
    pt = [e(nc.sbuf_tensor(f"pt{i}", [128, 1024], BF16)) for i in range(2)]
    sf = [e(nc.sbuf_tensor(f"sf{i}", [128, 1024], F32)) for i in range(2)]
    xr = [e(nc.sbuf_tensor(f"xr{i}", [128, 1024], F32)) for i in range(2)]
    osb = [e(nc.sbuf_tensor(f"osb{i}", [128, 1024], F32)) for i in range(1)]
    stage_hi = e(nc.sbuf_tensor("stage_hi", [128, N], BF16))
    brow_bo = e(nc.sbuf_tensor("brow_bo", [1, D], F32))
    brow_fb = e(nc.sbuf_tensor("brow_fb", [1, D], F32))
    brow_c = e(nc.sbuf_tensor("brow_c", [1, D], BF16))
    al_st = e(nc.sbuf_tensor("al_st", [1, 3], F32))
    bc3 = e(nc.sbuf_tensor("bc3", [128, 3], F32))
    wsum = e(nc.sbuf_tensor("wsum", [128, DC], F32))
    wsum1 = e(nc.sbuf_tensor("wsum1", [128, 1], F32))
    wqT = e(nc.sbuf_tensor("wqT", [128, DC * 1024], BF16))
    wkT = e(nc.sbuf_tensor("wkT", [128, DC * 1024], BF16))
    wvT = e(nc.sbuf_tensor("wvT", [128, DC * 1024], BF16))
    vwo = nc.virtual_sbuf_tensor("vwo", [128, DC * 1024], BF16)
    nc.back_virtual_tensor_with(vwo, wqT.tensor if hasattr(wqT, "tensor") else wqT)
    vft = nc.virtual_sbuf_tensor("vft", [128, DC * 1024], BF16)
    nc.back_virtual_tensor_with(vft, wkT.tensor if hasattr(wkT, "tensor") else wkT)
    woT = [vwo.ap()[:, 1024 * c:1024 * (c + 1)] for c in range(DC)]
    ftT = [vft.ap()[:, 1024 * c:1024 * (c + 1)] for c in range(DC)]
    pt.append(osb[0].ap().bitcast(BF16)[:, 0:1024])

    pA = [e(nc.psum_tensor(f"pA{i}", [128, 512], F32)) for i in range(2)]
    pBw = e(nc.psum_tensor("pBw", [128, 2048], F32))
    pB = [pBw.ap()[:, 1024 * i:1024 * (i + 1)] for i in range(2)]
    pU = [e(nc.psum_tensor(f"pU{i}", [65, 512], F32)) for i in range(2)]

    s_c = _Sem(e(nc.semaphore("s_c")))
    s_x = [_Sem(e(nc.semaphore("s_x0"))), _Sem(e(nc.semaphore("s_x1")))]
    s_sg = _Sem(e(nc.semaphore("s_sg")))
    s_mls = _Sem(e(nc.semaphore("s_mls")))
    s_rb = _Sem(e(nc.semaphore("s_rb")))
    s_nz = _Sem(e(nc.semaphore("s_nz")))
    s_w = [_Sem(e(nc.semaphore("s_w0"))), _Sem(e(nc.semaphore("s_w1")))]
    s_st = [_Sem(e(nc.semaphore("s_st0"))), _Sem(e(nc.semaphore("s_st1")))]
    s_pe = _Sem(e(nc.semaphore("s_pe")))
    s_dv = _Sem(e(nc.semaphore("s_dv")))
    s_ac = _Sem(e(nc.semaphore("s_ac")))
    s_gp = _Sem(e(nc.semaphore("s_gp")))

    P = _Prog()

    def dma(dst, src, waits=(), sem=None):
        return P.op("sync", lambda g, d=dst, s=src: g.dma_start(out=d, in_=s),
                    waits=waits, inc=sem, amt=16)

    def store(dst, src, waits=(), sem=None):
        return P.op("sync", lambda g, d=dst, s=src: g.dma_start(out=d, in_=s),
                    waits=waits, inc=sem, amt=16)

    def vop(fn, waits=(), inc=s_dv):
        return P.op("vector", fn, waits=waits, inc=inc)

    free_tok = {}

    def acquire(key):
        toks = free_tok.get(key, ())
        free_tok[key] = []
        return list(toks)

    def consumed(key, tok):
        free_tok.setdefault(key, []).append(tok)

    dma(mqt[:], dmqt, sem=s_c)
    dma(ident[:], dident, sem=s_c)
    dma(identf[:], didf, sem=s_c)
    dma(mask_tm[:], dmtm, sem=s_c)
    dma(mask_kb[:], dmkb, sem=s_c)
    for h in range(H):
        P.op("gpsimd", lambda g, h=h: g.dma_start(out=relb[:, 32 * h:32 * h + 32],
                                                  in_=drel[h]), inc=s_rb, amt=16)
    t_relb = (s_rb, s_rb.n)
    for c in range(DC):
        dma(gcol[:, c:c + 1], dga[c * 128:(c + 1) * 128].rearrange("(p o) -> p o", o=1), sem=s_c)
        dma(becol[:, c:c + 1], dbe[c * 128:(c + 1) * 128].rearrange("(p o) -> p o", o=1), sem=s_c)
        dma(bqcol[:, c:c + 1], dbq[c * 128:(c + 1) * 128].rearrange("(p o) -> p o", o=1), sem=s_c)
        dma(bkcol[:, c:c + 1], dbk[c * 128:(c + 1) * 128].rearrange("(p o) -> p o", o=1), sem=s_c)
        dma(bvcol[:, c:c + 1], dbv[c * 128:(c + 1) * 128].rearrange("(p o) -> p o", o=1), sem=s_c)
    dma(brow_bo[:], dbo, sem=s_c)
    dma(brow_fb[:], dfb, sem=s_c)
    dma(al_st[:, 2:3], dal, sem=s_c)
    for i in range(4):
        dma(qaug[i][64:96, :], dmqt, sem=s_c)
    t_const = (s_c, s_c.n)
    t_mqt = t_id = t_idf = t_mtm = t_mkb = t_cols = t_rows = t_const
    qa_mq = [t_const] * 4

    P.op("vector", lambda v: v.memset(onescol[:], 1.0))
    P.op("vector", lambda v: v.memset(onesrow_b[:], 1.0))
    P.op("vector", lambda v: v.memset(onesrow_f[:], 1.0))
    ka_ones = [P.op("vector", lambda v, i=i: v.memset(kaug[i][96:98, :], 1.0), inc=s_dv)
               for i in range(4)]
    vt_ones = []
    for i in range(2):
        P.op("vector", lambda v, i=i: v.memset(
            vtm[i].ap().rearrange("p (t c) -> p t c", c=130)[:, :, 64:65], 1.0))
        vt_ones.append(P.op("vector", lambda v, i=i: v.memset(
            vtm[i].ap().rearrange("p (t c) -> p t c", c=130)[:, :, 129:130], 1.0),
            inc=s_dv))
    t_beb = vop(lambda v: v.tensor_copy(becol_bf[:], becol[:]), waits=[t_cols])

    def gen_statsA():
        for t in range(NT):
            xs = t % 2
            tx = dma(xr[xs][:], dx[128 * t:128 * (t + 1), :], waits=acquire(("xr", xs)),
                     sem=s_x[xs])
            tr = vop(lambda v, t=t, xs=xs: v.reduce_sum(st_mu[:, t:t + 1], xr[xs][:],
                                                        axis=AX), waits=[tx])
            consumed(("xr", xs), tr)
            ss = t % 2
            tq = P.op("scalar", lambda s, xs=xs, ss=ss, t=t: s.activation(
                out=sf[ss][:], in_=xr[xs][:], func=AF.Square,
                accum_out=st_s2[:, t:t + 1]), waits=[tx] + acquire(("sf", ss)), inc=s_ac)
            consumed(("xr", xs), tq)
            consumed(("sf", ss), tq)
            yield


    wT_all = {"q": wqT, "k": wkT, "v": wvT}
    dw_all = {"q": dwq, "k": dwk, "v": dwv}
    bcol_all = {"q": bqcol, "k": bkcol, "v": bvcol}
    bias_all = {"q": biasq, "k": biask, "v": biasv}
    bias_tok = {}
    ev_tok = {}

    stage_free = {"any": None}
    wst = [stage_hi.ap()[:, 0:1024], stage_hi.ap()[:, 1024:2048]]
    pBbf = [pB[i].bitcast(BF16) for i in range(2)]

    def gen_wprep():
      for wn in ("q", "k", "v"):
        for sl in range(DC):
            xs = sl % 2
            tw = P.op("gpsimd", lambda g, wn=wn, sl=sl, xs=xs: g.dma_start(
                out=wst[xs], in_=dw_all[wn][128 * sl:128 * (sl + 1), :]),
                waits=acquire(("wst", xs)), inc=s_w[xs], amt=16)
            evs = []
            for half in range(2):
                ps = half
                tws = acquire(("pB", ps))
                tp = None
                for q in range(4):
                    c = 4 * half + q
                    tp = P.op("tensor", lambda te, xs=xs, c=c, ps=ps, q=q: te.transpose(
                        pBbf[ps][:, 128 * q:128 * (q + 1)], wst[xs][:, 128 * c:128 * (c + 1)],
                        ident[:]), waits=([tw, t_id] + tws) if q == 0 else (),
                        inc=s_pe if q == 3 else None)
                consumed(("wst", xs), tp)
                tv = P.op("scalar", lambda s, wn=wn, sl=sl, half=half, ps=ps: s.activation(
                    out=wT_all[wn][:, 1024 * sl + 512 * half:1024 * sl + 512 * half + 512],
                    in_=pBbf[ps][:, 0:512], func=AF.Identity), waits=[tp], inc=s_ac)
                consumed(("pB", ps), tv)
                evs.append(tv)
            ev_tok[(wn, sl)] = evs[-1]
            us = 0
            utws = acquire(("pB", us))
            tm = None
            for c in range(DC):
                tm = P.op("tensor", lambda te, wn=wn, sl=sl, c=c, us=us: te.matmul(
                    pB[us][:, 0:1],
                    wT_all[wn][:, 1024 * sl + 128 * c:1024 * sl + 128 * (c + 1)],
                    becol_bf[:, c:c + 1], start=(c == 0), stop=(c == DC - 1)),
                    waits=([evs[-1], t_beb] + utws) if c == 0 else (),
                    inc=s_pe if c == DC - 1 else None)
            if wn == "q":
                tb_ = vop(lambda v, wn=wn, sl=sl, us=us: v.tensor_scalar(
                    out=bias_all[wn][:, sl:sl + 1], in0=pB[us][:, 0:1],
                    scalar1=bcol_all[wn][:, sl:sl + 1], scalar2=0.125,
                    op0=ALU.add, op1=ALU.mult), waits=[tm, t_cols])
            else:
                tb_ = vop(lambda v, wn=wn, sl=sl, us=us: v.tensor_scalar(
                    out=bias_all[wn][:, sl:sl + 1], in0=pB[us][:, 0:1],
                    scalar1=bcol_all[wn][:, sl:sl + 1], scalar2=None,
                    op0=ALU.add), waits=[tm, t_cols])
            consumed(("pB", us), tb_)
            bias_tok[(wn, sl)] = tb_
            yield

    _gw = gen_wprep()
    for _ in gen_statsA():
        next(_gw, None)
    t_statsA_ac = (s_ac, s_ac.n)

    tk = vop(lambda v: v.tensor_scalar(out=st_mu[:], in0=st_mu[:], scalar1=1.0 / D,
                                       scalar2=None, op0=ALU.mult), waits=[t_statsA_ac])
    tk = vop(lambda v: v.tensor_scalar(out=st_s2[:], in0=st_s2[:], scalar1=1.0 / D,
                                       scalar2=None, op0=ALU.mult))
    tk = vop(lambda v: v.tensor_tensor(st_t1[:], st_mu[:], st_mu[:], ALU.mult))
    tk = vop(lambda v: v.tensor_tensor(st_var[:], st_s2[:], st_t1[:], ALU.subtract))
    tk = vop(lambda v: v.tensor_scalar(out=st_t1[:], in0=st_var[:], scalar1=EPS,
                                       scalar2=None, op0=ALU.add))
    tk = vop(lambda v: v.reciprocal(st_t2[:], st_t1[:]))
    tk = vop(lambda v: v.tensor_tensor(st_t3[:], st_var[:], st_t2[:], ALU.mult))
    tk = vop(lambda v: v.tensor_scalar(out=st_t3[:], in0=st_t3[:], scalar1=EPS,
                                       scalar2=None, op0=ALU.add))
    tab = P.op("scalar", lambda s: s.activation(out=ab_all[:], in_=st_t3[:], func=AF.Sqrt),
               waits=[tk], inc=s_ac)
    tk = vop(lambda v: v.reciprocal(st_t4[:], st_t3[:]))
    tk = vop(lambda v: v.tensor_tensor(st_t4[:], st_t2[:], st_t4[:], ALU.mult))
    tb = P.op("scalar", lambda s: s.activation(out=b_all[:], in_=st_t4[:], func=AF.Sqrt),
              waits=[tk], inc=s_ac)

    def newton_sqrt(y, u, tmp1, tmp2, w0):
        t1 = vop(lambda v: v.tensor_tensor(tmp1[:], y[:], y[:], ALU.mult), waits=[w0])
        t1 = vop(lambda v: v.tensor_tensor(tmp1[:], u[:], tmp1[:], ALU.subtract))
        t1 = vop(lambda v: v.reciprocal(tmp2[:], y[:]))
        t1 = vop(lambda v: v.tensor_tensor(tmp1[:], tmp1[:], tmp2[:], ALU.mult))
        t1 = vop(lambda v: v.tensor_scalar(out=tmp1[:], in0=tmp1[:], scalar1=0.5,
                                           scalar2=None, op0=ALU.mult))
        return vop(lambda v: v.tensor_tensor(y[:], y[:], tmp1[:], ALU.add))

    t_ab = newton_sqrt(ab_all, st_t3, st_t1, st_t2, tab)
    t_b = newton_sqrt(b_all, st_t4, st_t1, st_t2, tb)
    tk = vop(lambda v: v.tensor_tensor(nmub[:], st_mu[:], b_all[:], ALU.mult))
    t_nmub = vop(lambda v: v.tensor_scalar(out=nmub[:], in0=nmub[:], scalar1=-1.0,
                                           scalar2=None, op0=ALU.mult))

    def gen_passB():
        for t in range(NT):
            xs = t % 2
            tx = dma(xr[xs][:], dx[128 * t:128 * (t + 1), :], waits=acquire(("xr", xs)),
                     sem=s_x[xs])
            ss = t % 2
            ta = vop(lambda v, t=t, xs=xs, ss=ss: v.tensor_scalar(
                out=sf[ss][:], in0=xr[xs][:], scalar1=b_all[:, t:t + 1],
                scalar2=nmub[:, t:t + 1], op0=ALU.mult, op1=ALU.add),
                waits=[tx, t_b, t_nmub] + acquire(("sf", ss)))
            consumed(("xr", xs), ta)
            for half in range(2):
                ps = half
                tws = acquire(("pB", ps))
                tp = None
                for q in range(4):
                    c = 4 * half + q
                    last = (q == 3)
                    tp = P.op("tensor", lambda te, ss=ss, c=c, ps=ps, q=q: te.transpose(
                        pB[ps][:, 128 * q:128 * (q + 1)], sf[ss][:, 128 * c:128 * (c + 1)],
                        identf[:]), waits=([ta, t_idf] + tws) if q == 0 else (),
                        inc=s_pe if last else None)
                consumed(("sf", ss), tp)
                dst = xutA.ap().rearrange("p (c t2 k) -> p c t2 k", c=DC, t2=NT)[:, 4 * half:4 * half + 4, t, :]
                tv = P.op("scalar", lambda s, dst=dst, ps=ps: s.activation(
                    out=dst, in_=pB[ps][:, 0:512].rearrange("p (q k) -> p q k", q=4),
                    func=AF.Identity), waits=[tp], inc=s_ac)
                consumed(("pB", ps), tv)
            yield

    for _ in gen_passB():
        next(_gw, None)
        next(_gw, None)
    for _ in _gw:
        pass
    stage_free["any"] = (s_pe, s_pe.n)
    t_allbias_pe = (s_pe, s_pe.n)
    t_xut = (s_dv, s_dv.n)

    gscale_tok = {}
    for wn in ("q", "k", "v"):
        for c in range(DC):
            ap = wT_all[wn].ap().rearrange("p (s c2 k) -> p s c2 k", s=DC, c2=DC)[:, :, c, :]
            if wn == "q":
                gscale_tok[(wn, c)] = vop(lambda v, ap=ap, c=c: v.tensor_scalar(
                    out=ap, in0=ap, scalar1=gcol[:, c:c + 1], scalar2=0.125,
                    op0=ALU.mult, op1=ALU.mult), waits=[t_allbias_pe, t_cols])
            else:
                gscale_tok[(wn, c)] = vop(lambda v, ap=ap, c=c: v.tensor_scalar(
                    out=ap, in0=ap, scalar1=gcol[:, c:c + 1], scalar2=None,
                    op0=ALU.mult), waits=[t_allbias_pe, t_cols])

    qaug_ready = [None] * H
    kaug_ready = [None] * H
    kaug_rel = [None] * H
    vtm_ready = [None] * 8
    mls_ready = [None] * H
    pass2_done_pe = [None] * H
    pass1_done_pe = [None] * H
    p1_lastexp = [None] * H
    ucopy = {}
    uevac = {}
    prev_scatter = [None]

    def emit_qkv_pair(j):
        a, b = 2 * j, 2 * j + 1
        sa, sb = a % 4, b % 4
        guard_a = [pass2_done_pe[a - 4], pass1_done_pe[a - 4]] if a >= 4 else []
        guard_b = [pass2_done_pe[b - 4], pass1_done_pe[b - 4]] if b >= 4 else []
        vguard = ([pass2_done_pe[2 * (j - 2)], pass2_done_pe[2 * (j - 2) + 1]]
                  if j >= 2 else [])
        for h, sh, gd in ((a, sa, guard_a), (b, sb, guard_b)):
            tvl = None
            for ch in range(4):
                ps = ch % 2
                tws = acquire(("pB", ps))
                tm = P.op("tensor", lambda te, h=h, ch=ch, ps=ps: te.matmul(
                    pB[ps][64:96, 0:512], relb[:, 32 * h:32 * h + 32],
                    mqt[:, 512 * ch:512 * (ch + 1)], start=True, stop=True,
                    tile_position=(0, 64)),
                    waits=[t_relb, t_mqt] + tws, inc=s_pe)
                tvl = P.op("scalar", lambda s, sh=sh, ch=ch, ps=ps: s.activation(
                    out=kaug[sh][64:96, 512 * ch:512 * (ch + 1)],
                    in_=pB[ps][64:96, 0:512], func=AF.Identity),
                    waits=[tm] + gd, inc=s_ac)
                consumed(("pB", ps), tvl)
            kaug_rel[h] = tvl

        for wn in ("v", "q", "k"):
            ev_hi = []
            for half in range(2):
                gw = [gscale_tok[(wn, c)] for c in range(DC)] if j == 0 else []
                tms = {}
                for sub in range(2):
                    tc = 2 * half + sub
                    ps = sub
                    tws = acquire(("pA", ps))
                    tm = None
                    for c in range(DC):
                        tm = P.op("tensor", lambda te, wn=wn, j=j, c=c, tc=tc, ps=ps: te.matmul(
                            pA[ps][:],
                            wT_all[wn][:, 1024 * j + 128 * c:1024 * j + 128 * (c + 1)],
                            xut[c][:, 512 * tc:512 * (tc + 1)],
                            start=(c == 0), stop=(c == DC - 1)),
                            waits=(gw + tws + [t_xut]) if c == 0 else (),
                            inc=s_pe if c == DC - 1 else None)
                    tms[sub] = tm
                if wn == "v":
                    ssl = half
                    tvh = {}
                    for sub in range(2):
                        tvh[sub] = P.op("scalar", lambda s, j=j, ps=sub, ssl=ssl, sub=sub: s.activation(
                            out=sf[ssl][:, 512 * sub:512 * sub + 512], in_=pA[ps][:],
                            func=AF.Identity, bias=biasv[:, j:j + 1]),
                            waits=[tms[sub], bias_tok[("v", j)]] +
                                  (acquire(("sf", ssl)) if sub == 0 else []), inc=s_ac)
                        consumed(("pA", sub), tvh[sub])
                    tv = tvh[1]
                    vslot = j % 2
                    tplast = None
                    for g in range(2):
                        pq = g
                        ptws = acquire(("pB", pq))
                        tp = None
                        for q in range(4):
                            qq = 4 * g + q
                            tp = P.op("tensor", lambda te, ssl=ssl, qq=qq, q=q, pq=pq: te.transpose(
                                pB[pq][:, 128 * q:128 * (q + 1)],
                                sf[ssl][:, 128 * qq:128 * (qq + 1)],
                                identf[:]), waits=([tvh[g], t_idf] + ptws) if q == 0 else (),
                                inc=s_pe if q == 3 else None)
                        vdst = vtm[vslot].ap().rearrange("p (t c) -> p t c", c=130)[
                            :, 8 * ssl + 4 * g:8 * ssl + 4 * g + 4, :]
                        psrc = pB[pq][:, 0:512].rearrange("p (q k) -> p q k", q=4)
                        tva = P.op("scalar", lambda s, vdst=vdst, psrc=psrc: s.activation(
                            out=vdst[:, :, 0:64], in_=psrc[:, :, 0:64],
                            func=AF.Identity), waits=[tp] + vguard, inc=s_ac)
                        tvv = P.op("scalar", lambda s, vdst=vdst, psrc=psrc: s.activation(
                            out=vdst[:, :, 65:129], in_=psrc[:, :, 64:128],
                            func=AF.Identity), inc=s_ac)
                        consumed(("pB", pq), tva)
                        consumed(("pB", pq), tvv)
                        tplast = tp
                    consumed(("sf", ssl), tplast)
                    vtm_ready[j] = tvv
                else:
                    dst = qaug[sa] if wn == "q" else kaug[sa]
                    bt = bias_tok[(wn, j)]
                    tv1 = None
                    for sub in range(2):
                        tc = 2 * half + sub
                        tv1 = P.op("scalar", lambda s, dst=dst, tc=tc, ps=sub, wn=wn, j=j: s.activation(
                            out=dst[0:64, 512 * tc:512 * tc + 512], in_=pA[ps][0:64, :],
                            func=AF.Identity, bias=bias_all[wn][0:64, j:j + 1]),
                            waits=[tms[sub], bt] + guard_a, inc=s_ac)
                        tv2 = P.op("scalar", lambda s, tc=tc, ps=sub, wn=wn, j=j: s.activation(
                            out=stage_hi[64:128, 512 * tc:512 * tc + 512],
                            in_=pA[ps][64:128, :],
                            func=AF.Identity, bias=bias_all[wn][64:128, j:j + 1]),
                            waits=[tms[sub], bt, stage_free["any"]], inc=s_ac)
                        consumed(("pA", sub), tv1)
                        consumed(("pA", sub), tv2)
                        ev_hi.append(tv2)
                    if half == 1:
                        if wn == "q":
                            qaug_ready[a] = tv1
                        else:
                            kaug_ready[a] = tv1
            if wn in ("q", "k"):
                dstb = qaug[sb] if wn == "q" else kaug[sb]
                td = P.op("sync", lambda g, dstb=dstb: g.dma_start(
                    out=dstb[0:64, :], in_=stage_hi[64:128, :]),
                    waits=ev_hi + guard_b, inc=s_sg, amt=16)
                stage_free["any"] = td
                if wn == "q":
                    qaug_ready[b] = td
                else:
                    kaug_ready[b] = td

    def gen_p1rows(h):
        sh = h % 4
        r = h % 2
        prev = mls_ready[h - 2] if h >= 2 else None
        slot_guard = [pass2_done_pe[h - 4]] if h >= 4 else []
        tz = vop(lambda v, r=r: v.memset(npm[r][:], 3.0e38), waits=[prev] if prev else [])
        qr = qaug_ready[h]
        kr = [kaug_ready[h], kaug_rel[h], ka_ones[sh], qa_mq[sh]]
        pc = 0
        for qt in range(NT):
            L = 128 * (qt + 1)
            npieces = (L + 511) // 512
            for p in range(npieces):
                lo = 512 * p
                w = min(512, L - lo)
                ps = pc % 2
                pc += 1
                tws = acquire(("pA", ps))
                has_mask = (lo + w == L)
                ol = L - 128 - lo if has_mask else -1
                tm = P.op("tensor", lambda te, sh=sh, qt=qt, lo=lo, w=w, ps=ps,
                          hm=has_mask: te.matmul(
                    pA[ps][:, 0:w], qaug[sh][0:96, 128 * qt:128 * qt + 128],
                    kaug[sh][0:96, lo:lo + w], start=True, stop=(not hm)),
                    waits=[qr] + kr + tws, inc=None if has_mask else s_pe)
                if has_mask:
                    tm = P.op("tensor", lambda te, ol=ol, ps=ps: te.matmul(
                        pA[ps][:, ol:ol + 128], ident[:], mask_tm[:],
                        start=False, stop=True), waits=[t_id, t_mtm], inc=s_pe)
                col = 4 * qt + p
                tr = vop(lambda v, r=r, col=col, ps=ps, w=w: v.tensor_reduce(
                    npm[r][:, col:col + 1], pA[ps][:, 0:w], axis=AX, op=ALU.max,
                    negate=True), waits=[tm])
                consumed(("pA", ps), tr)
                yield
        pass1_done_pe[h] = (s_pe, s_pe.n)

    def gen_p1mls(h):
        sh = h % 4
        r = h % 2
        slot_guard = [pass2_done_pe[h - 4]] if h >= 4 else []
        t1 = vop(lambda v, r=r: v.tensor_reduce(
            nmx16[r][:], npm[r].ap().rearrange("p (q c) -> p q c", c=4), axis=AX,
            op=ALU.min))
        yield
        ps = 0
        tws = acquire(("pB", ps))
        tt = P.op("tensor", lambda te, r=r, ps=ps: te.transpose(
            pB[ps][0:16, 0:128], nmx16[r][:], identf[:]),
            waits=[t1, t_idf] + tws, inc=s_pe)
        tv = P.op("scalar", lambda s, r=r, ps=ps: s.activation(
            out=nm_stage[r][:], in_=pB[ps][0:16, 0:128], func=AF.Identity),
            waits=[tt], inc=s_ac)
        consumed(("pB", ps), tv)
        yield
        td1 = P.op("sync", lambda g, sh=sh, r=r: g.dma_start(
            out=qaug[sh][96:97, :], in_=nm_stage[r][:]),
            waits=[tv, prev_scatter[0]] + slot_guard, inc=s_mls, amt=16)
        mls_ready[h] = td1
        prev_scatter[0] = td1
        yield

    def gen_pass2(h):
        sh = h % 4
        j = h // 2
        vslot = j % 2
        odd = h % 2
        wc = 0
        from collections import deque
        avq = deque()
        postq = deque()

        def tick():
            for it in postq:
                it[0] -= 1
            while postq and postq[0][0] <= 0:
                postq.popleft()[1]()

        def flushpost():
            while postq:
                postq.popleft()[1]()

        for nj in range(4):
            us = nj % 2
            if nj >= 2:
                utws = [ucopy[(h, nj - 2)]]
            elif h > 0:
                utws = [ucopy[(h - 1, nj + 2)]]
            else:
                utws = []
            for k0 in range(0, 4 * nj + 4, 2):
                ws = wc % 2
                tws = acquire(("pB", ws))
                tmh = {}
                offs = {}
                for sub in range(2):
                    kt = k0 + sub
                    diag = (kt >= 4 * nj)
                    o = 128 * (kt - 4 * nj) if diag else 0
                    offs[sub] = o
                    tm = P.op("tensor", lambda te, sh=sh, kt=kt, nj=nj, ws=ws, sub=sub,
                              o=o, diag=diag: te.matmul(
                        pB[ws][:, 512 * sub + o:512 * sub + 512],
                        kaug[sh][0:97, 128 * kt:128 * kt + 128],
                        qaug[sh][0:97, 512 * nj + o:512 * nj + 512], start=True,
                        stop=(not diag)),
                        waits=([mls_ready[h], kaug_ready[h], kaug_rel[h],
                                vtm_ready[j]] + tws) if sub == 0 else (),
                        inc=None if diag else s_pe)
                    if diag:
                        tm = P.op("tensor", lambda te, o=o, ws=ws, sub=sub: te.matmul(
                            pB[ws][:, 512 * sub + o:512 * sub + o + 128],
                            ident[:], mask_kb[:, 384:512],
                            start=False, stop=True), waits=[t_mkb], inc=s_pe)
                    tmh[sub] = tm
                pts = wc % 3
                wc += 1
                ptws = acquire(("pt", pts))
                if offs[0] == 0 and offs[1] == 0:
                    te_ = P.op("scalar", lambda s, ws=ws, pts=pts: s.activation(
                        out=pt[pts][:], in_=pB[ws][:], func=AF.Exp),
                        waits=[tmh[1]] + ptws, inc=s_ac)
                    teh = {0: te_, 1: te_}
                else:
                    teh = {}
                    for sub in range(2):
                        o = offs[sub]
                        teh[sub] = P.op("scalar", lambda s, ws=ws, pts=pts, sub=sub,
                                        o=o: s.activation(
                            out=pt[pts][:, 512 * sub + o:512 * sub + 512],
                            in_=pB[ws][:, 512 * sub + o:512 * sub + 512], func=AF.Exp),
                            waits=[tmh[sub]] + (ptws if sub == 0 else []), inc=s_ac)
                    te_ = teh[1]
                consumed(("pB", ws), te_)
                if len(avq) >= 2:
                    avq.popleft()()
                avq.append(lambda vs=vslot, h=h, k0=k0, us=us, pts=pts, nj=nj,
                           teh=dict(teh), offs=dict(offs),
                           utws=(utws if k0 == 0 else []), odd=odd:
                           _av2(vs, h, k0, us, pts, nj, teh, offs, utws, odd))
                tick()
                yield
            while avq:
                avq.popleft()()
            _norm_chain(h, nj, us, odd, (s_pe, s_pe.n), postq)
        flushpost()
        pass2_done_pe[h] = (s_pe, s_pe.n)

    def _av2(vs, h, k0, us, pts, nj, teh, offs, utws, odd):
        for sub in range(2):
            kt = k0 + sub
            o = offs[sub]
            ta = P.op("tensor", lambda te, vs=vs, kt=kt, us=us, pts=pts, nj=nj,
                      sub=sub, o=o, odd=odd: te.matmul(
                pU[us][0:65, o:512],
                vtm[vs][:, 130 * kt + 65 * odd:130 * kt + 65 * odd + 65],
                pt[pts][:, 512 * sub + o:512 * sub + 512],
                start=(kt == 0), stop=(kt == 4 * nj + 3)),
                waits=[teh[sub]] + (list(utws) if sub == 0 else []),
                inc=s_pe if sub == 1 else None)
            if sub == 1:
                consumed(("pt", pts), ta)

    last_nz = [None]
    last_stt = {0: None, 1: None, "zb": None}

    def _norm_chain(h, nj, us, odd, av_tok, postq):
        while postq:
            postq.popleft()[1]()
        ub = nj % 2
        tcp = P.op("scalar", lambda s, us=us, ub=ub: s.activation(
            out=usb[ub][:], in_=pU[us][0:65, :], func=AF.Identity),
            waits=[av_tok, last_stt[ub]], inc=s_ac)
        ucopy[(h, nj)] = tcp

        def mid(tcp=tcp):
            tl = P.op("scalar", lambda s, ub=ub: s.activation(
                out=zrowr, in_=usb[ub][64:65, :], func=AF.Ln),
                waits=[tcp], inc=s_ac)
            tx = P.op("scalar", lambda s, ub=ub: s.activation(
                out=usb[ub][64:65, :], in_=zrowr, func=AF.Exp, scale=-1.0),
                waits=[tl], inc=s_ac)
            td3 = P.op("sync", lambda g, ub=ub: g.dma_start(
                out=zb[0:64, :],
                in_=usb[ub][64:65, :].unsqueeze(1).broadcast_to([1, 64, 512])),
                waits=[tx, last_stt["zb"], last_nz[0]], inc=s_nz, amt=16)
            postq.append([2, lambda td3=td3: fin(td3)])

        def fin(td3):
            ub2 = nj % 2
            if odd == 0:
                tst = P.op("gpsimd", lambda g, h=h, nj=nj, ub2=ub2: g.tensor_tensor(
                    ot[h // 2][0:64, 512 * nj:512 * nj + 512],
                    usb[ub2][0:64, :], zb[0:64, :], ALU.mult), waits=[td3], inc=s_gp)
                uevac[(h, nj)] = tst
                last_nz[0] = td3
                last_stt[nj % 2] = tst
                last_stt["zb"] = tst
            else:
                tst = P.op("gpsimd", lambda g, ub2=ub2: g.tensor_tensor(
                    zsh[:], usb[ub2][0:64, :], zb[0:64, :], ALU.mult),
                    waits=[td3, last_nz[0]], inc=s_gp)
                tsh = P.op("sync", lambda g, h=h, nj=nj: g.dma_start(
                    out=ot[h // 2][64:128, 512 * nj:512 * nj + 512], in_=zsh[:]),
                    waits=[tst], inc=s_nz, amt=16)
                uevac[(h, nj)] = tsh
                last_nz[0] = tsh
                last_stt[nj % 2] = tst
                last_stt["zb"] = tst

        postq.append([1, mid])

    p3tok = {}
    p3tok = {}

    def gen_p3prep():
        for sl in range(DC):
            xs = sl % 2
            tw = dma(xr[xs][:], dfw[128 * sl:128 * (sl + 1), :], waits=acquire(("xr", xs)),
                     sem=s_x[xs])
            tr = vop(lambda v, sl=sl, xs=xs: v.tensor_reduce(
                wsum[:, sl:sl + 1], xr[xs][:], axis=AX, op=ALU.add,
                apply_absolute_value=True), waits=[tw])
            consumed(("xr", xs), tr)
            yield
        t1 = vop(lambda v: v.tensor_reduce(wsum1[:], wsum[:], axis=AX, op=ALU.add))
        yield
        bs = 0
        btws = acquire(("pB", bs))
        tm = P.op("tensor", lambda te, bs=bs: te.matmul(
            pB[bs][0:1, 0:1], wsum1[:], onescol[:], start=True, stop=True),
            waits=[t1] + btws, inc=s_pe)
        t2 = vop(lambda v, bs=bs: v.tensor_scalar(
            out=al_st[:, 0:1], in0=pB[bs][0:1, 0:1], scalar1=0.7 / float(D * D),
            scalar2=None, op0=ALU.mult), waits=[tm])
        consumed(("pB", bs), t2)
        t3 = vop(lambda v: v.tensor_scalar(out=al_st[:, 1:2], in0=al_st[:, 0:1],
                                           scalar1=-1.0, scalar2=None, op0=ALU.mult),
                 waits=[t_rows])
        yield
        bs = 1
        btws = acquire(("pB", bs))
        tm = P.op("tensor", lambda te, bs=bs: te.matmul(
            pB[bs][:, 0:3], onesrow_f[:], al_st[:], start=True, stop=True),
            waits=[t3] + btws, inc=s_pe)
        t_bc3 = vop(lambda v, bs=bs: v.tensor_copy(bc3[:], pB[bs][:, 0:3]), waits=[tm])
        consumed(("pB", bs), t_bc3)
        yield

        tcb = vop(lambda v: v.tensor_tensor(brow_bo[:], brow_bo[:], brow_fb[:], ALU.add),
                  waits=[t_rows])
        p3tok["crow"] = vop(lambda v: v.tensor_copy(brow_c[:], brow_bo[:]))

        t_ph2 = t_pair7[0]
        tw_next = dma(xr[0][:], dwo[0:128, :], waits=acquire(("xr", 0)), sem=s_x[0])
        for sl in range(DC):
            xs = sl % 2
            tw = tw_next
            if sl + 1 < DC:
                nxs = (sl + 1) % 2
                tw_next = dma(xr[nxs][:], dwo[128 * (sl + 1):128 * (sl + 2), :],
                              waits=acquire(("xr", nxs)), sem=s_x[nxs])
            for half in range(2):
                ps = half
                tws = acquire(("pB", ps))
                tp = None
                for q in range(4):
                    c = 4 * half + q
                    tp = P.op("tensor", lambda te, xs=xs, c=c, ps=ps, q=q: te.transpose(
                        pB[ps][:, 128 * q:128 * (q + 1)], xr[xs][:, 128 * c:128 * (c + 1)],
                        identf[:]), waits=([tw, t_idf] + tws) if q == 0 else (),
                        inc=s_pe if q == 3 else None)
                consumed(("xr", xs), tp)
                dst = vwo.ap().rearrange("p (c s2 k) -> p c s2 k", c=DC, s2=DC)[:, 4 * half:4 * half + 4, sl, :]
                tv = P.op("scalar", lambda s, dst=dst, ps=ps: s.activation(
                    out=dst, in_=pB[ps][:, 0:512].rearrange("p (q k) -> p q k", q=4),
                    func=AF.Identity), waits=[tp, t_ph2], inc=s_ac)
                consumed(("pB", ps), tv)
                yield
        p3tok["wo"] = (s_dv, s_dv.n)

        tw_next = dma(xr[0][:], dfw[0:128, :], waits=acquire(("xr", 0)), sem=s_x[0])
        for sl in range(DC):
            xs = sl % 2
            tw = tw_next
            if sl + 1 < DC:
                nxs = (sl + 1) % 2
                tw_next = dma(xr[nxs][:], dfw[128 * (sl + 1):128 * (sl + 2), :],
                              waits=acquire(("xr", nxs)), sem=s_x[nxs])
            s0w = acquire(("sf", 0))
            s1w = acquire(("sf", 1))
            tg1 = vop(lambda v, xs=xs: v.tensor_scalar(
                out=sf[0][:], in0=xr[xs][:], scalar1=bc3[:, 0:1], scalar2=None,
                op0=ALU.is_gt), waits=[tw, t_bc3] + s0w)
            tg2 = vop(lambda v, xs=xs: v.tensor_scalar(
                out=sf[1][:], in0=xr[xs][:], scalar1=bc3[:, 1:2], scalar2=None,
                op0=ALU.is_lt), waits=[tw] + s1w)
            tg3 = vop(lambda v: v.tensor_tensor(sf[0][:], sf[0][:], sf[1][:],
                                                ALU.subtract))
            consumed(("xr", xs), tg2)
            consumed(("xr", xs), tg1)
            consumed(("sf", 1), tg3)
            tplast = None
            for half in range(2):
                ps = half
                tws = acquire(("pB", ps))
                tp = None
                for q in range(4):
                    c = 4 * half + q
                    tp = P.op("tensor", lambda te, c=c, ps=ps, q=q: te.transpose(
                        pB[ps][:, 128 * q:128 * (q + 1)], sf[0][:, 128 * c:128 * (c + 1)],
                        identf[:]), waits=([tg3, t_idf] + tws) if q == 0 else (),
                        inc=s_pe if q == 3 else None)
                tplast = tp
                dst = vft.ap().rearrange("p (c s2 k) -> p c s2 k", c=DC, s2=DC)[:, 4 * half:4 * half + 4, sl, :]
                tv = P.op("scalar", lambda s, dst=dst, ps=ps: s.activation(
                    out=dst, in_=pB[ps][:, 0:512].rearrange("p (q k) -> p q k", q=4),
                    func=AF.Identity, scale=bc3[:, 2:3]), waits=[tp, t_ph2], inc=s_ac)
                consumed(("pB", ps), tv)
                yield
            consumed(("sf", 0), tplast)
        p3tok["ft"] = (s_dv, s_dv.n)



    def run_all(g):
        for _ in g:
            pass

    ROWS_STEPS = 40
    P2_STEPS = 20
    HOLD = 4

    def zip_head(g1r, g1m, g2, g3):
        d1 = g1r is None
        dm = g1m is None
        d2 = g2 is None
        d3 = g3 is None
        rem2 = P2_STEPS if g2 is not None else 0
        credit = 0.0
        rate = (P2_STEPS - HOLD) / float(ROWS_STEPS)
        while not d1:
            if not d2 and rem2 > (HOLD if not dm else 0):
                credit += rate
                while credit >= 1.0 and rem2 > HOLD:
                    if next(g2, StopIteration) is StopIteration:
                        d2 = True
                        break
                    rem2 -= 1
                    credit -= 1.0
            if next(g1r, StopIteration) is StopIteration:
                d1 = True
            if not d3:
                if next(g3, StopIteration) is StopIteration:
                    d3 = True
        while not (dm and d2):
            if not d2:
                if next(g2, StopIteration) is StopIteration:
                    d2 = True
                else:
                    rem2 -= 1
            if not dm:
                if next(g1m, StopIteration) is StopIteration:
                    dm = True
            if not d3:
                if next(g3, StopIteration) is StopIteration:
                    d3 = True

    emit_qkv_pair(0)
    run_all(gen_p1rows(0))
    run_all(gen_p1mls(0))
    t_pair7 = [None]
    p3g = [None]
    for h in range(H):
        nh = h + 1
        if nh < H and nh % 2 == 0:
            emit_qkv_pair(nh // 2)
            if nh // 2 == 7:
                t_pair7[0] = (s_pe, s_pe.n)
                p3g[0] = gen_p3prep()
        zip_head(gen_p1rows(nh) if nh < H else None,
                 gen_p1mls(nh) if nh < H else None,
                 gen_pass2(h), p3g[0])
    if p3g[0] is not None:
        run_all(p3g[0])

    u_done = [uevac[(2 * c + 1, 3)] for c in range(DC)]
    tx_next = dma(xr[0][:], dx[0:128, :], waits=acquire(("xr", 0)), sem=s_x[0])
    for t in range(NT):
        xs = t % 2
        tx = tx_next
        if t + 1 < NT:
            nxs = (t + 1) % 2
            tx_next = dma(xr[nxs][:], dx[128 * (t + 1):128 * (t + 2), :],
                          waits=acquire(("xr", nxs)), sem=s_x[nxs])
        oslot = 0
        otws = acquire(("osb", oslot))
        st_tok = None
        for cc in range(2):
            aslot = cc
            fslot = cc
            atws = acquire(("pB", aslot))
            ta = None
            for c in range(DC):
                ta = P.op("tensor", lambda te, c=c, t=t, cc=cc, ps=aslot: te.matmul(
                    pB[ps][:, 0:512], ot[c][:, 128 * t:128 * (t + 1)],
                    woT[c][:, 512 * cc:512 * (cc + 1)], start=(c == 0), stop=False),
                    waits=([p3tok["wo"]] + u_done + atws) if c == 0 else ())
            ta = P.op("tensor", lambda te, t=t, cc=cc, ps=aslot: te.matmul(
                pB[ps][:, 0:512], onesrow_b[:], brow_c[0:1, 512 * cc:512 * (cc + 1)],
                start=False, stop=True), waits=[p3tok["crow"]], inc=s_pe)
            ftws = acquire(("pA", fslot))
            tf = None
            for c in range(DC):
                tf = P.op("tensor", lambda te, c=c, t=t, cc=cc, ps=fslot: te.matmul(
                    pA[ps][:], xut[c][:, 128 * t:128 * (t + 1)],
                    ftT[c][:, 512 * cc:512 * (cc + 1)], start=(c == 0),
                    stop=(c == DC - 1)),
                    waits=([p3tok["ft"], t_xut] + ftws) if c == 0 else (),
                    inc=s_pe if c == DC - 1 else None)
            tv1 = vop(lambda v, t=t, cc=cc, os=oslot, ps=fslot, xs=xs: v.scalar_tensor_tensor(
                out=osb[os][:, 512 * cc:512 * (cc + 1)], in0=pA[ps][:],
                scalar=ab_all[:, t:t + 1], in1=xr[xs][:, 512 * cc:512 * (cc + 1)],
                op0=ALU.mult, op1=ALU.add),
                waits=[tf, tx, t_ab] + otws + (u_done if t == 0 else []))
            otws = []
            consumed(("pA", fslot), tv1)
            tv2 = vop(lambda v, cc=cc, os=oslot, ps=aslot: v.tensor_tensor(
                osb[os][:, 512 * cc:512 * (cc + 1)], osb[os][:, 512 * cc:512 * (cc + 1)],
                pB[ps][:, 0:512], ALU.add), waits=[ta])
            consumed(("pB", aslot), tv2)
            consumed(("xr", xs), tv1)
            st_tok = tv2
        ts = store(dout[128 * t:128 * (t + 1), :], osb[oslot][:], waits=[st_tok],
                   sem=s_st[oslot])
        consumed(("osb", oslot), ts)

    if _DBG:
        allw = [(s_pe, s_pe.n), (s_dv, s_dv.n), (s_ac, s_ac.n), (s_gp, s_gp.n),
                (s_c, s_c.n), (s_x[0], s_x[0].n), (s_x[1], s_x[1].n),
                (s_sg, s_sg.n), (s_mls, s_mls.n), (s_rb, s_rb.n)]
        tap = {"st_mu": st_mu, "st_s2": st_s2, "st_var": st_var, "b_all": b_all,
               "st_t1": st_t1, "st_t2": st_t2, "st_t3": st_t3, "st_t4": st_t4,
               "wsum1": wsum1,
               "ab_all": ab_all, "nmub": nmub, "bc3": bc3, "wsum": wsum,
               "biasq": biasq, "biask": biask, "biasv": biasv, "brow_c": brow_c,
               "gcol": gcol, "becol": becol, "al_st": al_st}
        for c in range(DC):
            tap[f"xut{c}"] = xut[c]
            tap[f"ot{c}"] = ot[c]
            tap[f"woT{c}"] = woT[c]
            tap[f"ftT{c}"] = ftT[c]
        for i in range(4):
            tap[f"qaug{i}"] = qaug[i]
            tap[f"kaug{i}"] = kaug[i]
        for i in range(2):
            tap[f"vtm{i}"] = vtm[i]
            tap[f"npm{i}"] = npm[i]
            tap[f"nmx16{i}"] = nmx16[i]
        tap["wqT"] = wqT
        tap["wkT"] = wkT
        tap["wvT"] = wvT
        for nm_ in _DBG:
            t = tap[nm_]
            try:
                ap = t.ap()
            except TypeError:
                ap = t
            sh = list(ap.shape)
            dd = nc.dram_tensor(f"dbg_{nm_}", sh, ap.dtype, kind="ExternalOutput").ap()
            P.op("sync", lambda g, dd=dd, ap=ap: g.dma_start(out=dd, in_=ap),
                 waits=allw, inc=s_st, amt=16)

    P.op("sync", lambda g: g.nop(),
         waits=[(s_st[0], s_st[0].n), (s_st[1], s_st[1].n)])

    with ctx:
        with nc.Block() as block:
            @block.sync
            def _(eng):
                P.emit("sync", eng)

            @block.tensor
            def _(eng):
                P.emit("tensor", eng)

            @block.vector
            def _(eng):
                P.emit("vector", eng, self_sem=s_dv)

            @block.scalar
            def _(eng):
                P.emit("scalar", eng, self_sem=s_ac)

            @block.gpsimd
            def _(eng):
                P.emit("gpsimd", eng, self_sem=s_gp)
    return nc


def kernel(L0_attn_x, L0_attn_Wq, L0_attn_bq, L0_attn_Wk, L0_attn_bk,
           L0_attn_Wv, L0_attn_bv, L0_attn_Wo, L0_attn_bo, L0_attn_gamma,
           L0_attn_beta, L0_attn_rel_bias, L0_ffn_weight, L0_ffn_alpha,
           L0_ffn_bias):
    f = lambda a: np.ascontiguousarray(np.asarray(a, dtype=np.float32))
    x = f(L0_attn_x)
    wq, wk, wv, wo, fw = f(L0_attn_Wq), f(L0_attn_Wk), f(L0_attn_Wv), f(L0_attn_Wo), f(L0_ffn_weight)
    bq, bk, bv = f(L0_attn_bq), f(L0_attn_bk), f(L0_attn_bv)
    bo = f(L0_attn_bo).reshape(1, D)
    fb = f(L0_ffn_bias).reshape(1, D)
    ga, be = f(L0_attn_gamma), f(L0_attn_beta)
    al = f(L0_ffn_alpha).reshape(1, 1)
    relT = np.ascontiguousarray(f(L0_attn_rel_bias).transpose(0, 2, 1))

    mqt = _interp_matrix_T()
    ident_bf = np.eye(128, dtype=np.float32).astype(ml_dtypes.bfloat16)
    ident_f32 = np.eye(128, dtype=np.float32)
    mask_tm = np.triu(np.full((128, 128), BIG_NEG, dtype=np.float32), 1).astype(ml_dtypes.bfloat16)
    mm, vv = np.meshgrid(np.arange(128), np.arange(512), indexing="ij")
    mask_kb = np.where(mm + 384 > vv, np.float32(BIG_NEG),
                       np.float32(0.0)).astype(ml_dtypes.bfloat16)

    nc = build_nc()
    base = dict(wq=wq, wk=wk, wv=wv, wo=wo, fw=fw, bq=bq, bk=bk, bv=bv, bo=bo,
                fb=fb, gamma=ga, beta=be, alpha=al, relT=relT, d_mqt=mqt,
                ident_bf=ident_bf, ident_f32=ident_f32, d_mask_tm=mask_tm,
                d_mask_kb=mask_kb)
    in_maps = [dict(base, x=np.ascontiguousarray(x[i])) for i in range(8)]
    res = run_bass_kernel_spmd(nc, in_maps, core_ids=list(range(8)))
    if _DBG:
        globals()["_dbg_res"] = res.results
    return np.stack([r["out"] for r in res.results], axis=0)


# version 40
# speedup vs baseline: 1.0626x; 1.0051x over previous
import numpy as np
import ml_dtypes
import concourse.bass as bass
import concourse.mybir as mybir
from concourse.bass_utils import run_bass_kernel_spmd
from contextlib import ExitStack

F32 = mybir.dt.float32
BF16 = mybir.dt.bfloat16
ALU = mybir.AluOpType
AF = mybir.ActivationFunctionType
AX = mybir.AxisListType.X

N = 2048
D = 1024
H = 16
NT = N // 128
DC = D // 128
BIG_NEG = -1.0e30
EPS = 1.0e-5

ENGINES = ("sync", "tensor", "vector", "scalar", "gpsimd")

_DBG = ()


class _Sem:
    __slots__ = ("h", "n")

    def __init__(self, h):
        self.h = h
        self.n = 0


class _Prog:
    def __init__(self):
        self.stream = {e: [] for e in ENGINES}

    def op(self, eng, fn, waits=(), inc=None, amt=1):
        ws = tuple(w for w in waits if w is not None)
        if inc is not None:
            inc.n += amt
        self.stream[eng].append((ws, fn, inc, amt, inc.n if inc else 0))
        return (inc, inc.n) if inc is not None else None

    def emit(self, eng_name, eng, self_sem=None):
        waited = {}
        prev_cnt = None
        for ws, fn, inc, amt, post in self.stream[eng_name]:
            if self_sem is not None and prev_cnt is not None:
                k = id(self_sem)
                if waited.get(k, -1) < prev_cnt:
                    eng.wait_ge(self_sem.h, prev_cnt)
                    waited[k] = prev_cnt
            for sem, val in ws:
                k = id(sem)
                if waited.get(k, -1) < val:
                    eng.wait_ge(sem.h, val)
                    waited[k] = val
            inst = fn(eng)
            if inc is not None:
                inst.then_inc(inc.h, amt)
                if inc is self_sem:
                    prev_cnt = post


def _interp_matrix_T():
    s = 32
    pos = np.arange(N, dtype=np.float32) * ((s - 1) / (N - 1))
    i0 = np.floor(pos).astype(np.int32)
    i1 = np.minimum(i0 + 1, s - 1)
    frac = (pos - i0.astype(np.float32)).astype(np.float32)
    M = np.zeros((N, s), dtype=np.float32)
    rows = np.arange(N)
    np.add.at(M, (rows, i0), 1.0 - frac)
    np.add.at(M, (rows, i1), frac)
    M[N - 1, :] = 0.0
    return np.ascontiguousarray(M.T).astype(ml_dtypes.bfloat16)


def build_nc():
    nc = bass.Bass("TRN2", target_bir_lowering=False)

    dx = nc.dram_tensor("x", [N, D], F32, kind="ExternalInput").ap()
    dwq = nc.dram_tensor("wq", [D, D], F32, kind="ExternalInput").ap()
    dwk = nc.dram_tensor("wk", [D, D], F32, kind="ExternalInput").ap()
    dwv = nc.dram_tensor("wv", [D, D], F32, kind="ExternalInput").ap()
    dwo = nc.dram_tensor("wo", [D, D], F32, kind="ExternalInput").ap()
    dfw = nc.dram_tensor("fw", [D, D], F32, kind="ExternalInput").ap()
    dbq = nc.dram_tensor("bq", [D], F32, kind="ExternalInput").ap()
    dbk = nc.dram_tensor("bk", [D], F32, kind="ExternalInput").ap()
    dbv = nc.dram_tensor("bv", [D], F32, kind="ExternalInput").ap()
    dbo = nc.dram_tensor("bo", [1, D], F32, kind="ExternalInput").ap()
    dfb = nc.dram_tensor("fb", [1, D], F32, kind="ExternalInput").ap()
    dga = nc.dram_tensor("gamma", [D], F32, kind="ExternalInput").ap()
    dbe = nc.dram_tensor("beta", [D], F32, kind="ExternalInput").ap()
    dal = nc.dram_tensor("alpha", [1, 1], F32, kind="ExternalInput").ap()
    drel = nc.dram_tensor("relT", [H, 32, 32], F32, kind="ExternalInput").ap()
    dmqt = nc.dram_tensor("d_mqt", [32, N], BF16, kind="ExternalInput").ap()
    dident = nc.dram_tensor("ident_bf", [128, 128], BF16, kind="ExternalInput").ap()
    didf = nc.dram_tensor("ident_f32", [128, 128], F32, kind="ExternalInput").ap()
    dmtm = nc.dram_tensor("d_mask_tm", [128, 128], BF16, kind="ExternalInput").ap()
    dmkb = nc.dram_tensor("d_mask_kb", [128, 512], BF16, kind="ExternalInput").ap()
    dout = nc.dram_tensor("out", [N, D], F32, kind="ExternalOutput").ap()

    ctx = ExitStack()
    e = ctx.enter_context

    xutA = e(nc.sbuf_tensor("xutA", [128, DC * N], BF16))
    xut = [xutA.ap()[:, c * N:(c + 1) * N] for c in range(DC)]
    ot = [e(nc.sbuf_tensor(f"ot{c}", [128, N], BF16)) for c in range(DC)]
    mqt = e(nc.sbuf_tensor("mqt", [32, N], BF16))
    ident = e(nc.sbuf_tensor("ident", [128, 128], BF16))
    identf = e(nc.sbuf_tensor("identf", [128, 128], F32))
    mask_tm = e(nc.sbuf_tensor("mask_tm", [128, 128], BF16))
    mask_kb = e(nc.sbuf_tensor("mask_kb", [128, 512], BF16))
    gcol = e(nc.sbuf_tensor("gcol", [128, DC], F32))
    becol = e(nc.sbuf_tensor("becol", [128, DC], F32))
    becol_bf = e(nc.sbuf_tensor("becol_bf", [128, DC], BF16))
    bqcol = e(nc.sbuf_tensor("bqcol", [128, DC], F32))
    bkcol = e(nc.sbuf_tensor("bkcol", [128, DC], F32))
    bvcol = e(nc.sbuf_tensor("bvcol", [128, DC], F32))
    biasq = e(nc.sbuf_tensor("biasq", [128, DC], F32))
    biask = e(nc.sbuf_tensor("biask", [128, DC], F32))
    biasv = e(nc.sbuf_tensor("biasv", [128, DC], F32))
    relb = e(nc.sbuf_tensor("relb", [32, 512], BF16))
    onescol = e(nc.sbuf_tensor("onescol", [128, 1], F32))
    onesrow_b = e(nc.sbuf_tensor("onesrow_b", [1, 128], BF16))
    onesrow_f = e(nc.sbuf_tensor("onesrow_f", [1, 128], F32))
    st_mu = e(nc.sbuf_tensor("st_mu", [128, NT], F32))
    st_s2 = e(nc.sbuf_tensor("st_s2", [128, NT], F32))
    st_var = e(nc.sbuf_tensor("st_var", [128, NT], F32))
    st_t1 = e(nc.sbuf_tensor("st_t1", [128, NT], F32))
    st_t2 = e(nc.sbuf_tensor("st_t2", [128, NT], F32))
    st_t3 = e(nc.sbuf_tensor("st_t3", [128, NT], F32))
    st_t4 = e(nc.sbuf_tensor("st_t4", [128, NT], F32))
    b_all = e(nc.sbuf_tensor("b_all", [128, NT], F32))
    ab_all = e(nc.sbuf_tensor("ab_all", [128, NT], F32))
    nmub = e(nc.sbuf_tensor("nmub", [128, NT], F32))
    npm = [e(nc.sbuf_tensor(f"npm{i}", [128, 64], F32)) for i in range(2)]
    nmx16 = [e(nc.sbuf_tensor(f"nmx16{i}", [128, NT], F32)) for i in range(2)]
    nm_stage = [e(nc.sbuf_tensor(f"nmstage{i}", [16, 128], BF16)) for i in range(2)]
    usb = [e(nc.sbuf_tensor(f"usb{i}", [65, 512], F32)) for i in range(2)]
    zb = e(nc.sbuf_tensor("zb", [65, 512], F32))
    zrowr = zb.ap()[64:65, :]
    zsh = e(nc.sbuf_tensor("zsh", [64, 512], BF16))
    qaug = [e(nc.sbuf_tensor(f"qaug{i}", [98, N], BF16)) for i in range(4)]
    kaug = [e(nc.sbuf_tensor(f"kaug{i}", [98, N], BF16)) for i in range(4)]
    vtm = [e(nc.sbuf_tensor(f"vtm{i}", [128, NT * 130], BF16)) for i in range(2)]
    pt = [e(nc.sbuf_tensor(f"pt{i}", [128, 1024], BF16)) for i in range(2)]
    sf = [e(nc.sbuf_tensor(f"sf{i}", [128, 1024], F32)) for i in range(2)]
    xr = [e(nc.sbuf_tensor(f"xr{i}", [128, 1024], F32)) for i in range(2)]
    osb = [e(nc.sbuf_tensor(f"osb{i}", [128, 1024], F32)) for i in range(1)]
    stage_hi = e(nc.sbuf_tensor("stage_hi", [128, N], BF16))
    brow_bo = e(nc.sbuf_tensor("brow_bo", [1, D], F32))
    brow_fb = e(nc.sbuf_tensor("brow_fb", [1, D], F32))
    brow_c = e(nc.sbuf_tensor("brow_c", [1, D], BF16))
    al_st = e(nc.sbuf_tensor("al_st", [1, 3], F32))
    bc3 = e(nc.sbuf_tensor("bc3", [128, 3], F32))
    wsum = e(nc.sbuf_tensor("wsum", [128, DC], F32))
    wsum1 = e(nc.sbuf_tensor("wsum1", [128, 1], F32))
    wqT = e(nc.sbuf_tensor("wqT", [128, DC * 1024], BF16))
    wkT = e(nc.sbuf_tensor("wkT", [128, DC * 1024], BF16))
    wvT = e(nc.sbuf_tensor("wvT", [128, DC * 1024], BF16))
    vwo = nc.virtual_sbuf_tensor("vwo", [128, DC * 1024], BF16)
    nc.back_virtual_tensor_with(vwo, wqT.tensor if hasattr(wqT, "tensor") else wqT)
    vft = nc.virtual_sbuf_tensor("vft", [128, DC * 1024], BF16)
    nc.back_virtual_tensor_with(vft, wkT.tensor if hasattr(wkT, "tensor") else wkT)
    woT = [vwo.ap()[:, 1024 * c:1024 * (c + 1)] for c in range(DC)]
    ftT = [vft.ap()[:, 1024 * c:1024 * (c + 1)] for c in range(DC)]
    pt.append(osb[0].ap().bitcast(BF16)[:, 0:1024])

    pA = [e(nc.psum_tensor(f"pA{i}", [128, 512], F32)) for i in range(2)]
    pBw = e(nc.psum_tensor("pBw", [128, 2048], F32))
    pB = [pBw.ap()[:, 1024 * i:1024 * (i + 1)] for i in range(2)]
    pU = [e(nc.psum_tensor(f"pU{i}", [65, 512], F32)) for i in range(2)]

    s_c = _Sem(e(nc.semaphore("s_c")))
    s_c2 = _Sem(e(nc.semaphore("s_c2")))
    s_c3 = _Sem(e(nc.semaphore("s_c3")))
    s_x = [_Sem(e(nc.semaphore("s_x0"))), _Sem(e(nc.semaphore("s_x1")))]
    s_sg = _Sem(e(nc.semaphore("s_sg")))
    s_mls = _Sem(e(nc.semaphore("s_mls")))
    s_rb = _Sem(e(nc.semaphore("s_rb")))
    s_nz = _Sem(e(nc.semaphore("s_nz")))
    s_w = [_Sem(e(nc.semaphore("s_w0"))), _Sem(e(nc.semaphore("s_w1")))]
    s_st = [_Sem(e(nc.semaphore("s_st0"))), _Sem(e(nc.semaphore("s_st1")))]
    s_pe = _Sem(e(nc.semaphore("s_pe")))
    s_dv = _Sem(e(nc.semaphore("s_dv")))
    s_ac = _Sem(e(nc.semaphore("s_ac")))
    s_gp = _Sem(e(nc.semaphore("s_gp")))

    P = _Prog()

    def dma(dst, src, waits=(), sem=None):
        return P.op("sync", lambda g, d=dst, s=src: g.dma_start(out=d, in_=s),
                    waits=waits, inc=sem, amt=16)

    def store(dst, src, waits=(), sem=None):
        return P.op("sync", lambda g, d=dst, s=src: g.dma_start(out=d, in_=s),
                    waits=waits, inc=sem, amt=16)

    def vop(fn, waits=(), inc=s_dv):
        return P.op("vector", fn, waits=waits, inc=inc)

    free_tok = {}

    def acquire(key):
        toks = free_tok.get(key, ())
        free_tok[key] = []
        return list(toks)

    def consumed(key, tok):
        free_tok.setdefault(key, []).append(tok)

    dma(mqt[:], dmqt, sem=s_c)
    dma(ident[:], dident, sem=s_c)
    dma(identf[:], didf, sem=s_c)
    dma(mask_tm[:], dmtm, sem=s_c)
    dma(mask_kb[:], dmkb, sem=s_c)
    for h in range(H):
        P.op("gpsimd", lambda g, h=h: g.dma_start(out=relb[:, 32 * h:32 * h + 32],
                                                  in_=drel[h]), inc=s_rb, amt=16)
    t_relb = (s_rb, s_rb.n)
    t_const = (s_c, s_c.n)
    t_mqt = t_id = t_idf = t_mtm = t_mkb = t_const
    for c in range(DC):
        P.op("gpsimd", lambda g, c=c: g.dma_start(
            out=gcol[:, c:c + 1],
            in_=dga[c * 128:(c + 1) * 128].rearrange("(p o) -> p o", o=1)),
            inc=s_c2, amt=16)
        P.op("gpsimd", lambda g, c=c: g.dma_start(
            out=becol[:, c:c + 1],
            in_=dbe[c * 128:(c + 1) * 128].rearrange("(p o) -> p o", o=1)),
            inc=s_c2, amt=16)
        P.op("gpsimd", lambda g, c=c: g.dma_start(
            out=bqcol[:, c:c + 1],
            in_=dbq[c * 128:(c + 1) * 128].rearrange("(p o) -> p o", o=1)),
            inc=s_c2, amt=16)
        P.op("gpsimd", lambda g, c=c: g.dma_start(
            out=bkcol[:, c:c + 1],
            in_=dbk[c * 128:(c + 1) * 128].rearrange("(p o) -> p o", o=1)),
            inc=s_c2, amt=16)
        P.op("gpsimd", lambda g, c=c: g.dma_start(
            out=bvcol[:, c:c + 1],
            in_=dbv[c * 128:(c + 1) * 128].rearrange("(p o) -> p o", o=1)),
            inc=s_c2, amt=16)
    dma(brow_bo[:], dbo, sem=s_c3)
    dma(brow_fb[:], dfb, sem=s_c3)
    dma(al_st[:, 2:3], dal, sem=s_c3)
    for i in range(4):
        dma(qaug[i][64:96, :], dmqt, sem=s_c3)
    t_cols = (s_c2, s_c2.n)
    t_rows = (s_c3, s_c3.n)
    qa_mq = [t_rows] * 4

    P.op("vector", lambda v: v.memset(onescol[:], 1.0))
    P.op("vector", lambda v: v.memset(onesrow_b[:], 1.0))
    P.op("vector", lambda v: v.memset(onesrow_f[:], 1.0))
    ka_ones = [P.op("vector", lambda v, i=i: v.memset(kaug[i][96:98, :], 1.0), inc=s_dv)
               for i in range(4)]
    vt_ones = []
    for i in range(2):
        P.op("vector", lambda v, i=i: v.memset(
            vtm[i].ap().rearrange("p (t c) -> p t c", c=130)[:, :, 64:65], 1.0))
        vt_ones.append(P.op("vector", lambda v, i=i: v.memset(
            vtm[i].ap().rearrange("p (t c) -> p t c", c=130)[:, :, 129:130], 1.0),
            inc=s_dv))
    t_beb = vop(lambda v: v.tensor_copy(becol_bf[:], becol[:]), waits=[t_cols])

    def gen_statsA():
        for t in range(NT):
            xs = t % 2
            tx = dma(xr[xs][:], dx[128 * t:128 * (t + 1), :], waits=acquire(("xr", xs)),
                     sem=s_x[xs])
            tr = vop(lambda v, t=t, xs=xs: v.reduce_sum(st_mu[:, t:t + 1], xr[xs][:],
                                                        axis=AX), waits=[tx])
            consumed(("xr", xs), tr)
            ss = t % 2
            tq = P.op("scalar", lambda s, xs=xs, ss=ss, t=t: s.activation(
                out=sf[ss][:], in_=xr[xs][:], func=AF.Square,
                accum_out=st_s2[:, t:t + 1]), waits=[tx] + acquire(("sf", ss)), inc=s_ac)
            consumed(("xr", xs), tq)
            consumed(("sf", ss), tq)
            yield


    wT_all = {"q": wqT, "k": wkT, "v": wvT}
    dw_all = {"q": dwq, "k": dwk, "v": dwv}
    bcol_all = {"q": bqcol, "k": bkcol, "v": bvcol}
    bias_all = {"q": biasq, "k": biask, "v": biasv}
    bias_tok = {}
    ev_tok = {}

    stage_free = {"any": None}
    wst = [stage_hi.ap()[:, 0:1024], stage_hi.ap()[:, 1024:2048]]
    pBbf = [pB[i].bitcast(BF16) for i in range(2)]

    def gen_wprep():
      for wn in ("q", "k", "v"):
        for sl in range(DC):
            xs = sl % 2
            tw = P.op("gpsimd", lambda g, wn=wn, sl=sl, xs=xs: g.dma_start(
                out=wst[xs], in_=dw_all[wn][128 * sl:128 * (sl + 1), :]),
                waits=acquire(("wst", xs)), inc=s_w[xs], amt=16)
            evs = []
            for half in range(2):
                ps = half
                tws = acquire(("pB", ps))
                tp = None
                for q in range(4):
                    c = 4 * half + q
                    tp = P.op("tensor", lambda te, xs=xs, c=c, ps=ps, q=q: te.transpose(
                        pBbf[ps][:, 128 * q:128 * (q + 1)], wst[xs][:, 128 * c:128 * (c + 1)],
                        ident[:]), waits=([tw, t_id] + tws) if q == 0 else (),
                        inc=s_pe if q == 3 else None)
                consumed(("wst", xs), tp)
                tv = P.op("scalar", lambda s, wn=wn, sl=sl, half=half, ps=ps: s.activation(
                    out=wT_all[wn][:, 1024 * sl + 512 * half:1024 * sl + 512 * half + 512],
                    in_=pBbf[ps][:, 0:512], func=AF.Identity), waits=[tp], inc=s_ac)
                consumed(("pB", ps), tv)
                evs.append(tv)
            ev_tok[(wn, sl)] = evs[-1]
            us = 0
            utws = acquire(("pB", us))
            tm = None
            for c in range(DC):
                tm = P.op("tensor", lambda te, wn=wn, sl=sl, c=c, us=us: te.matmul(
                    pB[us][:, 0:1],
                    wT_all[wn][:, 1024 * sl + 128 * c:1024 * sl + 128 * (c + 1)],
                    becol_bf[:, c:c + 1], start=(c == 0), stop=(c == DC - 1)),
                    waits=([evs[-1], t_beb] + utws) if c == 0 else (),
                    inc=s_pe if c == DC - 1 else None)
            if wn == "q":
                tb_ = vop(lambda v, wn=wn, sl=sl, us=us: v.tensor_scalar(
                    out=bias_all[wn][:, sl:sl + 1], in0=pB[us][:, 0:1],
                    scalar1=bcol_all[wn][:, sl:sl + 1], scalar2=0.125,
                    op0=ALU.add, op1=ALU.mult), waits=[tm, t_cols])
            else:
                tb_ = vop(lambda v, wn=wn, sl=sl, us=us: v.tensor_scalar(
                    out=bias_all[wn][:, sl:sl + 1], in0=pB[us][:, 0:1],
                    scalar1=bcol_all[wn][:, sl:sl + 1], scalar2=None,
                    op0=ALU.add), waits=[tm, t_cols])
            consumed(("pB", us), tb_)
            bias_tok[(wn, sl)] = tb_
            yield

    _gw = gen_wprep()
    for _ in gen_statsA():
        next(_gw, None)
    t_statsA_ac = (s_ac, s_ac.n)

    tk = vop(lambda v: v.tensor_scalar(out=st_mu[:], in0=st_mu[:], scalar1=1.0 / D,
                                       scalar2=None, op0=ALU.mult), waits=[t_statsA_ac])
    tk = vop(lambda v: v.tensor_scalar(out=st_s2[:], in0=st_s2[:], scalar1=1.0 / D,
                                       scalar2=None, op0=ALU.mult))
    tk = vop(lambda v: v.tensor_tensor(st_t1[:], st_mu[:], st_mu[:], ALU.mult))
    tk = vop(lambda v: v.tensor_tensor(st_var[:], st_s2[:], st_t1[:], ALU.subtract))
    tk = vop(lambda v: v.tensor_scalar(out=st_t1[:], in0=st_var[:], scalar1=EPS,
                                       scalar2=None, op0=ALU.add))
    tk = vop(lambda v: v.reciprocal(st_t2[:], st_t1[:]))
    tk = vop(lambda v: v.tensor_tensor(st_t3[:], st_var[:], st_t2[:], ALU.mult))
    tk = vop(lambda v: v.tensor_scalar(out=st_t3[:], in0=st_t3[:], scalar1=EPS,
                                       scalar2=None, op0=ALU.add))
    tab = P.op("scalar", lambda s: s.activation(out=ab_all[:], in_=st_t3[:], func=AF.Sqrt),
               waits=[tk], inc=s_ac)
    tk = vop(lambda v: v.reciprocal(st_t4[:], st_t3[:]))
    tk = vop(lambda v: v.tensor_tensor(st_t4[:], st_t2[:], st_t4[:], ALU.mult))
    tb = P.op("scalar", lambda s: s.activation(out=b_all[:], in_=st_t4[:], func=AF.Sqrt),
              waits=[tk], inc=s_ac)

    def newton_sqrt(y, u, tmp1, tmp2, w0):
        t1 = vop(lambda v: v.tensor_tensor(tmp1[:], y[:], y[:], ALU.mult), waits=[w0])
        t1 = vop(lambda v: v.tensor_tensor(tmp1[:], u[:], tmp1[:], ALU.subtract))
        t1 = vop(lambda v: v.reciprocal(tmp2[:], y[:]))
        t1 = vop(lambda v: v.tensor_tensor(tmp1[:], tmp1[:], tmp2[:], ALU.mult))
        t1 = vop(lambda v: v.tensor_scalar(out=tmp1[:], in0=tmp1[:], scalar1=0.5,
                                           scalar2=None, op0=ALU.mult))
        return vop(lambda v: v.tensor_tensor(y[:], y[:], tmp1[:], ALU.add))

    t_ab = newton_sqrt(ab_all, st_t3, st_t1, st_t2, tab)
    t_b = newton_sqrt(b_all, st_t4, st_t1, st_t2, tb)
    tk = vop(lambda v: v.tensor_tensor(nmub[:], st_mu[:], b_all[:], ALU.mult))
    t_nmub = vop(lambda v: v.tensor_scalar(out=nmub[:], in0=nmub[:], scalar1=-1.0,
                                           scalar2=None, op0=ALU.mult))

    def gen_passB():
        for t in range(NT):
            xs = t % 2
            tx = dma(xr[xs][:], dx[128 * t:128 * (t + 1), :], waits=acquire(("xr", xs)),
                     sem=s_x[xs])
            ss = t % 2
            ta = vop(lambda v, t=t, xs=xs, ss=ss: v.tensor_scalar(
                out=sf[ss][:], in0=xr[xs][:], scalar1=b_all[:, t:t + 1],
                scalar2=nmub[:, t:t + 1], op0=ALU.mult, op1=ALU.add),
                waits=[tx, t_b, t_nmub] + acquire(("sf", ss)))
            consumed(("xr", xs), ta)
            for half in range(2):
                ps = half
                tws = acquire(("pB", ps))
                tp = None
                for q in range(4):
                    c = 4 * half + q
                    last = (q == 3)
                    tp = P.op("tensor", lambda te, ss=ss, c=c, ps=ps, q=q: te.transpose(
                        pB[ps][:, 128 * q:128 * (q + 1)], sf[ss][:, 128 * c:128 * (c + 1)],
                        identf[:]), waits=([ta, t_idf] + tws) if q == 0 else (),
                        inc=s_pe if last else None)
                consumed(("sf", ss), tp)
                dst = xutA.ap().rearrange("p (c t2 k) -> p c t2 k", c=DC, t2=NT)[:, 4 * half:4 * half + 4, t, :]
                tv = P.op("scalar", lambda s, dst=dst, ps=ps: s.activation(
                    out=dst, in_=pB[ps][:, 0:512].rearrange("p (q k) -> p q k", q=4),
                    func=AF.Identity), waits=[tp], inc=s_ac)
                consumed(("pB", ps), tv)
            yield

    for _ in gen_passB():
        next(_gw, None)
        next(_gw, None)
    for _ in _gw:
        pass
    stage_free["any"] = (s_pe, s_pe.n)
    t_allbias_pe = (s_pe, s_pe.n)
    t_xut = (s_dv, s_dv.n)

    gscale_tok = {}
    for wn in ("q", "k", "v"):
        for c in range(DC):
            ap = wT_all[wn].ap().rearrange("p (s c2 k) -> p s c2 k", s=DC, c2=DC)[:, :, c, :]
            if wn == "q":
                gscale_tok[(wn, c)] = vop(lambda v, ap=ap, c=c: v.tensor_scalar(
                    out=ap, in0=ap, scalar1=gcol[:, c:c + 1], scalar2=0.125,
                    op0=ALU.mult, op1=ALU.mult), waits=[t_allbias_pe, t_cols])
            else:
                gscale_tok[(wn, c)] = vop(lambda v, ap=ap, c=c: v.tensor_scalar(
                    out=ap, in0=ap, scalar1=gcol[:, c:c + 1], scalar2=None,
                    op0=ALU.mult), waits=[t_allbias_pe, t_cols])

    qaug_ready = [None] * H
    kaug_ready = [None] * H
    kaug_rel = [None] * H
    vtm_ready = [None] * 8
    mls_ready = [None] * H
    pass2_done_pe = [None] * H
    pass1_done_pe = [None] * H
    p1_lastexp = [None] * H
    ucopy = {}
    uevac = {}
    prev_scatter = [None]

    def emit_qkv_pair(j):
        a, b = 2 * j, 2 * j + 1
        sa, sb = a % 4, b % 4
        guard_a = [pass2_done_pe[a - 4], pass1_done_pe[a - 4]] if a >= 4 else []
        guard_b = [pass2_done_pe[b - 4], pass1_done_pe[b - 4]] if b >= 4 else []
        vguard = ([pass2_done_pe[2 * (j - 2)], pass2_done_pe[2 * (j - 2) + 1]]
                  if j >= 2 else [])
        for h, sh, gd in ((a, sa, guard_a), (b, sb, guard_b)):
            tvl = None
            for ch in range(4):
                ps = ch % 2
                tws = acquire(("pB", ps))
                tm = P.op("tensor", lambda te, h=h, ch=ch, ps=ps: te.matmul(
                    pB[ps][64:96, 0:512], relb[:, 32 * h:32 * h + 32],
                    mqt[:, 512 * ch:512 * (ch + 1)], start=True, stop=True,
                    tile_position=(0, 64)),
                    waits=[t_relb, t_mqt] + tws, inc=s_pe)
                tvl = P.op("scalar", lambda s, sh=sh, ch=ch, ps=ps: s.activation(
                    out=kaug[sh][64:96, 512 * ch:512 * (ch + 1)],
                    in_=pB[ps][64:96, 0:512], func=AF.Identity),
                    waits=[tm] + gd, inc=s_ac)
                consumed(("pB", ps), tvl)
            kaug_rel[h] = tvl

        for wn in ("v", "q", "k"):
            ev_hi = []
            for half in range(2):
                gw = [gscale_tok[(wn, c)] for c in range(DC)] if j == 0 else []
                tms = {}
                for sub in range(2):
                    tc = 2 * half + sub
                    ps = sub
                    tws = acquire(("pA", ps))
                    tm = None
                    for c in range(DC):
                        tm = P.op("tensor", lambda te, wn=wn, j=j, c=c, tc=tc, ps=ps: te.matmul(
                            pA[ps][:],
                            wT_all[wn][:, 1024 * j + 128 * c:1024 * j + 128 * (c + 1)],
                            xut[c][:, 512 * tc:512 * (tc + 1)],
                            start=(c == 0), stop=(c == DC - 1)),
                            waits=(gw + tws + [t_xut]) if c == 0 else (),
                            inc=s_pe if c == DC - 1 else None)
                    tms[sub] = tm
                if wn == "v":
                    ssl = half
                    tvh = {}
                    for sub in range(2):
                        tvh[sub] = P.op("scalar", lambda s, j=j, ps=sub, ssl=ssl, sub=sub: s.activation(
                            out=sf[ssl][:, 512 * sub:512 * sub + 512], in_=pA[ps][:],
                            func=AF.Identity, bias=biasv[:, j:j + 1]),
                            waits=[tms[sub], bias_tok[("v", j)]] +
                                  (acquire(("sf", ssl)) if sub == 0 else []), inc=s_ac)
                        consumed(("pA", sub), tvh[sub])
                    tv = tvh[1]
                    vslot = j % 2
                    tplast = None
                    for g in range(2):
                        pq = g
                        ptws = acquire(("pB", pq))
                        tp = None
                        for q in range(4):
                            qq = 4 * g + q
                            tp = P.op("tensor", lambda te, ssl=ssl, qq=qq, q=q, pq=pq: te.transpose(
                                pB[pq][:, 128 * q:128 * (q + 1)],
                                sf[ssl][:, 128 * qq:128 * (qq + 1)],
                                identf[:]), waits=([tvh[g], t_idf] + ptws) if q == 0 else (),
                                inc=s_pe if q == 3 else None)
                        vdst = vtm[vslot].ap().rearrange("p (t c) -> p t c", c=130)[
                            :, 8 * ssl + 4 * g:8 * ssl + 4 * g + 4, :]
                        psrc = pB[pq][:, 0:512].rearrange("p (q k) -> p q k", q=4)
                        tva = P.op("scalar", lambda s, vdst=vdst, psrc=psrc: s.activation(
                            out=vdst[:, :, 0:64], in_=psrc[:, :, 0:64],
                            func=AF.Identity), waits=[tp] + vguard, inc=s_ac)
                        tvv = P.op("scalar", lambda s, vdst=vdst, psrc=psrc: s.activation(
                            out=vdst[:, :, 65:129], in_=psrc[:, :, 64:128],
                            func=AF.Identity), inc=s_ac)
                        consumed(("pB", pq), tva)
                        consumed(("pB", pq), tvv)
                        tplast = tp
                    consumed(("sf", ssl), tplast)
                    vtm_ready[j] = tvv
                else:
                    dst = qaug[sa] if wn == "q" else kaug[sa]
                    bt = bias_tok[(wn, j)]
                    tv1 = None
                    for sub in range(2):
                        tc = 2 * half + sub
                        tv1 = P.op("scalar", lambda s, dst=dst, tc=tc, ps=sub, wn=wn, j=j: s.activation(
                            out=dst[0:64, 512 * tc:512 * tc + 512], in_=pA[ps][0:64, :],
                            func=AF.Identity, bias=bias_all[wn][0:64, j:j + 1]),
                            waits=[tms[sub], bt] + guard_a, inc=s_ac)
                        tv2 = P.op("scalar", lambda s, tc=tc, ps=sub, wn=wn, j=j: s.activation(
                            out=stage_hi[64:128, 512 * tc:512 * tc + 512],
                            in_=pA[ps][64:128, :],
                            func=AF.Identity, bias=bias_all[wn][64:128, j:j + 1]),
                            waits=[tms[sub], bt, stage_free["any"]], inc=s_ac)
                        consumed(("pA", sub), tv1)
                        consumed(("pA", sub), tv2)
                        ev_hi.append(tv2)
                    if half == 1:
                        if wn == "q":
                            qaug_ready[a] = tv1
                        else:
                            kaug_ready[a] = tv1
            if wn in ("q", "k"):
                dstb = qaug[sb] if wn == "q" else kaug[sb]
                td = P.op("sync", lambda g, dstb=dstb: g.dma_start(
                    out=dstb[0:64, :], in_=stage_hi[64:128, :]),
                    waits=ev_hi + guard_b, inc=s_sg, amt=16)
                stage_free["any"] = td
                if wn == "q":
                    qaug_ready[b] = td
                else:
                    kaug_ready[b] = td

    def gen_p1rows(h):
        sh = h % 4
        r = h % 2
        prev = mls_ready[h - 2] if h >= 2 else None
        slot_guard = [pass2_done_pe[h - 4]] if h >= 4 else []
        tz = vop(lambda v, r=r: v.memset(npm[r][:], 3.0e38), waits=[prev] if prev else [])
        qr = qaug_ready[h]
        kr = [kaug_ready[h], kaug_rel[h], ka_ones[sh], qa_mq[sh]]
        pc = 0
        for qt in range(NT):
            L = 128 * (qt + 1)
            npieces = (L + 511) // 512
            for p in range(npieces):
                lo = 512 * p
                w = min(512, L - lo)
                ps = pc % 2
                pc += 1
                tws = acquire(("pA", ps))
                has_mask = (lo + w == L)
                ol = L - 128 - lo if has_mask else -1
                tm = P.op("tensor", lambda te, sh=sh, qt=qt, lo=lo, w=w, ps=ps,
                          hm=has_mask: te.matmul(
                    pA[ps][:, 0:w], qaug[sh][0:96, 128 * qt:128 * qt + 128],
                    kaug[sh][0:96, lo:lo + w], start=True, stop=(not hm)),
                    waits=[qr] + kr + tws, inc=None if has_mask else s_pe)
                if has_mask:
                    tm = P.op("tensor", lambda te, ol=ol, ps=ps: te.matmul(
                        pA[ps][:, ol:ol + 128], ident[:], mask_tm[:],
                        start=False, stop=True), waits=[t_id, t_mtm], inc=s_pe)
                col = 4 * qt + p
                tr = vop(lambda v, r=r, col=col, ps=ps, w=w: v.tensor_reduce(
                    npm[r][:, col:col + 1], pA[ps][:, 0:w], axis=AX, op=ALU.max,
                    negate=True), waits=[tm])
                consumed(("pA", ps), tr)
                yield
        pass1_done_pe[h] = (s_pe, s_pe.n)

    def gen_p1mls(h):
        sh = h % 4
        r = h % 2
        slot_guard = [pass2_done_pe[h - 4]] if h >= 4 else []
        t1 = vop(lambda v, r=r: v.tensor_reduce(
            nmx16[r][:], npm[r].ap().rearrange("p (q c) -> p q c", c=4), axis=AX,
            op=ALU.min))
        yield
        ps = 0
        tws = acquire(("pB", ps))
        tt = P.op("tensor", lambda te, r=r, ps=ps: te.transpose(
            pB[ps][0:16, 0:128], nmx16[r][:], identf[:]),
            waits=[t1, t_idf] + tws, inc=s_pe)
        tv = P.op("scalar", lambda s, r=r, ps=ps: s.activation(
            out=nm_stage[r][:], in_=pB[ps][0:16, 0:128], func=AF.Identity),
            waits=[tt], inc=s_ac)
        consumed(("pB", ps), tv)
        yield
        td1 = P.op("sync", lambda g, sh=sh, r=r: g.dma_start(
            out=qaug[sh][96:97, :], in_=nm_stage[r][:]),
            waits=[tv, prev_scatter[0]] + slot_guard, inc=s_mls, amt=16)
        mls_ready[h] = td1
        prev_scatter[0] = td1
        yield

    def gen_pass2(h):
        sh = h % 4
        j = h // 2
        vslot = j % 2
        odd = h % 2
        wc = 0
        from collections import deque
        avq = deque()
        postq = deque()

        def tick():
            for it in postq:
                it[0] -= 1
            while postq and postq[0][0] <= 0:
                postq.popleft()[1]()

        def flushpost():
            while postq:
                postq.popleft()[1]()

        for nj in range(4):
            us = nj % 2
            if nj >= 2:
                utws = [ucopy[(h, nj - 2)]]
            elif h > 0:
                utws = [ucopy[(h - 1, nj + 2)]]
            else:
                utws = []
            for k0 in range(0, 4 * nj + 4, 2):
                ws = wc % 2
                tws = acquire(("pB", ws))
                tmh = {}
                offs = {}
                for sub in range(2):
                    kt = k0 + sub
                    diag = (kt >= 4 * nj)
                    o = 128 * (kt - 4 * nj) if diag else 0
                    offs[sub] = o
                    tm = P.op("tensor", lambda te, sh=sh, kt=kt, nj=nj, ws=ws, sub=sub,
                              o=o, diag=diag: te.matmul(
                        pB[ws][:, 512 * sub + o:512 * sub + 512],
                        kaug[sh][0:97, 128 * kt:128 * kt + 128],
                        qaug[sh][0:97, 512 * nj + o:512 * nj + 512], start=True,
                        stop=(not diag)),
                        waits=([mls_ready[h], kaug_ready[h], kaug_rel[h],
                                vtm_ready[j]] + tws) if sub == 0 else (),
                        inc=None if diag else s_pe)
                    if diag:
                        tm = P.op("tensor", lambda te, o=o, ws=ws, sub=sub: te.matmul(
                            pB[ws][:, 512 * sub + o:512 * sub + o + 128],
                            ident[:], mask_kb[:, 384:512],
                            start=False, stop=True), waits=[t_mkb], inc=s_pe)
                    tmh[sub] = tm
                pts = wc % 3
                wc += 1
                ptws = acquire(("pt", pts))
                if offs[0] == 0 and offs[1] == 0:
                    te_ = P.op("scalar", lambda s, ws=ws, pts=pts: s.activation(
                        out=pt[pts][:], in_=pB[ws][:], func=AF.Exp),
                        waits=[tmh[1]] + ptws, inc=s_ac)
                    teh = {0: te_, 1: te_}
                else:
                    teh = {}
                    for sub in range(2):
                        o = offs[sub]
                        teh[sub] = P.op("scalar", lambda s, ws=ws, pts=pts, sub=sub,
                                        o=o: s.activation(
                            out=pt[pts][:, 512 * sub + o:512 * sub + 512],
                            in_=pB[ws][:, 512 * sub + o:512 * sub + 512], func=AF.Exp),
                            waits=[tmh[sub]] + (ptws if sub == 0 else []), inc=s_ac)
                    te_ = teh[1]
                consumed(("pB", ws), te_)
                if len(avq) >= 2:
                    avq.popleft()()
                avq.append(lambda vs=vslot, h=h, k0=k0, us=us, pts=pts, nj=nj,
                           teh=dict(teh), offs=dict(offs),
                           utws=(utws if k0 == 0 else []), odd=odd:
                           _av2(vs, h, k0, us, pts, nj, teh, offs, utws, odd))
                tick()
                yield
            while avq:
                avq.popleft()()
            _norm_chain(h, nj, us, odd, (s_pe, s_pe.n), postq)
        flushpost()
        pass2_done_pe[h] = (s_pe, s_pe.n)

    def _av2(vs, h, k0, us, pts, nj, teh, offs, utws, odd):
        for sub in range(2):
            kt = k0 + sub
            o = offs[sub]
            ta = P.op("tensor", lambda te, vs=vs, kt=kt, us=us, pts=pts, nj=nj,
                      sub=sub, o=o, odd=odd: te.matmul(
                pU[us][0:65, o:512],
                vtm[vs][:, 130 * kt + 65 * odd:130 * kt + 65 * odd + 65],
                pt[pts][:, 512 * sub + o:512 * sub + 512],
                start=(kt == 0), stop=(kt == 4 * nj + 3)),
                waits=[teh[sub]] + (list(utws) if sub == 0 else []),
                inc=s_pe if sub == 1 else None)
            if sub == 1:
                consumed(("pt", pts), ta)

    last_nz = [None]
    last_stt = {0: None, 1: None, "zb": None}

    def _norm_chain(h, nj, us, odd, av_tok, postq):
        while postq:
            postq.popleft()[1]()
        ub = nj % 2
        tcp = P.op("scalar", lambda s, us=us, ub=ub: s.activation(
            out=usb[ub][:], in_=pU[us][0:65, :], func=AF.Identity),
            waits=[av_tok, last_stt[ub]], inc=s_ac)
        ucopy[(h, nj)] = tcp

        def mid(tcp=tcp):
            tl = P.op("scalar", lambda s, ub=ub: s.activation(
                out=zrowr, in_=usb[ub][64:65, :], func=AF.Ln),
                waits=[tcp], inc=s_ac)
            tx = P.op("scalar", lambda s, ub=ub: s.activation(
                out=usb[ub][64:65, :], in_=zrowr, func=AF.Exp, scale=-1.0),
                waits=[tl], inc=s_ac)
            td3 = P.op("sync", lambda g, ub=ub: g.dma_start(
                out=zb[0:64, :],
                in_=usb[ub][64:65, :].unsqueeze(1).broadcast_to([1, 64, 512])),
                waits=[tx, last_stt["zb"], last_nz[0]], inc=s_nz, amt=16)
            postq.append([2, lambda td3=td3: fin(td3)])

        def fin(td3):
            ub2 = nj % 2
            if odd == 0:
                tst = P.op("gpsimd", lambda g, h=h, nj=nj, ub2=ub2: g.tensor_tensor(
                    ot[h // 2][0:64, 512 * nj:512 * nj + 512],
                    usb[ub2][0:64, :], zb[0:64, :], ALU.mult), waits=[td3], inc=s_gp)
                uevac[(h, nj)] = tst
                last_nz[0] = td3
                last_stt[nj % 2] = tst
                last_stt["zb"] = tst
            else:
                tst = P.op("gpsimd", lambda g, ub2=ub2: g.tensor_tensor(
                    zsh[:], usb[ub2][0:64, :], zb[0:64, :], ALU.mult),
                    waits=[td3, last_nz[0]], inc=s_gp)
                tsh = P.op("sync", lambda g, h=h, nj=nj: g.dma_start(
                    out=ot[h // 2][64:128, 512 * nj:512 * nj + 512], in_=zsh[:]),
                    waits=[tst], inc=s_nz, amt=16)
                uevac[(h, nj)] = tsh
                last_nz[0] = tsh
                last_stt[nj % 2] = tst
                last_stt["zb"] = tst

        postq.append([1, mid])

    p3tok = {}
    p3tok = {}

    def gen_p3prep():
        for sl in range(DC):
            xs = sl % 2
            tw = dma(xr[xs][:], dfw[128 * sl:128 * (sl + 1), :], waits=acquire(("xr", xs)),
                     sem=s_x[xs])
            tr = vop(lambda v, sl=sl, xs=xs: v.tensor_reduce(
                wsum[:, sl:sl + 1], xr[xs][:], axis=AX, op=ALU.add,
                apply_absolute_value=True), waits=[tw])
            consumed(("xr", xs), tr)
            yield
        t1 = vop(lambda v: v.tensor_reduce(wsum1[:], wsum[:], axis=AX, op=ALU.add))
        yield
        bs = 0
        btws = acquire(("pB", bs))
        tm = P.op("tensor", lambda te, bs=bs: te.matmul(
            pB[bs][0:1, 0:1], wsum1[:], onescol[:], start=True, stop=True),
            waits=[t1] + btws, inc=s_pe)
        t2 = vop(lambda v, bs=bs: v.tensor_scalar(
            out=al_st[:, 0:1], in0=pB[bs][0:1, 0:1], scalar1=0.7 / float(D * D),
            scalar2=None, op0=ALU.mult), waits=[tm])
        consumed(("pB", bs), t2)
        t3 = vop(lambda v: v.tensor_scalar(out=al_st[:, 1:2], in0=al_st[:, 0:1],
                                           scalar1=-1.0, scalar2=None, op0=ALU.mult),
                 waits=[t_rows])
        yield
        bs = 1
        btws = acquire(("pB", bs))
        tm = P.op("tensor", lambda te, bs=bs: te.matmul(
            pB[bs][:, 0:3], onesrow_f[:], al_st[:], start=True, stop=True),
            waits=[t3] + btws, inc=s_pe)
        t_bc3 = vop(lambda v, bs=bs: v.tensor_copy(bc3[:], pB[bs][:, 0:3]), waits=[tm])
        consumed(("pB", bs), t_bc3)
        yield

        tcb = vop(lambda v: v.tensor_tensor(brow_bo[:], brow_bo[:], brow_fb[:], ALU.add),
                  waits=[t_rows])
        p3tok["crow"] = vop(lambda v: v.tensor_copy(brow_c[:], brow_bo[:]))

        t_ph2 = t_pair7[0]
        tw_next = dma(xr[0][:], dwo[0:128, :], waits=acquire(("xr", 0)), sem=s_x[0])
        for sl in range(DC):
            xs = sl % 2
            tw = tw_next
            if sl + 1 < DC:
                nxs = (sl + 1) % 2
                tw_next = dma(xr[nxs][:], dwo[128 * (sl + 1):128 * (sl + 2), :],
                              waits=acquire(("xr", nxs)), sem=s_x[nxs])
            for half in range(2):
                ps = half
                tws = acquire(("pB", ps))
                tp = None
                for q in range(4):
                    c = 4 * half + q
                    tp = P.op("tensor", lambda te, xs=xs, c=c, ps=ps, q=q: te.transpose(
                        pB[ps][:, 128 * q:128 * (q + 1)], xr[xs][:, 128 * c:128 * (c + 1)],
                        identf[:]), waits=([tw, t_idf] + tws) if q == 0 else (),
                        inc=s_pe if q == 3 else None)
                consumed(("xr", xs), tp)
                dst = vwo.ap().rearrange("p (c s2 k) -> p c s2 k", c=DC, s2=DC)[:, 4 * half:4 * half + 4, sl, :]
                tv = P.op("scalar", lambda s, dst=dst, ps=ps: s.activation(
                    out=dst, in_=pB[ps][:, 0:512].rearrange("p (q k) -> p q k", q=4),
                    func=AF.Identity), waits=[tp, t_ph2], inc=s_ac)
                consumed(("pB", ps), tv)
                yield
        p3tok["wo"] = (s_dv, s_dv.n)

        tw_next = dma(xr[0][:], dfw[0:128, :], waits=acquire(("xr", 0)), sem=s_x[0])
        for sl in range(DC):
            xs = sl % 2
            tw = tw_next
            if sl + 1 < DC:
                nxs = (sl + 1) % 2
                tw_next = dma(xr[nxs][:], dfw[128 * (sl + 1):128 * (sl + 2), :],
                              waits=acquire(("xr", nxs)), sem=s_x[nxs])
            s0w = acquire(("sf", 0))
            s1w = acquire(("sf", 1))
            tg1 = vop(lambda v, xs=xs: v.tensor_scalar(
                out=sf[0][:], in0=xr[xs][:], scalar1=bc3[:, 0:1], scalar2=None,
                op0=ALU.is_gt), waits=[tw, t_bc3] + s0w)
            tg2 = vop(lambda v, xs=xs: v.tensor_scalar(
                out=sf[1][:], in0=xr[xs][:], scalar1=bc3[:, 1:2], scalar2=None,
                op0=ALU.is_lt), waits=[tw] + s1w)
            tg3 = vop(lambda v: v.tensor_tensor(sf[0][:], sf[0][:], sf[1][:],
                                                ALU.subtract))
            consumed(("xr", xs), tg2)
            consumed(("xr", xs), tg1)
            consumed(("sf", 1), tg3)
            tplast = None
            for half in range(2):
                ps = half
                tws = acquire(("pB", ps))
                tp = None
                for q in range(4):
                    c = 4 * half + q
                    tp = P.op("tensor", lambda te, c=c, ps=ps, q=q: te.transpose(
                        pB[ps][:, 128 * q:128 * (q + 1)], sf[0][:, 128 * c:128 * (c + 1)],
                        identf[:]), waits=([tg3, t_idf] + tws) if q == 0 else (),
                        inc=s_pe if q == 3 else None)
                tplast = tp
                dst = vft.ap().rearrange("p (c s2 k) -> p c s2 k", c=DC, s2=DC)[:, 4 * half:4 * half + 4, sl, :]
                tv = P.op("scalar", lambda s, dst=dst, ps=ps: s.activation(
                    out=dst, in_=pB[ps][:, 0:512].rearrange("p (q k) -> p q k", q=4),
                    func=AF.Identity, scale=bc3[:, 2:3]), waits=[tp, t_ph2], inc=s_ac)
                consumed(("pB", ps), tv)
                yield
            consumed(("sf", 0), tplast)
        p3tok["ft"] = (s_dv, s_dv.n)



    def run_all(g):
        for _ in g:
            pass

    ROWS_STEPS = 40
    P2_STEPS = 20
    HOLD = 4

    def zip_head(g1r, g1m, g2, g3):
        d1 = g1r is None
        dm = g1m is None
        d2 = g2 is None
        d3 = g3 is None
        rem2 = P2_STEPS if g2 is not None else 0
        credit = 0.0
        rate = (P2_STEPS - HOLD) / float(ROWS_STEPS)
        while not d1:
            if not d2 and rem2 > (HOLD if not dm else 0):
                credit += rate
                while credit >= 1.0 and rem2 > HOLD:
                    if next(g2, StopIteration) is StopIteration:
                        d2 = True
                        break
                    rem2 -= 1
                    credit -= 1.0
            if next(g1r, StopIteration) is StopIteration:
                d1 = True
            if not d3:
                if next(g3, StopIteration) is StopIteration:
                    d3 = True
        while not (dm and d2):
            if not d2:
                if next(g2, StopIteration) is StopIteration:
                    d2 = True
                else:
                    rem2 -= 1
            if not dm:
                if next(g1m, StopIteration) is StopIteration:
                    dm = True
            if not d3:
                if next(g3, StopIteration) is StopIteration:
                    d3 = True

    emit_qkv_pair(0)
    run_all(gen_p1rows(0))
    run_all(gen_p1mls(0))
    t_pair7 = [None]
    p3g = [None]
    for h in range(H):
        nh = h + 1
        if nh < H and nh % 2 == 0:
            emit_qkv_pair(nh // 2)
            if nh // 2 == 7:
                t_pair7[0] = (s_pe, s_pe.n)
                p3g[0] = gen_p3prep()
        zip_head(gen_p1rows(nh) if nh < H else None,
                 gen_p1mls(nh) if nh < H else None,
                 gen_pass2(h), p3g[0])
    if p3g[0] is not None:
        run_all(p3g[0])

    u_done = [uevac[(2 * c + 1, 3)] for c in range(DC)]
    tx_next = dma(xr[0][:], dx[0:128, :], waits=acquire(("xr", 0)), sem=s_x[0])
    for t in range(NT):
        xs = t % 2
        tx = tx_next
        if t + 1 < NT:
            nxs = (t + 1) % 2
            tx_next = dma(xr[nxs][:], dx[128 * (t + 1):128 * (t + 2), :],
                          waits=acquire(("xr", nxs)), sem=s_x[nxs])
        oslot = 0
        otws = acquire(("osb", oslot))
        st_tok = None
        for cc in range(2):
            aslot = cc
            fslot = cc
            atws = acquire(("pB", aslot))
            ta = None
            for c in range(DC):
                ta = P.op("tensor", lambda te, c=c, t=t, cc=cc, ps=aslot: te.matmul(
                    pB[ps][:, 0:512], ot[c][:, 128 * t:128 * (t + 1)],
                    woT[c][:, 512 * cc:512 * (cc + 1)], start=(c == 0), stop=False),
                    waits=([p3tok["wo"]] + u_done + atws) if c == 0 else ())
            ta = P.op("tensor", lambda te, t=t, cc=cc, ps=aslot: te.matmul(
                pB[ps][:, 0:512], onesrow_b[:], brow_c[0:1, 512 * cc:512 * (cc + 1)],
                start=False, stop=True), waits=[p3tok["crow"]], inc=s_pe)
            ftws = acquire(("pA", fslot))
            tf = None
            for c in range(DC):
                tf = P.op("tensor", lambda te, c=c, t=t, cc=cc, ps=fslot: te.matmul(
                    pA[ps][:], xut[c][:, 128 * t:128 * (t + 1)],
                    ftT[c][:, 512 * cc:512 * (cc + 1)], start=(c == 0),
                    stop=(c == DC - 1)),
                    waits=([p3tok["ft"], t_xut] + ftws) if c == 0 else (),
                    inc=s_pe if c == DC - 1 else None)
            tv1 = vop(lambda v, t=t, cc=cc, os=oslot, ps=fslot, xs=xs: v.scalar_tensor_tensor(
                out=osb[os][:, 512 * cc:512 * (cc + 1)], in0=pA[ps][:],
                scalar=ab_all[:, t:t + 1], in1=xr[xs][:, 512 * cc:512 * (cc + 1)],
                op0=ALU.mult, op1=ALU.add),
                waits=[tf, tx, t_ab] + otws + (u_done if t == 0 else []))
            otws = []
            consumed(("pA", fslot), tv1)
            tv2 = vop(lambda v, cc=cc, os=oslot, ps=aslot: v.tensor_tensor(
                osb[os][:, 512 * cc:512 * (cc + 1)], osb[os][:, 512 * cc:512 * (cc + 1)],
                pB[ps][:, 0:512], ALU.add), waits=[ta])
            consumed(("pB", aslot), tv2)
            consumed(("xr", xs), tv1)
            st_tok = tv2
        ts = store(dout[128 * t:128 * (t + 1), :], osb[oslot][:], waits=[st_tok],
                   sem=s_st[oslot])
        consumed(("osb", oslot), ts)

    if _DBG:
        allw = [(s_pe, s_pe.n), (s_dv, s_dv.n), (s_ac, s_ac.n), (s_gp, s_gp.n),
                (s_c, s_c.n), (s_x[0], s_x[0].n), (s_x[1], s_x[1].n),
                (s_sg, s_sg.n), (s_mls, s_mls.n), (s_rb, s_rb.n)]
        tap = {"st_mu": st_mu, "st_s2": st_s2, "st_var": st_var, "b_all": b_all,
               "st_t1": st_t1, "st_t2": st_t2, "st_t3": st_t3, "st_t4": st_t4,
               "wsum1": wsum1,
               "ab_all": ab_all, "nmub": nmub, "bc3": bc3, "wsum": wsum,
               "biasq": biasq, "biask": biask, "biasv": biasv, "brow_c": brow_c,
               "gcol": gcol, "becol": becol, "al_st": al_st}
        for c in range(DC):
            tap[f"xut{c}"] = xut[c]
            tap[f"ot{c}"] = ot[c]
            tap[f"woT{c}"] = woT[c]
            tap[f"ftT{c}"] = ftT[c]
        for i in range(4):
            tap[f"qaug{i}"] = qaug[i]
            tap[f"kaug{i}"] = kaug[i]
        for i in range(2):
            tap[f"vtm{i}"] = vtm[i]
            tap[f"npm{i}"] = npm[i]
            tap[f"nmx16{i}"] = nmx16[i]
        tap["wqT"] = wqT
        tap["wkT"] = wkT
        tap["wvT"] = wvT
        for nm_ in _DBG:
            t = tap[nm_]
            try:
                ap = t.ap()
            except TypeError:
                ap = t
            sh = list(ap.shape)
            dd = nc.dram_tensor(f"dbg_{nm_}", sh, ap.dtype, kind="ExternalOutput").ap()
            P.op("sync", lambda g, dd=dd, ap=ap: g.dma_start(out=dd, in_=ap),
                 waits=allw, inc=s_st, amt=16)

    P.op("sync", lambda g: g.nop(),
         waits=[(s_st[0], s_st[0].n), (s_st[1], s_st[1].n)])

    with ctx:
        with nc.Block() as block:
            @block.sync
            def _(eng):
                P.emit("sync", eng)

            @block.tensor
            def _(eng):
                P.emit("tensor", eng)

            @block.vector
            def _(eng):
                P.emit("vector", eng, self_sem=s_dv)

            @block.scalar
            def _(eng):
                P.emit("scalar", eng, self_sem=s_ac)

            @block.gpsimd
            def _(eng):
                P.emit("gpsimd", eng, self_sem=s_gp)
    return nc


def kernel(L0_attn_x, L0_attn_Wq, L0_attn_bq, L0_attn_Wk, L0_attn_bk,
           L0_attn_Wv, L0_attn_bv, L0_attn_Wo, L0_attn_bo, L0_attn_gamma,
           L0_attn_beta, L0_attn_rel_bias, L0_ffn_weight, L0_ffn_alpha,
           L0_ffn_bias):
    f = lambda a: np.ascontiguousarray(np.asarray(a, dtype=np.float32))
    x = f(L0_attn_x)
    wq, wk, wv, wo, fw = f(L0_attn_Wq), f(L0_attn_Wk), f(L0_attn_Wv), f(L0_attn_Wo), f(L0_ffn_weight)
    bq, bk, bv = f(L0_attn_bq), f(L0_attn_bk), f(L0_attn_bv)
    bo = f(L0_attn_bo).reshape(1, D)
    fb = f(L0_ffn_bias).reshape(1, D)
    ga, be = f(L0_attn_gamma), f(L0_attn_beta)
    al = f(L0_ffn_alpha).reshape(1, 1)
    relT = np.ascontiguousarray(f(L0_attn_rel_bias).transpose(0, 2, 1))

    mqt = _interp_matrix_T()
    ident_bf = np.eye(128, dtype=np.float32).astype(ml_dtypes.bfloat16)
    ident_f32 = np.eye(128, dtype=np.float32)
    mask_tm = np.triu(np.full((128, 128), BIG_NEG, dtype=np.float32), 1).astype(ml_dtypes.bfloat16)
    mm, vv = np.meshgrid(np.arange(128), np.arange(512), indexing="ij")
    mask_kb = np.where(mm + 384 > vv, np.float32(BIG_NEG),
                       np.float32(0.0)).astype(ml_dtypes.bfloat16)

    nc = build_nc()
    base = dict(wq=wq, wk=wk, wv=wv, wo=wo, fw=fw, bq=bq, bk=bk, bv=bv, bo=bo,
                fb=fb, gamma=ga, beta=be, alpha=al, relT=relT, d_mqt=mqt,
                ident_bf=ident_bf, ident_f32=ident_f32, d_mask_tm=mask_tm,
                d_mask_kb=mask_kb)
    in_maps = [dict(base, x=np.ascontiguousarray(x[i])) for i in range(8)]
    res = run_bass_kernel_spmd(nc, in_maps, core_ids=list(range(8)))
    if _DBG:
        globals()["_dbg_res"] = res.results
    return np.stack([r["out"] for r in res.results], axis=0)
